# version 02
# baseline (speedup 1.0000x reference)
import numpy as np
import ml_dtypes
import concourse.bass as bass
import concourse.mybir as mybir
from concourse.bass_utils import run_bass_kernel_spmd

BFNP = ml_dtypes.bfloat16
F32 = mybir.dt.float32
BF16 = mybir.dt.bfloat16
AF = mybir.ActivationFunctionType
OP = mybir.AluOpType
AX = mybir.AxisListType

NCORE = 8
SEQS = 4
T = 2048
D = 1024
NC = 8
TT = 512
NTILE = T // TT
NEG = -1.0e30

DO_ATTNA = True
DO_FFN = True
DO_ATTNB = True
DEBUG_AB = False
DEBUG_SEQ = 0


class B:
    def __init__(self, nc):
        self.nc = nc
        self.s = {}
        self.c = {}

    def blk(self, name):
        self._bn = getattr(self, "_bn", 0) + 1
        return self.nc.Block("%s_%d" % (name, self._bn))

    def pt(self, name, shape, dt):
        self._pn = getattr(self, "_pn", 0) + 1
        return self.nc.psum_tensor("%s_%d" % (name, self._pn), shape, dt)

    def mksem(self, stack, *names):
        for n in names:
            self.s[n] = stack.enter_context(self.nc.semaphore(n))
            self.c[n] = 0

    def inc(self, inst, name, delta=1):
        inst.then_inc(self.s[name], delta)
        self.c[name] += delta
        return self.c[name]

    def dma(self, eng, out, in_, q="dmas"):
        return self.inc(eng.dma_start(out, in_), q, 16)

    def w(self, eng, name, tok):
        if tok > 0:
            eng.wait_ge(self.s[name], tok)

    def mark(self):
        return dict(self.c)

    def wsnap(self, eng, snap):
        for n in ("pe", "dve", "act", "pool"):
            self.w(eng, n, snap[n])


def build_nc():
    nc = bass.Bass("TRN2", target_bir_lowering=False)
    p = B(nc)

    xin = nc.dram_tensor("xin", [SEQS, NC, 128, T], F32, kind="ExternalInput").ap()
    y = nc.dram_tensor("y", [SEQS, NC, 128, T], F32, kind="ExternalOutput").ap()
    wth_i = nc.dram_tensor("wth", [128, 2048], F32, kind="ExternalInput").ap()
    wph_i = nc.dram_tensor("wph", [128, 2048], F32, kind="ExternalInput").ap()
    wg_i = nc.dram_tensor("wg", [128, 8192], F32, kind="ExternalInput").ap()
    wq_i = nc.dram_tensor("wq", [128, 8192], F32, kind="ExternalInput").ap()
    wk_i = nc.dram_tensor("wk", [128, 2048], F32, kind="ExternalInput").ap()
    wv_i = nc.dram_tensor("wv", [128, 2048], F32, kind="ExternalInput").ap()
    wo_i = nc.dram_tensor("wo", [128, 8192], F32, kind="ExternalInput").ap()
    w1_i = nc.dram_tensor("w1", [128, 4096], F32, kind="ExternalInput").ap()
    w2_i = nc.dram_tensor("w2", [128, 16384], F32, kind="ExternalInput").ap()
    wc_i = nc.dram_tensor("wc", [8, 128, 16384], F32, kind="ExternalInput").ap()
    cs_i = nc.dram_tensor("cs", [128, 4096], F32, kind="ExternalInput").ap()
    mk_i = nc.dram_tensor("mk", [128, 256], F32, kind="ExternalInput").ap()
    id_i = nc.dram_tensor("idb", [128, 128], BF16, kind="ExternalInput").ap()
    on_i = nc.dram_tensor("onb", [128, 2], BF16, kind="ExternalInput").ap()
    or_i = nc.dram_tensor("onr", [128, 128], F32, kind="ExternalInput").ap()
    qg_i = nc.dram_tensor("qg", [128, 2], F32, kind="ExternalInput").ap()
    kg_i = nc.dram_tensor("kg", [128, 2], F32, kind="ExternalInput").ap()

    dbg = {}
    if DEBUG_AB:
        for nm, cols in [("ln3", 16384), ("kv", 8192), ("qp", 8192), ("av", 8192),
                         ("pn0", 640), ("p32", 1024), ("avp", 32)]:
            dbg[nm] = nc.dram_tensor("dbg_" + nm, [128, cols],
                                     F32 if nm in ("p32", "avp") else BF16,
                                     kind="ExternalOutput").ap()
    wth_b = nc.dram_tensor("wth_b", [128, 2048], BF16, kind="Internal").ap()
    wph_b = nc.dram_tensor("wph_b", [128, 2048], BF16, kind="Internal").ap()
    wg_b = nc.dram_tensor("wg_b", [128, 8192], BF16, kind="Internal").ap()
    wq_b = nc.dram_tensor("wq_b", [128, 8192], BF16, kind="Internal").ap()
    wk_b = nc.dram_tensor("wk_b", [128, 2048], BF16, kind="Internal").ap()
    wv_b = nc.dram_tensor("wv_b", [128, 2048], BF16, kind="Internal").ap()
    wo_b = nc.dram_tensor("wo_b", [128, 8192], BF16, kind="Internal").ap()
    w1_b = nc.dram_tensor("w1_b", [128, 4096], BF16, kind="Internal").ap()
    w2_b = nc.dram_tensor("w2_b", [128, 16384], BF16, kind="Internal").ap()
    wc_b = nc.dram_tensor("wc_b", [8, 128, 16384], BF16, kind="Internal").ap()

    import contextlib
    with contextlib.ExitStack() as st:
        sbt = lambda name, shape, dt: st.enter_context(nc.sbuf_tensor(name, shape, dt))
        xfm = sbt("xfm", [128, NC * T], F32)
        lnb = sbt("lnb", [128, NC * T], BF16)
        cs = sbt("cs_sb", [128, 2 * T], F32)
        wsb = sbt("wsb", [128, 16384], BF16)
        biga = sbt("biga", [128, 16384], BF16)
        kvb = sbt("kvb", [128, 8192], BF16)
        scr32 = sbt("scr32", [128, 1024], F32)
        rsm = sbt("rsm", [128, 2048], F32)
        pnb = sbt("pnb", [128, 640], BF16)
        avp = sbt("avp", [128, 32], F32)
        mskL = sbt("mskL", [128, 128], F32)
        mskD = sbt("mskD", [128, 128], F32)
        idb = sbt("idb_sb", [128, 128], BF16)
        onb = sbt("onb_sb", [128, 2], BF16)
        onr = sbt("onr_sb", [128, 128], F32)
        qgs = sbt("qgs", [128, 2], F32)
        kgs = sbt("kgs", [128, 2], F32)

        p.mksem(st, "pe", "dve", "act", "pool", "dmas", "dmag")
        wtok = {}

        with p.blk("startup") as blk:
            @blk.gpsimd
            def _(g):
                wtok["wth"] = p.dma(g, wth_b, wth_i, "dmag")
                wtok["wph"] = p.dma(g, wph_b, wph_i, "dmag")
                wtok["wg"] = p.dma(g, wg_b, wg_i, "dmag")
                wtok["w1"] = p.dma(g, w1_b, w1_i, "dmag")
                wtok["w2"] = p.dma(g, w2_b, w2_i, "dmag")
                wtok["wq"] = p.dma(g, wq_b, wq_i, "dmag")
                wtok["wk"] = p.dma(g, wk_b, wk_i, "dmag")
                wtok["wv"] = p.dma(g, wv_b, wv_i, "dmag")
                wtok["wo"] = p.dma(g, wo_b, wo_i, "dmag")
                for m in range(8):
                    wtok["wc%d" % m] = p.dma(g, wc_b[m], wc_i[m], "dmag")
                wtok["castall"] = p.c["dmag"]

            @blk.sync
            def _(s):
                p.dma(s, cs[:, :], cs_i)
                p.dma(s, mskL[:, :], mk_i[:, 0:128])
                p.dma(s, mskD[:, :], mk_i[:, 128:256])
                p.dma(s, idb[:, :], id_i)
                p.dma(s, onb[:, :], on_i)
                p.dma(s, onr[:, :], or_i)
                p.dma(s, qgs[:, :], qg_i)
                p.dma(s, kgs[:, :], kg_i)
                for c in range(NC):
                    p.dma(s, xfm[:, c * T:(c + 1) * T], xin[0, c])
                wtok["x"] = p.c["dmas"]

            @blk.vector
            def _(v):
                p.w(v, "dmas", wtok["x"])
                p.inc(v.tensor_scalar_mul(qgs[:, :], qgs[:, :], 1.0 / 16.0), "dve")

        def emit_ln():
            snap = p.mark()
            pe0, dv0, ac0 = snap["pe"], snap["dve"], snap["act"]
            tok_h0, tok_h1 = dv0 + 38, dv0 + 54
            with p.pt("ps_m0", [1, 1024], F32) as pm0, \
                 p.pt("ps_e0", [1, 1024], F32) as pe_0, \
                 p.pt("ps_m1", [1, 1024], F32) as pm1, \
                 p.pt("ps_e1", [1, 1024], F32) as pe_1, \
                 p.blk("ln_st") as blk:

                @blk.vector
                def _(v):
                    p.w(v, "dmas", wtok["x"])
                    p.w(v, "pe", pe0)
                    p.w(v, "act", ac0)
                    for hf in range(2):
                        o = hf * 1024
                        for c in range(NC):
                            if c >= 4:
                                p.w(v, "pe", pe0 + hf * 32 + 2 * (c - 3))
                            slot = hf * 8192 + (c % 4) * 1024
                            tok = p.inc(v.tensor_copy(biga[:, slot:slot + 1024],
                                                      xfm[:, c * T + o:c * T + o + 1024]), "dve")
                            assert tok == dv0 + hf * 8 + c + 1

                @blk.scalar
                def _(a):
                    p.w(a, "dmas", wtok["x"])
                    p.w(a, "pe", pe0)
                    p.w(a, "dve", dv0)
                    for hf in range(2):
                        o = hf * 1024
                        for c in range(NC):
                            if c >= 4:
                                p.w(a, "pe", pe0 + hf * 32 + 16 + 2 * (c - 3))
                            slot = hf * 8192 + 4096 + (c % 4) * 1024
                            tok = p.inc(a.activation(biga[:, slot:slot + 1024],
                                                     xfm[:, c * T + o:c * T + o + 1024],
                                                     AF.Square), "act")
                            assert tok == ac0 + hf * 8 + c + 1

                @blk.tensor
                def _(t):
                    for hf in range(2):
                        pm = pm0 if hf == 0 else pm1
                        pex = pe_0 if hf == 0 else pe_1
                        for c in range(NC):
                            p.w(t, "dve", dv0 + hf * 8 + c + 1)
                            slot = hf * 8192 + (c % 4) * 1024
                            for n in range(2):
                                p.inc(t.matmul(pm[:, n * 512:(n + 1) * 512], onb[:, 0:1],
                                               biga[:, slot + n * 512:slot + (n + 1) * 512],
                                               start=(c == 0), stop=(c == NC - 1)), "pe")
                        for c in range(NC):
                            p.w(t, "act", ac0 + hf * 8 + c + 1)
                            slot = hf * 8192 + 4096 + (c % 4) * 1024
                            for n in range(2):
                                p.inc(t.matmul(pex[:, n * 512:(n + 1) * 512], onb[:, 0:1],
                                               biga[:, slot + n * 512:slot + (n + 1) * 512],
                                               start=(c == 0), stop=(c == NC - 1)), "pe")
                        assert p.c["pe"] == pe0 + 32 * (hf + 1)

                @blk.vector
                def _(v):
                    for hf in range(2):
                        o = hf * 1024
                        pm = pm0 if hf == 0 else pm1
                        pex = pe_0 if hf == 0 else pe_1
                        p.w(v, "pe", pe0 + 32 * (hf + 1))
                        p.inc(v.tensor_copy(rsm[0:1, o:o + 1024], pm[0:1, :]), "dve")
                        p.w(v, "act", ac0 + 16 + hf + 1)
                        p.inc(v.scalar_tensor_tensor(rsm[64:65, o:o + 1024], pex[0:1, :], 1e-5,
                                                     rsm[64:65, o:o + 1024], OP.add, OP.subtract), "dve")
                        p.w(v, "act", ac0 + 18 + hf + 1)
                        tok = p.inc(v.reciprocal(rsm[32:33, o:o + 1024], rsm[64:65, o:o + 1024]), "dve")
                        assert tok == dv0 + 17 + hf * 3 + 2
                        p.w(v, "dve", tok)

                @blk.scalar
                def _(a):
                    for hf in range(2):
                        o = hf * 1024
                        pm = pm0 if hf == 0 else pm1
                        p.w(a, "pe", pe0 + 32 * (hf + 1))
                        tok = p.inc(a.activation(rsm[64:65, o:o + 1024], pm[0:1, :], AF.Square), "act")
                        assert tok == ac0 + 16 + hf + 1
                    for hf in range(2):
                        o = hf * 1024
                        p.w(a, "dve", dv0 + 17 + hf * 3 + 1)
                        tok = p.inc(a.activation(rsm[64:65, o:o + 1024], rsm[64:65, o:o + 1024],
                                                 AF.Sqrt), "act")
                        assert tok == ac0 + 18 + hf + 1

            with p.pt("ps_bm0", [128, 1024], F32) as bm0, \
                 p.pt("ps_br0", [128, 1024], F32) as br0, \
                 p.pt("ps_bm1", [128, 1024], F32) as bm1, \
                 p.pt("ps_br1", [128, 1024], F32) as br1, \
                 p.blk("ln_ap") as blk:

                @blk.tensor
                def _(t):
                    p.w(t, "dve", dv0 + 22)
                    p.w(t, "act", ac0 + 20)
                    for hf in range(2):
                        o = hf * 1024
                        bm = bm0 if hf == 0 else bm1
                        br = br0 if hf == 0 else br1
                        for n in range(2):
                            p.inc(t.matmul(bm[:, n * 512:(n + 1) * 512], onr[0:1, :],
                                           rsm[0:1, o + n * 512:o + (n + 1) * 512],
                                           start=True, stop=True), "pe")
                        for n in range(2):
                            p.inc(t.matmul(br[:, n * 512:(n + 1) * 512], onr[32:33, :],
                                           rsm[32:33, o + n * 512:o + (n + 1) * 512],
                                           start=True, stop=True), "pe")
                        assert p.c["pe"] == pe0 + 64 + 4 * (hf + 1)

                @blk.vector
                def _(v):
                    for hf in range(2):
                        o = hf * 1024
                        bm = bm0 if hf == 0 else bm1
                        br = br0 if hf == 0 else br1
                        p.w(v, "pe", pe0 + 64 + 4 * (hf + 1))
                        for c in range(NC):
                            p.inc(v.tensor_sub(scr32[:, :], xfm[:, c * T + o:c * T + o + 1024],
                                               bm[:, :]), "dve")
                            p.inc(v.tensor_mul(lnb[:, c * T + o:c * T + o + 1024],
                                               scr32[:, :], br[:, :]), "dve")
                        assert p.c["dve"] == (tok_h0 if hf == 0 else tok_h1)
            return tok_h0, tok_h1

        def emit_attn_a():
            snap = p.mark()
            A0, B0 = 0, 8192
            TH0, S0, PH0, G0 = 0, 4096, 6144, 6656
            with p.blk("aa_w") as blk:
                @blk.sync
                def _(s):
                    p.w(s, "dmag", wtok["castall"])
                    p.wsnap(s, snap)
                    p.dma(s, wsb[:, B0:B0 + 2048], wth_b)
                    p.dma(s, wsb[:, B0 + 2048:B0 + 4096], wph_b)
                    p.w(s, "dmag", wtok["castall"])
                    wtok["aa"] = p.dma(s, wsb[:, A0:A0 + 8192], wg_b)

            with p.pt("ps_th0", [128, T], F32) as pt0, \
                 p.pt("ps_th1", [128, T], F32) as pt1, \
                 p.blk("aa_th") as blk:
                ttoks = [0, 0]

                @blk.tensor
                def _(t):
                    p.w(t, "act", snap["act"])
                    p.w(t, "dve", wtok["lnh0"])
                    p.w(t, "dmas", wtok["aa"])
                    for l in range(2):
                        ps = pt0 if l == 0 else pt1
                        if l == 1:
                            p.w(t, "dve", wtok["lnh1"])
                        for c in range(NC):
                            for n in range(4):
                                if l == 0 and c == 0 and n == 2:
                                    p.w(t, "dve", wtok["lnh1"])
                                p.inc(t.matmul(ps[:, n * 512:(n + 1) * 512],
                                               wsb[:, B0 + c * 256 + l * 128:B0 + c * 256 + l * 128 + 128],
                                               lnb[:, c * T + n * 512:c * T + (n + 1) * 512],
                                               start=(c == 0), stop=(c == NC - 1)), "pe")
                        ttoks[l] = p.c["pe"]

                @blk.scalar
                def _(a):
                    for l in range(2):
                        p.w(a, "pe", ttoks[l])
                        ps = pt0 if l == 0 else pt1
                        p.inc(a.copy(biga[:, TH0 + l * 2048:TH0 + (l + 1) * 2048], ps[:, :]), "act")
            th = p.mark()

            with p.pt("ps_ph", [128, 256], F32) as pph, \
                 p.pt("ps_g", [128, 1024], F32) as pg, \
                 p.pt("ps_s0", [128, 1024], F32) as ps0, \
                 p.pt("ps_s1", [128, 1024], F32) as ps1, \
                 p.blk("aa_s") as blk:
                dve0 = th["dve"]
                ev = lambda tt: dve0 + 2 * (tt + 1)
                pg_tok, s_tok = [], []

                @blk.tensor
                def _(t):
                    p.wsnap(t, th)
                    for tt in range(16):
                        if tt >= 1:
                            p.w(t, "dve", ev(tt - 1))
                        for c in range(NC):
                            p.inc(t.matmul(pph[:, :],
                                           lnb[:, c * T + tt * 128:c * T + tt * 128 + 128],
                                           wsb[:, B0 + 2048 + c * 256:B0 + 2048 + (c + 1) * 256],
                                           start=(c == 0), stop=(c == NC - 1)), "pe")
                        for c in range(NC):
                            for n in range(2):
                                p.inc(t.matmul(pg[:, n * 512:(n + 1) * 512],
                                               lnb[:, c * T + tt * 128:c * T + tt * 128 + 128],
                                               wsb[:, A0 + c * 1024 + n * 512:A0 + c * 1024 + (n + 1) * 512],
                                               start=(c == 0), stop=(c == NC - 1)), "pe")
                        pg_tok.append(p.c["pe"])
                        if tt >= 1:
                            j = tt - 1
                            par = j % 2
                            for l in range(2):
                                ps = ps0 if l == 0 else ps1
                                for n in range(2):
                                    p.inc(t.matmul(ps[:, n * 512:(n + 1) * 512],
                                                   biga[:, PH0 + par * 256 + l * 128:PH0 + par * 256 + l * 128 + 128],
                                                   biga[:, G0 + par * 1024 + n * 512:G0 + par * 1024 + (n + 1) * 512],
                                                   start=(j == 0), stop=(j == 15)), "pe")
                            s_tok.append(p.c["pe"])
                    p.w(t, "dve", ev(15))
                    j, par = 15, 15 % 2
                    for l in range(2):
                        ps = ps0 if l == 0 else ps1
                        for n in range(2):
                            p.inc(t.matmul(ps[:, n * 512:(n + 1) * 512],
                                           biga[:, PH0 + par * 256 + l * 128:PH0 + par * 256 + l * 128 + 128],
                                           biga[:, G0 + par * 1024 + n * 512:G0 + par * 1024 + (n + 1) * 512],
                                           start=False, stop=True), "pe")
                    s_tok.append(p.c["pe"])

                @blk.vector
                def _(v):
                    for tt in range(16):
                        par = tt % 2
                        p.w(v, "pe", pg_tok[tt])
                        if tt >= 2:
                            p.w(v, "pe", s_tok[tt - 2])
                        p.inc(v.tensor_copy(biga[:, PH0 + par * 256:PH0 + par * 256 + 256], pph[:, :]), "dve")
                        tok = p.inc(v.tensor_copy(biga[:, G0 + par * 1024:G0 + par * 1024 + 1024], pg[:, :]), "dve")
                        assert tok == ev(tt)
                    p.w(v, "pe", s_tok[15])
                    for l in range(2):
                        ps = ps0 if l == 0 else ps1
                        p.inc(v.tensor_scalar_mul(biga[:, S0 + l * 1024:S0 + (l + 1) * 1024],
                                                  ps[:, :], 1.0 / 32.0), "dve")
            sev = p.mark()

            with p.pt("ps_o0", [128, T], F32) as po0, \
                 p.pt("ps_o1", [128, T], F32) as po1, \
                 p.blk("aa_o") as blk:
                dve0 = sev["dve"]
                addt = lambda m: dve0 + m + 1
                mtoks = []

                @blk.tensor
                def _(t):
                    p.wsnap(t, sev)
                    for m in range(NC):
                        ps = po0 if m % 2 == 0 else po1
                        if m >= 2:
                            p.w(t, "dve", addt(m - 2))
                        for l in range(2):
                            for n in range(4):
                                p.inc(t.matmul(ps[:, n * 512:(n + 1) * 512],
                                               biga[:, S0 + l * 1024 + m * 128:S0 + l * 1024 + m * 128 + 128],
                                               biga[:, TH0 + l * 2048 + n * 512:TH0 + l * 2048 + (n + 1) * 512],
                                               start=(l == 0), stop=(l == 1)), "pe")
                        mtoks.append(p.c["pe"])

                @blk.vector
                def _(v):
                    for m in range(NC):
                        ps = po0 if m % 2 == 0 else po1
                        p.w(v, "pe", mtoks[m])
                        tok = p.inc(v.tensor_add(xfm[:, m * T:(m + 1) * T],
                                                 xfm[:, m * T:(m + 1) * T], ps[:, :]), "dve")
                        assert tok == addt(m)
            return p.mark()

        def emit_ffn():
            snap = p.mark()
            A0, WC0, B0, WC1 = 0, 4096, 8192, 12288
            with p.blk("ff_w") as blk:
                @blk.sync
                def _(s):
                    p.w(s, "dmag", wtok["castall"])
                    p.w(s, "pe", snap["pe"])
                    wtok["ffw1"] = p.dma(s, wsb[:, B0:B0 + 4096], w1_b)

            m0 = p.mark()
            pe0, dve0, act0, dm0, dg0 = m0["pe"], m0["dve"], m0["act"], m0["dmas"], m0["dmag"]
            pe_ymm = lambda tg, yci: pe0 + tg * 296 + yci * 5 + 1
            pe_rail = lambda tg, yci: pe0 + tg * 296 + yci * 5 + 5
            pe_cp = lambda tg, m: pe0 + tg * 296 + 40 + (m + 1) * 32
            yev = lambda tg, yci: dve0 + tg * 16 + yci + 1
            oadd = lambda tg, m: dve0 + tg * 16 + 8 + m + 1
            gel = lambda tg, zi: act0 + tg * 32 + zi + 1
            dm_w2 = lambda g: dm0 + (g * 17 + 1) * 16
            dm_wce = lambda tg, m: dm0 + ((tg // 4) * 17 + 1 + (tg % 4) * 4 + m // 2 + 1) * 16
            dm_wco = lambda tg, m: dg0 + (tg * 4 + (m - 1) // 2 + 1) * 16

            with p.pt("ps_y0", [128, TT], F32) as py0, \
                 p.pt("ps_y1", [128, TT], F32) as py1, \
                 p.pt("ps_z0", [128, TT], F32) as pz0, \
                 p.pt("ps_z1", [128, TT], F32) as pz1, \
                 p.pt("ps_oA", [128, TT], F32) as poA, \
                 p.pt("ps_oB", [128, TT], F32) as poB:
                for g in range(4):
                    for tile in range(NTILE):
                        tg = g * 4 + tile
                        with p.blk("ff") as blk:
                            @blk.sync
                            def _(s):
                                if tile == 0:
                                    if g == 0:
                                        p.w(s, "pe", snap["pe"])
                                    else:
                                        p.w(s, "pe", pe_rail(g * 4 - 1, 7))
                                    tok = p.dma(s, wsb[:, A0:A0 + 4096],
                                                w2_b[:, g * 4096:(g + 1) * 4096])
                                    assert tok == dm_w2(g)
                                for m in range(0, 8, 2):
                                    je = tg * 4 + m // 2
                                    if je == 0:
                                        p.w(s, "pe", snap["pe"])
                                    else:
                                        jp = je - 1
                                        p.w(s, "pe", pe_cp(jp // 4, 2 * (jp % 4)))
                                    tok = p.dma(s, wsb[:, WC0:WC0 + 4096],
                                                wc_b[m][:, g * 4096:(g + 1) * 4096])
                                    assert tok == dm_wce(tg, m)

                            @blk.gpsimd
                            def _(gq):
                                for m in range(1, 8, 2):
                                    jo = tg * 4 + (m - 1) // 2
                                    if jo == 0:
                                        p.w(gq, "pe", snap["pe"])
                                    else:
                                        jp = jo - 1
                                        p.w(gq, "pe", pe_cp(jp // 4, 2 * (jp % 4) + 1))
                                    tok = p.dma(gq, wsb[:, WC1:WC1 + 4096],
                                                wc_b[m][:, g * 4096:(g + 1) * 4096], "dmag")
                                    assert tok == dm_wco(tg, m)

                            @blk.tensor
                            def _(t):
                                if tg == 0:
                                    p.w(t, "dmas", wtok["ffw1"])
                                    p.w(t, "act", snap["act"])
                                p.w(t, "dve", wtok["lnh1" if tile >= 2 else "lnh0"])
                                if tile == 0:
                                    p.w(t, "dmas", dm_w2(g))
                                for yci in range(8):
                                    yc = g * 8 + yci
                                    c, jy = yc // 4, yc % 4
                                    ypar = yci % 2
                                    psy = py0 if ypar == 0 else py1
                                    gi = tg * 8 + yci
                                    if gi >= 2:
                                        p.w(t, "dve", yev((gi - 2) // 8, (gi - 2) % 8))
                                    tok = p.inc(t.matmul(psy[:, :],
                                                         wsb[:, B0 + c * 512 + jy * 128:B0 + c * 512 + (jy + 1) * 128],
                                                         lnb[:, c * T + tile * TT:c * T + (tile + 1) * TT],
                                                         start=True, stop=True), "pe")
                                    assert tok == pe_ymm(tg, yci)
                                    p.w(t, "dve", yev(tg, yci))
                                    for jj in range(4):
                                        zi = yci * 4 + jj
                                        psz = pz0 if zi % 2 == 0 else pz1
                                        zz = tg * 32 + zi
                                        if zz >= 2:
                                            p.w(t, "act", gel((zz - 2) // 32, (zz - 2) % 32))
                                        p.inc(t.matmul(psz[:, :],
                                                       wsb[:, A0 + yci * 512 + jj * 128:A0 + yci * 512 + (jj + 1) * 128],
                                                       kvb[:, ypar * 512:(ypar + 1) * 512],
                                                       start=True, stop=True), "pe")
                                    assert p.c["pe"] == pe_rail(tg, yci)
                                for m in range(8):
                                    pso = poA if m % 2 == 0 else poB
                                    if tg == 0 and m == 0:
                                        p.w(t, "dve", wtok["lnh1"])
                                    if m >= 2:
                                        p.w(t, "dve", oadd(tg, m - 2))
                                    elif tg >= 1:
                                        p.w(t, "dve", oadd(tg - 1, m + 6))
                                    if m % 2 == 0:
                                        p.w(t, "dmas", dm_wce(tg, m))
                                    else:
                                        p.w(t, "dmag", dm_wco(tg, m))
                                    wcb = WC0 if m % 2 == 0 else WC1
                                    for zi in range(32):
                                        if m == 0:
                                            p.w(t, "act", gel(tg, zi))
                                        p.inc(t.matmul(pso[:, :],
                                                       wsb[:, wcb + zi * 128:wcb + (zi + 1) * 128],
                                                       biga[:, zi * 512:(zi + 1) * 512],
                                                       start=(zi == 0), stop=(zi == 31)), "pe")
                                    assert p.c["pe"] == pe_cp(tg, m)

                            @blk.vector
                            def _(v):
                                for yci in range(8):
                                    ypar = yci % 2
                                    psy = py0 if ypar == 0 else py1
                                    p.w(v, "pe", pe_ymm(tg, yci))
                                    tok = p.inc(v.tensor_copy(kvb[:, ypar * 512:(ypar + 1) * 512],
                                                              psy[:, :]), "dve")
                                    assert tok == yev(tg, yci)
                                for m in range(8):
                                    pso = poA if m % 2 == 0 else poB
                                    p.w(v, "pe", pe_cp(tg, m))
                                    tok = p.inc(v.tensor_add(
                                        xfm[:, m * T + tile * TT:m * T + (tile + 1) * TT],
                                        xfm[:, m * T + tile * TT:m * T + (tile + 1) * TT],
                                        pso[:, :]), "dve")
                                    assert tok == oadd(tg, m)

                            @blk.scalar
                            def _(a):
                                for zi in range(32):
                                    psz = pz0 if zi % 2 == 0 else pz1
                                    yci = zi // 4
                                    p.w(a, "pe", pe_rail(tg, yci) - (3 - zi % 4))
                                    tok = p.inc(a.activation(biga[:, zi * 512:(zi + 1) * 512],
                                                             psz[:, :], AF.Gelu), "act")
                                    assert tok == gel(tg, zi)
            wtok["ffn_oadd_last"] = {m: oadd(15, m) for m in range(8)}
            return p.mark()

        def emit_dbg(nm, src_ap):
            m0 = p.mark()
            with p.blk("dbg") as blk:
                @blk.sync
                def _(s):
                    p.wsnap(s, m0)
                    p.dma(s, dbg[nm], src_ap)

        def emit_attn_b():
            if DEBUG_AB and wtok.get("dbg_seq") == DEBUG_SEQ and not wtok.get("dbg_done"):
                emit_dbg("ln3", lnb[:, :])
            snap = p.mark()
            A0, B0 = 0, 8192
            with p.blk("ab_w") as blk:
                @blk.sync
                def _(s):
                    p.w(s, "dmag", wtok["castall"])
                    p.wsnap(s, snap)
                    p.dma(s, wsb[:, A0:A0 + 8192], wq_b)
                    p.dma(s, wsb[:, B0:B0 + 2048], wk_b)
                    wtok["abqkv"] = p.dma(s, wsb[:, B0 + 2048:B0 + 4096], wv_b)

            def proj_norm_rope(kind, h, gam, dst, doff):
                for hf in range(2):
                    o = hf * 1024
                    sn = p.mark()
                    pe0, dv0, ac0 = sn["pe"], sn["dve"], sn["act"]
                    with p.pt("pp_e", [128, 1024], F32) as pqe, \
                         p.pt("pp_o", [128, 1024], F32) as pqo, \
                         p.pt("pp_ms", [1, 1024], F32) as pms, \
                         p.pt("pp_rb", [128, 1024], F32) as prb, \
                         p.blk("pnr") as blk:
                        wstride = 1024 if kind == "q" else 256

                        @blk.tensor
                        def _(t):
                            p.wsnap(t, snap)
                            p.w(t, "dmas", wtok["abqkv"])
                            p.w(t, "dve", dv0)
                            p.w(t, "act", ac0)
                            for kc in range(2):
                                ps = pqe if kc == 0 else pqo
                                for c in range(NC):
                                    for n in range(2):
                                        p.inc(t.matmul(
                                            ps[:, n * 512:(n + 1) * 512],
                                            wsb[:, (A0 if kind == "q" else B0) + c * wstride + h * 256 + kc * 128:
                                                   (A0 if kind == "q" else B0) + c * wstride + h * 256 + kc * 128 + 128],
                                            lnb[:, c * T + o + n * 512:c * T + o + (n + 1) * 512],
                                            start=(c == 0), stop=(c == NC - 1)), "pe")
                            assert p.c["pe"] == pe0 + 32
                            for kc in range(2):
                                p.w(t, "act", ac0 + kc + 1)
                                for n in range(2):
                                    p.inc(t.matmul(pms[:, n * 512:(n + 1) * 512], onb[:, 1:2],
                                                   biga[:, 8192 + kc * 1024 + n * 512:8192 + kc * 1024 + (n + 1) * 512],
                                                   start=(kc == 0), stop=(kc == 1)), "pe")
                            assert p.c["pe"] == pe0 + 36
                            p.w(t, "dve", dv0 + 4)
                            for n in range(2):
                                p.inc(t.matmul(prb[:, n * 512:(n + 1) * 512], onr[32:33, :],
                                               rsm[32:33, 1024 + n * 512:1024 + (n + 1) * 512],
                                               start=True, stop=True), "pe")
                            assert p.c["pe"] == pe0 + 38

                        @blk.scalar
                        def _(a):
                            p.w(a, "dve", dv0 + 1)
                            p.inc(a.activation(biga[:, 8192:8192 + 1024], scr32[:, :], AF.Square), "act")
                            p.w(a, "dve", dv0 + 2)
                            p.inc(a.activation(biga[:, 9216:9216 + 1024], rsm[:, 0:1024], AF.Square), "act")
                            p.w(a, "dve", dv0 + 3)
                            tok = p.inc(a.activation(rsm[64:65, 1024:2048], rsm[64:65, 1024:2048],
                                                     AF.Sqrt), "act")
                            assert tok == ac0 + 3

                        @blk.vector
                        def _(v):
                            p.w(v, "pe", pe0 + 16)
                            p.inc(v.tensor_copy(scr32[:, :], pqe[:, :]), "dve")
                            p.w(v, "pe", pe0 + 32)
                            p.inc(v.tensor_copy(rsm[:, 0:1024], pqo[:, :]), "dve")
                            p.w(v, "pe", pe0 + 36)
                            tok = p.inc(v.tensor_scalar_add(rsm[64:65, 1024:2048], pms[0:1, :], 1e-6), "dve")
                            assert tok == dv0 + 3
                            p.w(v, "act", ac0 + 3)
                            tok = p.inc(v.reciprocal(rsm[32:33, 1024:2048], rsm[64:65, 1024:2048]), "dve")
                            assert tok == dv0 + 4
                            p.w(v, "pe", pe0 + 38)
                            p.inc(v.tensor_mul(scr32[:, :], scr32[:, :], prb[:, :]), "dve")
                            p.inc(v.tensor_scalar_mul(scr32[:, :], scr32[:, :], gam[:, 0:1]), "dve")
                            p.inc(v.tensor_mul(rsm[:, 0:1024], rsm[:, 0:1024], prb[:, :]), "dve")
                            p.inc(v.tensor_scalar_mul(rsm[:, 0:1024], rsm[:, 0:1024], gam[:, 1:2]), "dve")
                            co = cs[:, o:o + 1024]
                            si = cs[:, T + o:T + o + 1024]
                            tA = biga[:, 8192:8192 + 1024]
                            tB = biga[:, 9216:9216 + 1024]
                            p.inc(v.tensor_mul(tA, scr32[:, :], co), "dve")
                            p.inc(v.tensor_mul(tB, rsm[:, 0:1024], si), "dve")
                            p.inc(v.tensor_sub(dst[:, doff + o:doff + o + 1024], tA, tB), "dve")
                            p.inc(v.tensor_mul(tA, scr32[:, :], si), "dve")
                            p.inc(v.tensor_mul(tB, rsm[:, 0:1024], co), "dve")
                            tok = p.inc(v.tensor_add(dst[:, doff + T + o:doff + T + o + 1024], tA, tB), "dve")
                            assert tok == dv0 + 14

            proj_norm_rope("k", 0, kgs, kvb, 0)

            kv0 = p.mark()
            with p.pt("ps_v0", [128, 256], F32) as pv0, \
                 p.pt("ps_v1", [128, 256], F32) as pv1, \
                 p.blk("ab_v") as blk:
                dve0 = kv0["dve"]
                vev = lambda tt: dve0 + tt + 1
                vtoks = []

                @blk.tensor
                def _(t):
                    p.wsnap(t, kv0)
                    for tt in range(16):
                        ps = pv0 if tt % 2 == 0 else pv1
                        if tt >= 2:
                            p.w(t, "dve", vev(tt - 2))
                        for c in range(NC):
                            p.inc(t.matmul(ps[:, :],
                                           lnb[:, c * T + tt * 128:c * T + tt * 128 + 128],
                                           wsb[:, B0 + 2048 + c * 256:B0 + 2048 + (c + 1) * 256],
                                           start=(c == 0), stop=(c == NC - 1)), "pe")
                        vtoks.append(p.c["pe"])

                @blk.vector
                def _(v):
                    for tt in range(16):
                        ps = pv0 if tt % 2 == 0 else pv1
                        p.w(v, "pe", vtoks[tt])
                        tok = p.inc(v.tensor_copy(kvb[:, 4096 + tt * 256:4096 + (tt + 1) * 256],
                                                  ps[:, :]), "dve")
                        assert tok == vev(tt)
            kvd = p.mark()
            if DEBUG_AB and wtok.get("dbg_seq") == DEBUG_SEQ and not wtok.get("dbg_done"):
                emit_dbg("kv", kvb[:, :])

            with p.blk("ab_wo") as blk:
                @blk.sync
                def _(s):
                    p.w(s, "dmag", wtok["castall"])
                    p.w(s, "dve", kvd["dve"])
                    p.w(s, "pe", kvd["pe"])
                    wtok["abwo"] = p.dma(s, wsb[:, 8192:16384], wo_b)

            for hp in range(2):
                for hl in range(2):
                    proj_norm_rope("q", hp * 2 + hl, qgs, biga, hl * 2 * T)
                if DEBUG_AB and hp == 0 and wtok.get("dbg_seq") == DEBUG_SEQ and not wtok.get("dbg_done"):
                    emit_dbg("qp", biga[:, 0:8192])
                qd = p.mark()

                with p.pt("ps_sc0", [128, 640], F32) as sc0, \
                     p.pt("ps_sc1", [128, 640], F32) as sc1, \
                     p.pt("ps_pt0", [128, 640], BF16) as qt0, \
                     p.pt("ps_pt1", [128, 640], BF16) as qt1, \
                     p.pt("ps_av0", [128, 256], F32) as av0, \
                     p.pt("ps_av1", [128, 256], F32) as av1:
                    iters = [(qb, hl) for qb in range(16) for hl in range(2)]
                    NIT = len(iters)
                    nkb = lambda qb: min(qb, 4) + 1
                    dve0, act0 = qd["dve"], qd["act"]
                    dtk = {k: {} for k in ("mask", "max", "sum", "rcp", "nrm", "ptv", "avv")}
                    dc = dve0
                    for i, (qb, hl) in enumerate(iters):
                        dc += 1 + (1 if qb >= 4 else 0); dtk["mask"][i] = dc
                        dc += 1; dtk["max"][i] = dc
                        dc += 1; dtk["sum"][i] = dc
                        dc += 1; dtk["rcp"][i] = dc
                        dc += 1; dtk["nrm"][i] = dc
                        dc += 1; dtk["ptv"][i] = dc
                        dc += 2; dtk["avv"][i] = dc
                    etk = lambda i: act0 + i + 1
                    pe_sc, pe_pt, pe_av = {}, {}, {}

                    def emit_pt_av(t, j):
                        qbp, hlp = iters[j]
                        parp = j % 2
                        ppt = qt0 if parp == 0 else qt1
                        pav = av0 if parp == 0 else av1
                        nkp = nkb(qbp)
                        kb0p = qbp - (nkp - 1)
                        p.w(t, "dve", dtk["nrm"][j])
                        if j >= 2:
                            p.w(t, "dve", dtk["ptv"][j - 2])
                        for kb in range(nkp):
                            p.inc(t.transpose(ppt[:, kb * 128:(kb + 1) * 128],
                                              pnb[:, kb * 128:(kb + 1) * 128], idb[:, :]), "pe")
                        pe_pt[j] = p.c["pe"]
                        return (pav, nkp, kb0p, qbp, hlp)

                    def emit_av(t, j, info):
                        pav, nkp, kb0p, qbp, hlp = info
                        p.w(t, "dve", dtk["ptv"][j])
                        if j >= 2:
                            p.w(t, "dve", dtk["avv"][j - 2])
                        for dh in range(2):
                            for kb in range(nkp):
                                p.inc(t.matmul(pav[:, dh * 128:(dh + 1) * 128],
                                               kvb[:, 4096 + (kb0p + kb) * 256 + dh * 128:
                                                      4096 + (kb0p + kb) * 256 + (dh + 1) * 128],
                                               pnb[:, kb * 128:(kb + 1) * 128],
                                               start=(kb == 0), stop=(kb == nkp - 1)), "pe")
                        pe_av[j] = p.c["pe"]

                    with p.blk("ab_at") as blk:
                        if DEBUG_AB and hp == 0 and wtok.get("dbg_seq") == DEBUG_SEQ:
                            @blk.sync
                            def _(s):
                                p.w(s, "dve", dtk["nrm"][0])
                                p.dma(s, dbg["pn0"], pnb[:, :])
                                p.dma(s, dbg["p32"], rsm[:, 0:1024])
                                p.dma(s, dbg["avp"], avp[:, :])
                                wtok["dbg_frz"] = p.c["dmas"]

                        @blk.tensor
                        def _(t):
                            p.wsnap(t, qd)
                            pend = None
                            for i, (qb, hl) in enumerate(iters):
                                par = i % 2
                                psc = sc0 if par == 0 else sc1
                                nk = nkb(qb)
                                kb0 = qb - (nk - 1)
                                if i >= 1:
                                    info = emit_pt_av(t, i - 1)
                                if i >= 2:
                                    p.w(t, "act", etk(i - 2))
                                for kc in range(2):
                                    lhs = biga[:, (hl * 2 + kc) * T + qb * 128:(hl * 2 + kc) * T + qb * 128 + 128]
                                    if nk > 1:
                                        p.inc(t.matmul(psc[:, 0:(nk - 1) * 128], lhs,
                                                       kvb[:, kc * T + kb0 * 128:kc * T + (kb0 + nk - 1) * 128],
                                                       start=(kc == 0), stop=(kc == 1 and nk == 5)), "pe")
                                    p.inc(t.matmul(psc[:, (nk - 1) * 128:nk * 128], lhs,
                                                   kvb[:, kc * T + qb * 128:kc * T + qb * 128 + 128],
                                                   start=(kc == 0 and (nk == 5 or nk == 1)),
                                                   stop=(kc == 1)), "pe")
                                pe_sc[i] = p.c["pe"]
                                if i >= 1:
                                    emit_av(t, i - 1, info)
                            info = emit_pt_av(t, NIT - 1)
                            emit_av(t, NIT - 1, info)

                        @blk.vector
                        def _(v):
                            for i, (qb, hl) in enumerate(iters):
                                par = i % 2
                                psc = sc0 if par == 0 else sc1
                                nk = nkb(qb)
                                if i == 1 and DEBUG_AB and hp == 0 and wtok.get("dbg_seq") == DEBUG_SEQ \
                                        and "dbg_frz" in wtok:
                                    p.w(v, "dmas", wtok.pop("dbg_frz"))
                                p.w(v, "pe", pe_sc[i])
                                if qb >= 4:
                                    p.inc(v.tensor_add(psc[:, 0:128], psc[:, 0:128], mskL[:, :]), "dve")
                                tok = p.inc(v.tensor_add(psc[:, (nk - 1) * 128:nk * 128],
                                                         psc[:, (nk - 1) * 128:nk * 128], mskD[:, :]), "dve")
                                assert tok == dtk["mask"][i]
                                if i >= 2:
                                    p.w(v, "act", etk(i - 2))
                                tok = p.inc(v.reduce_max(avp[:, par * 8:par * 8 + 1], psc[:, 0:nk * 128],
                                                         axis=AX.X, negate=True), "dve")
                                assert tok == dtk["max"][i]
                                p.w(v, "act", etk(i))
                                tok = p.inc(v.reduce_sum(avp[:, par * 8 + 1:par * 8 + 2],
                                                         rsm[:, par * 1024:par * 1024 + nk * 128],
                                                         axis=AX.X), "dve")
                                assert tok == dtk["sum"][i]
                                p.w(v, "dve", dtk["sum"][i])
                                tok = p.inc(v.reciprocal(avp[:, par * 8 + 2:par * 8 + 3],
                                                         avp[:, par * 8 + 1:par * 8 + 2]), "dve")
                                assert tok == dtk["rcp"][i]
                                p.w(v, "dve", dtk["rcp"][i])
                                if i >= 1:
                                    p.w(v, "pe", pe_av[i - 1])
                                tok = p.inc(v.tensor_scalar_mul(pnb[:, 0:nk * 128],
                                                                rsm[:, par * 1024:par * 1024 + nk * 128],
                                                                avp[:, par * 8 + 2:par * 8 + 3]), "dve")
                                assert tok == dtk["nrm"][i]
                                p.w(v, "pe", pe_pt[i])
                                ppt = qt0 if par == 0 else qt1
                                tok = p.inc(v.tensor_copy(pnb[:, 0:nk * 128], ppt[:, 0:nk * 128]), "dve")
                                assert tok == dtk["ptv"][i]
                                p.w(v, "pe", pe_av[i])
                                pav = av0 if par == 0 else av1
                                d1 = biga[:, 8192 + (hl * 2) * T + qb * 128:8192 + (hl * 2) * T + qb * 128 + 128]
                                d2 = biga[:, 8192 + (hl * 2 + 1) * T + qb * 128:8192 + (hl * 2 + 1) * T + qb * 128 + 128]
                                p.inc(v.tensor_copy(d1, pav[:, 0:128]), "dve")
                                tok = p.inc(v.tensor_copy(d2, pav[:, 128:256]), "dve")
                                assert tok == dtk["avv"][i]

                        @blk.scalar
                        def _(a):
                            for i, (qb, hl) in enumerate(iters):
                                par = i % 2
                                psc = sc0 if par == 0 else sc1
                                nk = nkb(qb)
                                p.w(a, "dve", dtk["max"][i])
                                tok = p.inc(a.activation(rsm[:, par * 1024:par * 1024 + nk * 128],
                                                         psc[:, 0:nk * 128], AF.Exp,
                                                         bias=avp[:, par * 8:par * 8 + 1]), "act")
                                assert tok == etk(i)
                atd = p.mark()
                if DEBUG_AB and hp == 0 and wtok.get("dbg_seq") == DEBUG_SEQ and not wtok.get("dbg_done"):
                    emit_dbg("av", biga[:, 8192:16384])
                    wtok["dbg_done"] = True

                with p.pt("ps_w0", [128, 512], F32) as pw0, \
                     p.pt("ps_w1", [128, 512], F32) as pw1, \
                     p.blk("ab_wo2") as blk:
                    dve0 = atd["dve"]
                    wadd = lambda jj: dve0 + jj + 1
                    wtoks = []

                    @blk.tensor
                    def _(t):
                        p.wsnap(t, atd)
                        p.w(t, "dmas", wtok["abwo"])
                        jj = 0
                        for gq in range(4):
                            for m in range(8):
                                ps = pw0 if jj % 2 == 0 else pw1
                                if jj >= 2:
                                    p.w(t, "dve", wadd(jj - 2))
                                for cl in range(4):
                                    c = hp * 4 + cl
                                    p.inc(t.matmul(ps[:, :],
                                                   wsb[:, 8192 + c * 1024 + m * 128:8192 + c * 1024 + (m + 1) * 128],
                                                   biga[:, 8192 + cl * T + gq * 512:8192 + cl * T + (gq + 1) * 512],
                                                   start=(cl == 0), stop=(cl == 3)), "pe")
                                wtoks.append(p.c["pe"])
                                jj += 1

                    @blk.vector
                    def _(v):
                        jj = 0
                        for gq in range(4):
                            for m in range(8):
                                ps = pw0 if jj % 2 == 0 else pw1
                                p.w(v, "pe", wtoks[jj])
                                tok = p.inc(v.tensor_add(xfm[:, m * T + gq * 512:m * T + (gq + 1) * 512],
                                                         xfm[:, m * T + gq * 512:m * T + (gq + 1) * 512],
                                                         ps[:, :]), "dve")
                                assert tok == wadd(jj)
                                jj += 1
            return p.mark()

        def ln_phase():
            t0, t1 = emit_ln()
            wtok["lnh0"], wtok["lnh1"] = t0, t1

        for s_i in range(SEQS):
            wtok["dbg_seq"] = s_i
            ln_phase()
            if DO_ATTNA:
                emit_attn_a()
            if DO_FFN:
                ln_phase()
                emit_ffn()
            if DO_ATTNB:
                ln_phase()
                emit_attn_b()
            if DO_FFN:
                ln_phase()
                emit_ffn()
            fin = p.mark()
            last_oadd = wtok.get("ffn_oadd_last") if DO_FFN else None
            with p.blk("store") as blk:
                @blk.sync
                def _(s):
                    if last_oadd is not None and fin["dve"] == last_oadd[7]:
                        p.w(s, "pe", fin["pe"])
                        p.w(s, "act", fin["act"])
                        for c in range(NC):
                            p.w(s, "dve", last_oadd[c])
                            p.dma(s, y[s_i, c], xfm[:, c * T:(c + 1) * T])
                    else:
                        p.wsnap(s, fin)
                        for c in range(NC):
                            p.dma(s, y[s_i, c], xfm[:, c * T:(c + 1) * T])
                    if s_i + 1 < SEQS:
                        for c in range(NC):
                            p.dma(s, xfm[:, c * T:(c + 1) * T], xin[s_i + 1, c])
            wtok["x"] = p.c["dmas"]

        with p.blk("fin") as blk:
            final = p.c["dmas"]

            @blk.sync
            def _(s):
                s.wait_ge(p.s["dmas"], final)
    return nc


def _fmaj(wT):
    O = wT.shape[1]
    return np.ascontiguousarray(
        wT.reshape(8, 128, O).transpose(1, 0, 2).reshape(128, 8 * O)).astype(np.float32)


def _consts():
    inv = (np.float32(1.0) / np.float32(1e6) ** (np.arange(0, 256, 2, dtype=np.float32) / np.float32(256))).astype(np.float32)
    t = np.arange(T, dtype=np.float32)
    fr = np.outer(t, inv).astype(np.float32)
    co = np.cos(fr.astype(np.float64)).astype(np.float32).T.copy()
    si = np.sin(fr.astype(np.float64)).astype(np.float32).T.copy()
    cs = np.concatenate([co, si], axis=1)
    qi = np.arange(128)[:, None]
    kj = np.arange(128)[None, :]
    mL = np.where(kj >= qi, 0.0, NEG).astype(np.float32)
    mD = np.where(kj <= qi, 0.0, NEG).astype(np.float32)
    mk = np.concatenate([mL, mD], axis=1)
    idb = np.eye(128, dtype=BFNP)
    onb = np.zeros((128, 2), dtype=BFNP)
    onb[:, 0] = BFNP(1.0 / 1024.0)
    onb[:, 1] = BFNP(1.0 / 256.0)
    onr = np.ones((128, 128), dtype=np.float32)
    return cs, mk, idb, onb, onr


def kernel(L0_attn_a_x, L0_attn_a_W_theta, L0_attn_a_W_phi, L0_attn_a_W_g,
           L0_attn_b_Wq, L0_attn_b_Wk, L0_attn_b_Wv, L0_attn_b_Wo,
           L0_attn_b_q_gamma, L0_attn_b_k_gamma,
           L0_ffn_rail_maker, L0_ffn_rail_up_proj, L0_ffn_w_cproj):
    x = np.asarray(L0_attn_a_x, dtype=np.float32)
    xl = np.ascontiguousarray(x.reshape(32, T, NC, 128).transpose(0, 2, 3, 1))

    wth = _fmaj(np.asarray(L0_attn_a_W_theta, np.float32).T)
    wph = _fmaj(np.asarray(L0_attn_a_W_phi, np.float32).T)
    wg = _fmaj(np.asarray(L0_attn_a_W_g, np.float32).T)

    deint = np.concatenate([np.arange(0, 256, 2), np.arange(1, 256, 2)])
    qcols = np.concatenate([h * 256 + deint for h in range(4)])
    wq = _fmaj(np.asarray(L0_attn_b_Wq, np.float32).T[:, qcols])
    wk = _fmaj(np.asarray(L0_attn_b_Wk, np.float32).T[:, deint])
    wv = _fmaj(np.asarray(L0_attn_b_Wv, np.float32).T)
    wo = _fmaj(np.asarray(L0_attn_b_Wo, np.float32).T)

    maker = np.asarray(L0_ffn_rail_maker, np.float32)
    up = np.asarray(L0_ffn_rail_up_proj, np.float32)
    w1t = np.zeros((8, 128, 512), np.float32)
    w2t = np.zeros((32, 128, 512), np.float32)
    for h in range(4):
        for rr in range(64):
            c, rp = rr // 8, rr % 8
            w1t[c, rp * 16:rp * 16 + 16, rp * 64 + h * 16:rp * 64 + h * 16 + 16] = maker[h, rr]
            jy, rrel = (rr % 8) // 2, rr % 2
            yc = c * 4 + jy
            w2t[yc, rrel * 64 + h * 16:rrel * 64 + h * 16 + 16,
                rrel * 256 + h * 64:rrel * 256 + h * 64 + 64] = up[h, rr].T
    w1 = np.ascontiguousarray(w1t.transpose(1, 0, 2).reshape(128, 4096))
    w2 = np.ascontiguousarray(w2t.transpose(1, 0, 2).reshape(128, 16384))

    wc = np.asarray(L0_ffn_w_cproj, np.float32)
    ZC, PP = np.meshgrid(np.arange(128), np.arange(128), indexing="ij")
    cc_ = ZC // 16
    jy_ = (ZC % 16) // 4
    jz_ = ZC % 4
    v_ = jz_ * 128 + PP
    rrel_ = v_ // 256
    hh_ = (v_ % 256) // 64
    kk_ = v_ % 64
    r_ = cc_ * 8 + 2 * jy_ + rrel_
    zcol = (hh_ * 4096 + r_ * 64 + kk_).reshape(-1)
    wcT = wc[:, zcol].T
    wcd = np.ascontiguousarray(
        wcT.reshape(128, 128, 8, 128).transpose(2, 1, 0, 3).reshape(8, 128, 16384))

    qg = np.asarray(L0_attn_b_q_gamma, np.float32)
    kg = np.asarray(L0_attn_b_k_gamma, np.float32)
    qg2 = np.ascontiguousarray(np.stack([qg[0::2], qg[1::2]], axis=1))
    kg2 = np.ascontiguousarray(np.stack([kg[0::2], kg[1::2]], axis=1))

    csC, mkC, idC, onC, orC = _consts()

    nc = build_nc()
    in_maps = []
    for core in range(NCORE):
        in_maps.append({
            "xin": np.ascontiguousarray(xl[core * SEQS:(core + 1) * SEQS]),
            "wth": wth, "wph": wph, "wg": wg, "wq": wq, "wk": wk, "wv": wv,
            "wo": wo, "w1": w1, "w2": w2, "wc": wcd, "cs": csC, "mk": mkC,
            "idb": idC, "onb": onC, "onr": orC, "qg": qg2, "kg": kg2,
        })
    res = run_bass_kernel_spmd(nc, in_maps, core_ids=list(range(NCORE)))
    out = np.empty((32, T, D), dtype=np.float32)
    for core in range(NCORE):
        yv = np.asarray(res.results[core]["y"])
        out[core * SEQS:(core + 1) * SEQS] = yv.transpose(0, 3, 1, 2).reshape(SEQS, T, D)
    return out


# version 03
# speedup vs baseline: 1.0480x; 1.0480x over previous
import numpy as np
import ml_dtypes
import concourse.bass as bass
import concourse.mybir as mybir
from concourse.bass_utils import run_bass_kernel_spmd

BFNP = ml_dtypes.bfloat16
F32 = mybir.dt.float32
BF16 = mybir.dt.bfloat16
AF = mybir.ActivationFunctionType
OP = mybir.AluOpType
AX = mybir.AxisListType

NCORE = 8
SEQS = 4
T = 2048
D = 1024
NC = 8
TT = 512
NTILE = T // TT
NEG = -1.0e30

DO_ATTNA = True
DO_FFN = True
DO_ATTNB = True
DEBUG_AB = False
DEBUG_SEQ = 0


class B:
    def __init__(self, nc):
        self.nc = nc
        self.s = {}
        self.c = {}

    def blk(self, name):
        self._bn = getattr(self, "_bn", 0) + 1
        return self.nc.Block("%s_%d" % (name, self._bn))

    def pt(self, name, shape, dt):
        self._pn = getattr(self, "_pn", 0) + 1
        return self.nc.psum_tensor("%s_%d" % (name, self._pn), shape, dt)

    def mksem(self, stack, *names):
        for n in names:
            self.s[n] = stack.enter_context(self.nc.semaphore(n))
            self.c[n] = 0

    def inc(self, inst, name, delta=1):
        inst.then_inc(self.s[name], delta)
        self.c[name] += delta
        return self.c[name]

    def dma(self, eng, out, in_, q="dmas"):
        return self.inc(eng.dma_start(out, in_), q, 16)

    def w(self, eng, name, tok):
        if tok > 0:
            eng.wait_ge(self.s[name], tok)

    def mark(self):
        return dict(self.c)

    def wsnap(self, eng, snap):
        for n in ("pe", "dve", "act", "pool"):
            self.w(eng, n, snap[n])


def build_nc():
    nc = bass.Bass("TRN2", target_bir_lowering=False)
    p = B(nc)

    xin = nc.dram_tensor("xin", [SEQS, NC, 128, T], F32, kind="ExternalInput").ap()
    y = nc.dram_tensor("y", [SEQS, NC, 128, T], F32, kind="ExternalOutput").ap()
    wth_i = nc.dram_tensor("wth", [128, 2048], F32, kind="ExternalInput").ap()
    wph_i = nc.dram_tensor("wph", [128, 2048], F32, kind="ExternalInput").ap()
    wg_i = nc.dram_tensor("wg", [128, 8192], F32, kind="ExternalInput").ap()
    wq_i = nc.dram_tensor("wq", [128, 8192], F32, kind="ExternalInput").ap()
    wk_i = nc.dram_tensor("wk", [128, 2048], F32, kind="ExternalInput").ap()
    wv_i = nc.dram_tensor("wv", [128, 2048], F32, kind="ExternalInput").ap()
    wo_i = nc.dram_tensor("wo", [128, 8192], F32, kind="ExternalInput").ap()
    w1_i = nc.dram_tensor("w1", [128, 4096], F32, kind="ExternalInput").ap()
    w2_i = nc.dram_tensor("w2", [128, 16384], F32, kind="ExternalInput").ap()
    wc_i = nc.dram_tensor("wc", [8, 128, 16384], F32, kind="ExternalInput").ap()
    cs_i = nc.dram_tensor("cs", [128, 4096], F32, kind="ExternalInput").ap()
    mk_i = nc.dram_tensor("mk", [128, 256], F32, kind="ExternalInput").ap()
    id_i = nc.dram_tensor("idb", [128, 128], BF16, kind="ExternalInput").ap()
    on_i = nc.dram_tensor("onb", [128, 2], BF16, kind="ExternalInput").ap()
    or_i = nc.dram_tensor("onr", [128, 128], F32, kind="ExternalInput").ap()
    qg_i = nc.dram_tensor("qg", [128, 2], F32, kind="ExternalInput").ap()
    kg_i = nc.dram_tensor("kg", [128, 2], F32, kind="ExternalInput").ap()

    dbg = {}
    if DEBUG_AB:
        for nm, cols in [("ln3", 16384), ("kv", 8192), ("qp", 8192), ("av", 8192),
                         ("pn0", 640), ("p32", 1024), ("avp", 32)]:
            dbg[nm] = nc.dram_tensor("dbg_" + nm, [128, cols],
                                     F32 if nm in ("p32", "avp") else BF16,
                                     kind="ExternalOutput").ap()
    wth_b = nc.dram_tensor("wth_b", [128, 2048], BF16, kind="Internal").ap()
    wph_b = nc.dram_tensor("wph_b", [128, 2048], BF16, kind="Internal").ap()
    wg_b = nc.dram_tensor("wg_b", [128, 8192], BF16, kind="Internal").ap()
    wq_b = nc.dram_tensor("wq_b", [128, 8192], BF16, kind="Internal").ap()
    wk_b = nc.dram_tensor("wk_b", [128, 2048], BF16, kind="Internal").ap()
    wv_b = nc.dram_tensor("wv_b", [128, 2048], BF16, kind="Internal").ap()
    wo_b = nc.dram_tensor("wo_b", [128, 8192], BF16, kind="Internal").ap()
    w1_b = nc.dram_tensor("w1_b", [128, 4096], BF16, kind="Internal").ap()
    w2_b = nc.dram_tensor("w2_b", [128, 16384], BF16, kind="Internal").ap()
    wc_b = nc.dram_tensor("wc_b", [8, 128, 16384], BF16, kind="Internal").ap()

    import contextlib
    with contextlib.ExitStack() as st:
        sbt = lambda name, shape, dt: st.enter_context(nc.sbuf_tensor(name, shape, dt))
        xfm = sbt("xfm", [128, NC * T], F32)
        lnb = sbt("lnb", [128, NC * T], BF16)
        cs = sbt("cs_sb", [128, 2 * T], F32)
        wsb = sbt("wsb", [128, 16384], BF16)
        biga = sbt("biga", [128, 16384], BF16)
        kvb = sbt("kvb", [128, 8192], BF16)
        scr32 = sbt("scr32", [128, 1024], F32)
        rsm = sbt("rsm", [128, 2048], F32)
        pnb = sbt("pnb", [128, 640], BF16)
        avp = sbt("avp", [128, 32], F32)
        mskL = sbt("mskL", [128, 128], F32)
        mskD = sbt("mskD", [128, 128], F32)
        idb = sbt("idb_sb", [128, 128], BF16)
        onb = sbt("onb_sb", [128, 2], BF16)
        onr = sbt("onr_sb", [128, 128], F32)
        qgs = sbt("qgs", [128, 2], F32)
        kgs = sbt("kgs", [128, 2], F32)

        p.mksem(st, "pe", "dve", "act", "pool", "dmas", "dmag")
        wtok = {}

        with p.blk("startup") as blk:
            @blk.gpsimd
            def _(g):
                wtok["wth"] = p.dma(g, wth_b, wth_i, "dmag")
                wtok["wph"] = p.dma(g, wph_b, wph_i, "dmag")
                wtok["wg"] = p.dma(g, wg_b, wg_i, "dmag")
                wtok["w1"] = p.dma(g, w1_b, w1_i, "dmag")
                wtok["w2"] = p.dma(g, w2_b, w2_i, "dmag")
                wtok["wq"] = p.dma(g, wq_b, wq_i, "dmag")
                wtok["wk"] = p.dma(g, wk_b, wk_i, "dmag")
                wtok["wv"] = p.dma(g, wv_b, wv_i, "dmag")
                wtok["wo"] = p.dma(g, wo_b, wo_i, "dmag")
                for m in range(8):
                    wtok["wc%d" % m] = p.dma(g, wc_b[m], wc_i[m], "dmag")
                wtok["castall"] = p.c["dmag"]

            @blk.sync
            def _(s):
                p.dma(s, cs[:, :], cs_i)
                p.dma(s, mskL[:, :], mk_i[:, 0:128])
                p.dma(s, mskD[:, :], mk_i[:, 128:256])
                p.dma(s, idb[:, :], id_i)
                p.dma(s, onb[:, :], on_i)
                p.dma(s, onr[:, :], or_i)
                p.dma(s, qgs[:, :], qg_i)
                p.dma(s, kgs[:, :], kg_i)
                for c in range(NC):
                    p.dma(s, xfm[:, c * T:(c + 1) * T], xin[0, c])
                wtok["x"] = p.c["dmas"]

            @blk.vector
            def _(v):
                p.w(v, "dmas", wtok["x"])
                p.inc(v.tensor_scalar_mul(qgs[:, :], qgs[:, :], 1.0 / 16.0), "dve")

        def emit_ln():
            snap = p.mark()
            pe0, dv0, ac0 = snap["pe"], snap["dve"], snap["act"]
            tok_h0, tok_h1 = dv0 + 38, dv0 + 54
            with p.pt("ps_m0", [1, 1024], F32) as pm0, \
                 p.pt("ps_e0", [1, 1024], F32) as pe_0, \
                 p.pt("ps_m1", [1, 1024], F32) as pm1, \
                 p.pt("ps_e1", [1, 1024], F32) as pe_1, \
                 p.blk("ln_st") as blk:

                @blk.vector
                def _(v):
                    p.w(v, "dmas", wtok["x"])
                    p.w(v, "pe", pe0)
                    p.w(v, "act", ac0)
                    for hf in range(2):
                        o = hf * 1024
                        for c in range(NC):
                            if c >= 4:
                                p.w(v, "pe", pe0 + hf * 32 + 2 * (c - 3))
                            slot = hf * 8192 + (c % 4) * 1024
                            tok = p.inc(v.tensor_copy(biga[:, slot:slot + 1024],
                                                      xfm[:, c * T + o:c * T + o + 1024]), "dve")
                            assert tok == dv0 + hf * 8 + c + 1

                @blk.scalar
                def _(a):
                    p.w(a, "dmas", wtok["x"])
                    p.w(a, "pe", pe0)
                    p.w(a, "dve", dv0)
                    for hf in range(2):
                        o = hf * 1024
                        for c in range(NC):
                            if c >= 4:
                                p.w(a, "pe", pe0 + hf * 32 + 16 + 2 * (c - 3))
                            slot = hf * 8192 + 4096 + (c % 4) * 1024
                            tok = p.inc(a.activation(biga[:, slot:slot + 1024],
                                                     xfm[:, c * T + o:c * T + o + 1024],
                                                     AF.Square), "act")
                            assert tok == ac0 + hf * 8 + c + 1

                @blk.tensor
                def _(t):
                    for hf in range(2):
                        pm = pm0 if hf == 0 else pm1
                        pex = pe_0 if hf == 0 else pe_1
                        for c in range(NC):
                            p.w(t, "dve", dv0 + hf * 8 + c + 1)
                            slot = hf * 8192 + (c % 4) * 1024
                            for n in range(2):
                                p.inc(t.matmul(pm[:, n * 512:(n + 1) * 512], onb[:, 0:1],
                                               biga[:, slot + n * 512:slot + (n + 1) * 512],
                                               start=(c == 0), stop=(c == NC - 1)), "pe")
                        for c in range(NC):
                            p.w(t, "act", ac0 + hf * 8 + c + 1)
                            slot = hf * 8192 + 4096 + (c % 4) * 1024
                            for n in range(2):
                                p.inc(t.matmul(pex[:, n * 512:(n + 1) * 512], onb[:, 0:1],
                                               biga[:, slot + n * 512:slot + (n + 1) * 512],
                                               start=(c == 0), stop=(c == NC - 1)), "pe")
                        assert p.c["pe"] == pe0 + 32 * (hf + 1)

                @blk.vector
                def _(v):
                    for hf in range(2):
                        o = hf * 1024
                        pm = pm0 if hf == 0 else pm1
                        pex = pe_0 if hf == 0 else pe_1
                        p.w(v, "pe", pe0 + 32 * (hf + 1))
                        p.inc(v.tensor_copy(rsm[0:1, o:o + 1024], pm[0:1, :]), "dve")
                        p.w(v, "act", ac0 + 16 + hf + 1)
                        p.inc(v.scalar_tensor_tensor(rsm[64:65, o:o + 1024], pex[0:1, :], 1e-5,
                                                     rsm[64:65, o:o + 1024], OP.add, OP.subtract), "dve")
                        p.w(v, "act", ac0 + 18 + hf + 1)
                        tok = p.inc(v.reciprocal(rsm[32:33, o:o + 1024], rsm[64:65, o:o + 1024]), "dve")
                        assert tok == dv0 + 17 + hf * 3 + 2
                        p.w(v, "dve", tok)

                @blk.scalar
                def _(a):
                    for hf in range(2):
                        o = hf * 1024
                        pm = pm0 if hf == 0 else pm1
                        p.w(a, "pe", pe0 + 32 * (hf + 1))
                        tok = p.inc(a.activation(rsm[64:65, o:o + 1024], pm[0:1, :], AF.Square), "act")
                        assert tok == ac0 + 16 + hf + 1
                    for hf in range(2):
                        o = hf * 1024
                        p.w(a, "dve", dv0 + 17 + hf * 3 + 1)
                        tok = p.inc(a.activation(rsm[64:65, o:o + 1024], rsm[64:65, o:o + 1024],
                                                 AF.Sqrt), "act")
                        assert tok == ac0 + 18 + hf + 1

            with p.pt("ps_bm0", [128, 1024], F32) as bm0, \
                 p.pt("ps_br0", [128, 1024], F32) as br0, \
                 p.pt("ps_bm1", [128, 1024], F32) as bm1, \
                 p.pt("ps_br1", [128, 1024], F32) as br1, \
                 p.blk("ln_ap") as blk:

                @blk.tensor
                def _(t):
                    for hf in range(2):
                        p.w(t, "dve", dv0 + 19 if hf == 0 else dv0 + 22)
                        p.w(t, "act", ac0 + 19 if hf == 0 else ac0 + 20)
                        o = hf * 1024
                        bm = bm0 if hf == 0 else bm1
                        br = br0 if hf == 0 else br1
                        for n in range(2):
                            p.inc(t.matmul(bm[:, n * 512:(n + 1) * 512], onr[0:1, :],
                                           rsm[0:1, o + n * 512:o + (n + 1) * 512],
                                           start=True, stop=True), "pe")
                        for n in range(2):
                            p.inc(t.matmul(br[:, n * 512:(n + 1) * 512], onr[32:33, :],
                                           rsm[32:33, o + n * 512:o + (n + 1) * 512],
                                           start=True, stop=True), "pe")
                        assert p.c["pe"] == pe0 + 64 + 4 * (hf + 1)

                @blk.vector
                def _(v):
                    for hf in range(2):
                        o = hf * 1024
                        bm = bm0 if hf == 0 else bm1
                        br = br0 if hf == 0 else br1
                        p.w(v, "pe", pe0 + 64 + 4 * (hf + 1))
                        for c in range(NC):
                            p.inc(v.tensor_sub(scr32[:, :], xfm[:, c * T + o:c * T + o + 1024],
                                               bm[:, :]), "dve")
                            p.inc(v.tensor_mul(lnb[:, c * T + o:c * T + o + 1024],
                                               scr32[:, :], br[:, :]), "dve")
                        assert p.c["dve"] == (tok_h0 if hf == 0 else tok_h1)
            wtok["ln_ct"] = {(hf, c): dv0 + 22 + hf * 16 + 2 * (c + 1)
                             for hf in range(2) for c in range(NC)}
            return tok_h0, tok_h1

        def emit_attn_a():
            snap = p.mark()
            A0, B0 = 0, 8192
            TH0, S0, PH0, G0 = 0, 4096, 6144, 6656
            with p.blk("aa_w") as blk:
                @blk.sync
                def _(s):
                    p.w(s, "dmag", wtok["castall"])
                    p.wsnap(s, snap)
                    p.dma(s, wsb[:, B0:B0 + 2048], wth_b)
                    p.dma(s, wsb[:, B0 + 2048:B0 + 4096], wph_b)
                    p.w(s, "dmag", wtok["castall"])
                    wtok["aa"] = p.dma(s, wsb[:, A0:A0 + 8192], wg_b)

            with p.pt("ps_th0", [128, T], F32) as pt0, \
                 p.pt("ps_th1", [128, T], F32) as pt1, \
                 p.blk("aa_th") as blk:
                ttoks = [0, 0]

                @blk.tensor
                def _(t):
                    p.w(t, "act", snap["act"])
                    p.w(t, "dve", wtok["ln_ct"][(0, 0)])
                    p.w(t, "dmas", wtok["aa"])
                    for l in range(2):
                        ps = pt0 if l == 0 else pt1
                        for c in range(NC):
                            for n in range(4):
                                p.w(t, "dve", wtok["ln_ct"][(n // 2, c)])
                                p.inc(t.matmul(ps[:, n * 512:(n + 1) * 512],
                                               wsb[:, B0 + c * 256 + l * 128:B0 + c * 256 + l * 128 + 128],
                                               lnb[:, c * T + n * 512:c * T + (n + 1) * 512],
                                               start=(c == 0), stop=(c == NC - 1)), "pe")
                        ttoks[l] = p.c["pe"]

                @blk.scalar
                def _(a):
                    for l in range(2):
                        p.w(a, "pe", ttoks[l])
                        ps = pt0 if l == 0 else pt1
                        p.inc(a.copy(biga[:, TH0 + l * 2048:TH0 + (l + 1) * 2048], ps[:, :]), "act")
            th = p.mark()

            with p.pt("ps_ph", [128, 256], F32) as pph, \
                 p.pt("ps_g", [128, 1024], F32) as pg, \
                 p.pt("ps_s0", [128, 1024], F32) as ps0, \
                 p.pt("ps_s1", [128, 1024], F32) as ps1, \
                 p.blk("aa_s") as blk:
                dve0 = th["dve"]
                ev = lambda tt: dve0 + 2 * (tt + 1)
                pg_tok, s_tok = [], []

                @blk.tensor
                def _(t):
                    p.wsnap(t, th)
                    for tt in range(16):
                        if tt >= 1:
                            p.w(t, "dve", ev(tt - 1))
                        for c in range(NC):
                            p.inc(t.matmul(pph[:, :],
                                           lnb[:, c * T + tt * 128:c * T + tt * 128 + 128],
                                           wsb[:, B0 + 2048 + c * 256:B0 + 2048 + (c + 1) * 256],
                                           start=(c == 0), stop=(c == NC - 1)), "pe")
                        for c in range(NC):
                            for n in range(2):
                                p.inc(t.matmul(pg[:, n * 512:(n + 1) * 512],
                                               lnb[:, c * T + tt * 128:c * T + tt * 128 + 128],
                                               wsb[:, A0 + c * 1024 + n * 512:A0 + c * 1024 + (n + 1) * 512],
                                               start=(c == 0), stop=(c == NC - 1)), "pe")
                        pg_tok.append(p.c["pe"])
                        if tt >= 1:
                            j = tt - 1
                            par = j % 2
                            for l in range(2):
                                ps = ps0 if l == 0 else ps1
                                for n in range(2):
                                    p.inc(t.matmul(ps[:, n * 512:(n + 1) * 512],
                                                   biga[:, PH0 + par * 256 + l * 128:PH0 + par * 256 + l * 128 + 128],
                                                   biga[:, G0 + par * 1024 + n * 512:G0 + par * 1024 + (n + 1) * 512],
                                                   start=(j == 0), stop=(j == 15)), "pe")
                            s_tok.append(p.c["pe"])
                    p.w(t, "dve", ev(15))
                    j, par = 15, 15 % 2
                    for l in range(2):
                        ps = ps0 if l == 0 else ps1
                        for n in range(2):
                            p.inc(t.matmul(ps[:, n * 512:(n + 1) * 512],
                                           biga[:, PH0 + par * 256 + l * 128:PH0 + par * 256 + l * 128 + 128],
                                           biga[:, G0 + par * 1024 + n * 512:G0 + par * 1024 + (n + 1) * 512],
                                           start=False, stop=True), "pe")
                    s_tok.append(p.c["pe"])

                @blk.vector
                def _(v):
                    for tt in range(16):
                        par = tt % 2
                        p.w(v, "pe", pg_tok[tt])
                        if tt >= 2:
                            p.w(v, "pe", s_tok[tt - 2])
                        p.inc(v.tensor_copy(biga[:, PH0 + par * 256:PH0 + par * 256 + 256], pph[:, :]), "dve")
                        tok = p.inc(v.tensor_copy(biga[:, G0 + par * 1024:G0 + par * 1024 + 1024], pg[:, :]), "dve")
                        assert tok == ev(tt)
                    p.w(v, "pe", s_tok[15])
                    for l in range(2):
                        ps = ps0 if l == 0 else ps1
                        p.inc(v.tensor_scalar_mul(biga[:, S0 + l * 1024:S0 + (l + 1) * 1024],
                                                  ps[:, :], 1.0 / 32.0), "dve")
            sev = p.mark()

            with p.pt("ps_o0", [128, T], F32) as po0, \
                 p.pt("ps_o1", [128, T], F32) as po1, \
                 p.blk("aa_o") as blk:
                dve0 = sev["dve"]
                addt = lambda m: dve0 + m + 1
                mtoks = []

                @blk.tensor
                def _(t):
                    p.wsnap(t, sev)
                    for m in range(NC):
                        ps = po0 if m % 2 == 0 else po1
                        if m >= 2:
                            p.w(t, "dve", addt(m - 2))
                        for l in range(2):
                            for n in range(4):
                                p.inc(t.matmul(ps[:, n * 512:(n + 1) * 512],
                                               biga[:, S0 + l * 1024 + m * 128:S0 + l * 1024 + m * 128 + 128],
                                               biga[:, TH0 + l * 2048 + n * 512:TH0 + l * 2048 + (n + 1) * 512],
                                               start=(l == 0), stop=(l == 1)), "pe")
                        mtoks.append(p.c["pe"])

                @blk.vector
                def _(v):
                    for m in range(NC):
                        ps = po0 if m % 2 == 0 else po1
                        p.w(v, "pe", mtoks[m])
                        tok = p.inc(v.tensor_add(xfm[:, m * T:(m + 1) * T],
                                                 xfm[:, m * T:(m + 1) * T], ps[:, :]), "dve")
                        assert tok == addt(m)
            return p.mark()

        def emit_ffn():
            snap = p.mark()
            A0, WC0, B0, WC1 = 0, 4096, 8192, 12288
            with p.blk("ff_w") as blk:
                @blk.sync
                def _(s):
                    p.w(s, "dmag", wtok["castall"])
                    p.w(s, "pe", snap["pe"])
                    wtok["ffw1"] = p.dma(s, wsb[:, B0:B0 + 4096], w1_b)

            m0 = p.mark()
            pe0, dve0, act0, dm0, dg0 = m0["pe"], m0["dve"], m0["act"], m0["dmas"], m0["dmag"]
            pe_ymm = lambda tg, yci: pe0 + tg * 296 + yci * 5 + 1
            pe_rail = lambda tg, yci: pe0 + tg * 296 + yci * 5 + 5
            pe_cp = lambda tg, m: pe0 + tg * 296 + 40 + (m + 1) * 32
            yev = lambda tg, yci: dve0 + tg * 16 + yci + 1
            oadd = lambda tg, m: dve0 + tg * 16 + 8 + m + 1
            gel = lambda tg, zi: act0 + tg * 32 + zi + 1
            dm_w2 = lambda g: dm0 + (g * 17 + 1) * 16
            dm_wce = lambda tg, m: dm0 + ((tg // 4) * 17 + 1 + (tg % 4) * 4 + m // 2 + 1) * 16
            dm_wco = lambda tg, m: dg0 + (tg * 4 + (m - 1) // 2 + 1) * 16

            with p.pt("ps_y0", [128, TT], F32) as py0, \
                 p.pt("ps_y1", [128, TT], F32) as py1, \
                 p.pt("ps_z0", [128, TT], F32) as pz0, \
                 p.pt("ps_z1", [128, TT], F32) as pz1, \
                 p.pt("ps_oA", [128, TT], F32) as poA, \
                 p.pt("ps_oB", [128, TT], F32) as poB:
                for g in range(4):
                    for tile in range(NTILE):
                        tg = g * 4 + tile
                        with p.blk("ff") as blk:
                            @blk.sync
                            def _(s):
                                if tile == 0:
                                    if g == 0:
                                        p.w(s, "pe", snap["pe"])
                                    else:
                                        p.w(s, "pe", pe_rail(g * 4 - 1, 7))
                                    tok = p.dma(s, wsb[:, A0:A0 + 4096],
                                                w2_b[:, g * 4096:(g + 1) * 4096])
                                    assert tok == dm_w2(g)
                                for m in range(0, 8, 2):
                                    je = tg * 4 + m // 2
                                    if je == 0:
                                        p.w(s, "pe", snap["pe"])
                                    else:
                                        jp = je - 1
                                        p.w(s, "pe", pe_cp(jp // 4, 2 * (jp % 4)))
                                    tok = p.dma(s, wsb[:, WC0:WC0 + 4096],
                                                wc_b[m][:, g * 4096:(g + 1) * 4096])
                                    assert tok == dm_wce(tg, m)

                            @blk.gpsimd
                            def _(gq):
                                for m in range(1, 8, 2):
                                    jo = tg * 4 + (m - 1) // 2
                                    if jo == 0:
                                        p.w(gq, "pe", snap["pe"])
                                    else:
                                        jp = jo - 1
                                        p.w(gq, "pe", pe_cp(jp // 4, 2 * (jp % 4) + 1))
                                    tok = p.dma(gq, wsb[:, WC1:WC1 + 4096],
                                                wc_b[m][:, g * 4096:(g + 1) * 4096], "dmag")
                                    assert tok == dm_wco(tg, m)

                            @blk.tensor
                            def _(t):
                                if tg == 0:
                                    p.w(t, "dmas", wtok["ffw1"])
                                    p.w(t, "act", snap["act"])
                                if tile == 0:
                                    p.w(t, "dmas", dm_w2(g))
                                for yci in range(8):
                                    yc = g * 8 + yci
                                    c, jy = yc // 4, yc % 4
                                    p.w(t, "dve", wtok["ln_ct"][(tile // 2, c)])
                                    ypar = yci % 2
                                    psy = py0 if ypar == 0 else py1
                                    gi = tg * 8 + yci
                                    if gi >= 2:
                                        p.w(t, "dve", yev((gi - 2) // 8, (gi - 2) % 8))
                                    tok = p.inc(t.matmul(psy[:, :],
                                                         wsb[:, B0 + c * 512 + jy * 128:B0 + c * 512 + (jy + 1) * 128],
                                                         lnb[:, c * T + tile * TT:c * T + (tile + 1) * TT],
                                                         start=True, stop=True), "pe")
                                    assert tok == pe_ymm(tg, yci)
                                    p.w(t, "dve", yev(tg, yci))
                                    for jj in range(4):
                                        zi = yci * 4 + jj
                                        psz = pz0 if zi % 2 == 0 else pz1
                                        zz = tg * 32 + zi
                                        if zz >= 2:
                                            p.w(t, "act", gel((zz - 2) // 32, (zz - 2) % 32))
                                        p.inc(t.matmul(psz[:, :],
                                                       wsb[:, A0 + yci * 512 + jj * 128:A0 + yci * 512 + (jj + 1) * 128],
                                                       kvb[:, ypar * 512:(ypar + 1) * 512],
                                                       start=True, stop=True), "pe")
                                    assert p.c["pe"] == pe_rail(tg, yci)
                                for m in range(8):
                                    pso = poA if m % 2 == 0 else poB
                                    if tg == 0 and m == 0:
                                        p.w(t, "dve", wtok["lnh1"])
                                    if m >= 2:
                                        p.w(t, "dve", oadd(tg, m - 2))
                                    elif tg >= 1:
                                        p.w(t, "dve", oadd(tg - 1, m + 6))
                                    if m % 2 == 0:
                                        p.w(t, "dmas", dm_wce(tg, m))
                                    else:
                                        p.w(t, "dmag", dm_wco(tg, m))
                                    wcb = WC0 if m % 2 == 0 else WC1
                                    for zi in range(32):
                                        if m == 0:
                                            p.w(t, "act", gel(tg, zi))
                                        p.inc(t.matmul(pso[:, :],
                                                       wsb[:, wcb + zi * 128:wcb + (zi + 1) * 128],
                                                       biga[:, zi * 512:(zi + 1) * 512],
                                                       start=(zi == 0), stop=(zi == 31)), "pe")
                                    assert p.c["pe"] == pe_cp(tg, m)

                            @blk.vector
                            def _(v):
                                for yci in range(8):
                                    ypar = yci % 2
                                    psy = py0 if ypar == 0 else py1
                                    p.w(v, "pe", pe_ymm(tg, yci))
                                    tok = p.inc(v.tensor_copy(kvb[:, ypar * 512:(ypar + 1) * 512],
                                                              psy[:, :]), "dve")
                                    assert tok == yev(tg, yci)
                                for m in range(8):
                                    pso = poA if m % 2 == 0 else poB
                                    p.w(v, "pe", pe_cp(tg, m))
                                    tok = p.inc(v.tensor_add(
                                        xfm[:, m * T + tile * TT:m * T + (tile + 1) * TT],
                                        xfm[:, m * T + tile * TT:m * T + (tile + 1) * TT],
                                        pso[:, :]), "dve")
                                    assert tok == oadd(tg, m)

                            @blk.scalar
                            def _(a):
                                for zi in range(32):
                                    psz = pz0 if zi % 2 == 0 else pz1
                                    yci = zi // 4
                                    p.w(a, "pe", pe_rail(tg, yci) - (3 - zi % 4))
                                    tok = p.inc(a.activation(biga[:, zi * 512:(zi + 1) * 512],
                                                             psz[:, :], AF.Gelu), "act")
                                    assert tok == gel(tg, zi)
            wtok["ffn_oadd_last"] = {m: oadd(15, m) for m in range(8)}
            return p.mark()

        def emit_dbg(nm, src_ap):
            m0 = p.mark()
            with p.blk("dbg") as blk:
                @blk.sync
                def _(s):
                    p.wsnap(s, m0)
                    p.dma(s, dbg[nm], src_ap)

        def emit_attn_b():
            if DEBUG_AB and wtok.get("dbg_seq") == DEBUG_SEQ and not wtok.get("dbg_done"):
                emit_dbg("ln3", lnb[:, :])
            snap = p.mark()
            A0, B0 = 0, 8192
            with p.blk("ab_w") as blk:
                @blk.sync
                def _(s):
                    p.w(s, "dmag", wtok["castall"])
                    p.wsnap(s, snap)
                    p.dma(s, wsb[:, A0:A0 + 8192], wq_b)
                    p.dma(s, wsb[:, B0:B0 + 2048], wk_b)
                    wtok["abqkv"] = p.dma(s, wsb[:, B0 + 2048:B0 + 4096], wv_b)

            def proj_norm_rope(kind, h, gam, dst, doff):
                for hf in range(2):
                    o = hf * 1024
                    sn = p.mark()
                    pe0, dv0, ac0 = sn["pe"], sn["dve"], sn["act"]
                    with p.pt("pp_e", [128, 1024], F32) as pqe, \
                         p.pt("pp_o", [128, 1024], F32) as pqo, \
                         p.pt("pp_ms", [1, 1024], F32) as pms, \
                         p.pt("pp_rb", [128, 1024], F32) as prb, \
                         p.blk("pnr") as blk:
                        wstride = 1024 if kind == "q" else 256

                        @blk.tensor
                        def _(t):
                            p.wsnap(t, snap)
                            p.w(t, "dmas", wtok["abqkv"])
                            p.w(t, "dve", dv0)
                            p.w(t, "act", ac0)
                            for kc in range(2):
                                ps = pqe if kc == 0 else pqo
                                for c in range(NC):
                                    for n in range(2):
                                        p.inc(t.matmul(
                                            ps[:, n * 512:(n + 1) * 512],
                                            wsb[:, (A0 if kind == "q" else B0) + c * wstride + h * 256 + kc * 128:
                                                   (A0 if kind == "q" else B0) + c * wstride + h * 256 + kc * 128 + 128],
                                            lnb[:, c * T + o + n * 512:c * T + o + (n + 1) * 512],
                                            start=(c == 0), stop=(c == NC - 1)), "pe")
                            assert p.c["pe"] == pe0 + 32
                            for kc in range(2):
                                p.w(t, "act", ac0 + kc + 1)
                                for n in range(2):
                                    p.inc(t.matmul(pms[:, n * 512:(n + 1) * 512], onb[:, 1:2],
                                                   biga[:, 8192 + kc * 1024 + n * 512:8192 + kc * 1024 + (n + 1) * 512],
                                                   start=(kc == 0), stop=(kc == 1)), "pe")
                            assert p.c["pe"] == pe0 + 36
                            p.w(t, "dve", dv0 + 4)
                            for n in range(2):
                                p.inc(t.matmul(prb[:, n * 512:(n + 1) * 512], onr[32:33, :],
                                               rsm[32:33, 1024 + n * 512:1024 + (n + 1) * 512],
                                               start=True, stop=True), "pe")
                            assert p.c["pe"] == pe0 + 38

                        @blk.scalar
                        def _(a):
                            p.w(a, "dve", dv0 + 1)
                            p.inc(a.activation(biga[:, 8192:8192 + 1024], scr32[:, :], AF.Square), "act")
                            p.w(a, "dve", dv0 + 2)
                            p.inc(a.activation(biga[:, 9216:9216 + 1024], rsm[:, 0:1024], AF.Square), "act")
                            p.w(a, "dve", dv0 + 3)
                            tok = p.inc(a.activation(rsm[64:65, 1024:2048], rsm[64:65, 1024:2048],
                                                     AF.Sqrt), "act")
                            assert tok == ac0 + 3

                        @blk.vector
                        def _(v):
                            p.w(v, "pe", pe0 + 16)
                            p.inc(v.tensor_copy(scr32[:, :], pqe[:, :]), "dve")
                            p.w(v, "pe", pe0 + 32)
                            p.inc(v.tensor_copy(rsm[:, 0:1024], pqo[:, :]), "dve")
                            p.w(v, "pe", pe0 + 36)
                            tok = p.inc(v.tensor_scalar_add(rsm[64:65, 1024:2048], pms[0:1, :], 1e-6), "dve")
                            assert tok == dv0 + 3
                            p.w(v, "act", ac0 + 3)
                            tok = p.inc(v.reciprocal(rsm[32:33, 1024:2048], rsm[64:65, 1024:2048]), "dve")
                            assert tok == dv0 + 4
                            p.w(v, "pe", pe0 + 38)
                            p.inc(v.tensor_mul(scr32[:, :], scr32[:, :], prb[:, :]), "dve")
                            p.inc(v.tensor_scalar_mul(scr32[:, :], scr32[:, :], gam[:, 0:1]), "dve")
                            p.inc(v.tensor_mul(rsm[:, 0:1024], rsm[:, 0:1024], prb[:, :]), "dve")
                            p.inc(v.tensor_scalar_mul(rsm[:, 0:1024], rsm[:, 0:1024], gam[:, 1:2]), "dve")
                            co = cs[:, o:o + 1024]
                            si = cs[:, T + o:T + o + 1024]
                            tA = biga[:, 8192:8192 + 1024]
                            tB = biga[:, 9216:9216 + 1024]
                            p.inc(v.tensor_mul(tA, scr32[:, :], co), "dve")
                            p.inc(v.tensor_mul(tB, rsm[:, 0:1024], si), "dve")
                            p.inc(v.tensor_sub(dst[:, doff + o:doff + o + 1024], tA, tB), "dve")
                            p.inc(v.tensor_mul(tA, scr32[:, :], si), "dve")
                            p.inc(v.tensor_mul(tB, rsm[:, 0:1024], co), "dve")
                            tok = p.inc(v.tensor_add(dst[:, doff + T + o:doff + T + o + 1024], tA, tB), "dve")
                            assert tok == dv0 + 14

            proj_norm_rope("k", 0, kgs, kvb, 0)

            kv0 = p.mark()
            with p.pt("ps_v0", [128, 256], F32) as pv0, \
                 p.pt("ps_v1", [128, 256], F32) as pv1, \
                 p.blk("ab_v") as blk:
                dve0 = kv0["dve"]
                vev = lambda tt: dve0 + tt + 1
                vtoks = []

                @blk.tensor
                def _(t):
                    p.wsnap(t, kv0)
                    for tt in range(16):
                        ps = pv0 if tt % 2 == 0 else pv1
                        if tt >= 2:
                            p.w(t, "dve", vev(tt - 2))
                        for c in range(NC):
                            p.inc(t.matmul(ps[:, :],
                                           lnb[:, c * T + tt * 128:c * T + tt * 128 + 128],
                                           wsb[:, B0 + 2048 + c * 256:B0 + 2048 + (c + 1) * 256],
                                           start=(c == 0), stop=(c == NC - 1)), "pe")
                        vtoks.append(p.c["pe"])

                @blk.vector
                def _(v):
                    for tt in range(16):
                        ps = pv0 if tt % 2 == 0 else pv1
                        p.w(v, "pe", vtoks[tt])
                        tok = p.inc(v.tensor_copy(kvb[:, 4096 + tt * 256:4096 + (tt + 1) * 256],
                                                  ps[:, :]), "dve")
                        assert tok == vev(tt)
            kvd = p.mark()
            if DEBUG_AB and wtok.get("dbg_seq") == DEBUG_SEQ and not wtok.get("dbg_done"):
                emit_dbg("kv", kvb[:, :])

            with p.blk("ab_wo") as blk:
                @blk.sync
                def _(s):
                    p.w(s, "dmag", wtok["castall"])
                    p.w(s, "dve", kvd["dve"])
                    p.w(s, "pe", kvd["pe"])
                    wtok["abwo"] = p.dma(s, wsb[:, 8192:16384], wo_b)

            for hp in range(2):
                for hl in range(2):
                    proj_norm_rope("q", hp * 2 + hl, qgs, biga, hl * 2 * T)
                if DEBUG_AB and hp == 0 and wtok.get("dbg_seq") == DEBUG_SEQ and not wtok.get("dbg_done"):
                    emit_dbg("qp", biga[:, 0:8192])
                qd = p.mark()

                with p.pt("ps_sc0", [128, 640], F32) as sc0, \
                     p.pt("ps_sc1", [128, 640], F32) as sc1, \
                     p.pt("ps_pt0", [128, 640], BF16) as qt0, \
                     p.pt("ps_pt1", [128, 640], BF16) as qt1, \
                     p.pt("ps_av0", [128, 256], F32) as av0, \
                     p.pt("ps_av1", [128, 256], F32) as av1:
                    iters = [(qb, hl) for qb in range(16) for hl in range(2)]
                    NIT = len(iters)
                    nkb = lambda qb: min(qb, 4) + 1
                    dve0, act0 = qd["dve"], qd["act"]
                    dtk = {k: {} for k in ("mask", "max", "sum", "rcp")}
                    dc = dve0
                    for i, (qb, hl) in enumerate(iters):
                        dc += 1 + (1 if qb >= 4 else 0); dtk["mask"][i] = dc
                        dc += 1; dtk["max"][i] = dc
                        dc += 1; dtk["sum"][i] = dc
                        dc += 1; dtk["rcp"][i] = dc
                    atk = {k: {} for k in ("ptv", "exp", "nrm", "avv")}
                    ac = act0
                    for i in range(NIT):
                        if i >= 1:
                            ac += 1; atk["ptv"][i - 1] = ac
                        ac += 1; atk["exp"][i] = ac
                        ac += 1; atk["nrm"][i] = ac
                        if i >= 1:
                            ac += 2; atk["avv"][i - 1] = ac
                    ac += 1; atk["ptv"][NIT - 1] = ac
                    ac += 2; atk["avv"][NIT - 1] = ac
                    pe_sc, pe_pt, pe_av = {}, {}, {}

                    def emit_pt_av(t, j):
                        qbp, hlp = iters[j]
                        parp = j % 2
                        ppt = qt0 if parp == 0 else qt1
                        pav = av0 if parp == 0 else av1
                        nkp = nkb(qbp)
                        kb0p = qbp - (nkp - 1)
                        p.w(t, "act", atk["nrm"][j])
                        if j >= 2:
                            p.w(t, "act", atk["ptv"][j - 2])
                        for kb in range(nkp):
                            p.inc(t.transpose(ppt[:, kb * 128:(kb + 1) * 128],
                                              pnb[:, kb * 128:(kb + 1) * 128],
                                              idb[:, :]), "pe")
                        pe_pt[j] = p.c["pe"]
                        return (pav, nkp, kb0p, qbp, hlp)

                    def emit_av(t, j, info):
                        pav, nkp, kb0p, qbp, hlp = info
                        parp = j % 2
                        p.w(t, "act", atk["ptv"][j])
                        if j >= 2:
                            p.w(t, "act", atk["avv"][j - 2])
                        for dh in range(2):
                            for kb in range(nkp):
                                p.inc(t.matmul(pav[:, dh * 128:(dh + 1) * 128],
                                               kvb[:, 4096 + (kb0p + kb) * 256 + dh * 128:
                                                      4096 + (kb0p + kb) * 256 + (dh + 1) * 128],
                                               pnb[:, kb * 128:(kb + 1) * 128],
                                               start=(kb == 0), stop=(kb == nkp - 1)), "pe")
                        pe_av[j] = p.c["pe"]

                    with p.blk("ab_at") as blk:
                        @blk.tensor
                        def _(t):
                            p.wsnap(t, qd)
                            for i, (qb, hl) in enumerate(iters):
                                par = i % 2
                                psc = sc0 if par == 0 else sc1
                                nk = nkb(qb)
                                kb0 = qb - (nk - 1)
                                if i >= 1:
                                    info = emit_pt_av(t, i - 1)
                                if i >= 2:
                                    p.w(t, "act", atk["exp"][i - 2])
                                for kc in range(2):
                                    lhs = biga[:, (hl * 2 + kc) * T + qb * 128:(hl * 2 + kc) * T + qb * 128 + 128]
                                    if nk > 1:
                                        p.inc(t.matmul(psc[:, 0:(nk - 1) * 128], lhs,
                                                       kvb[:, kc * T + kb0 * 128:kc * T + (kb0 + nk - 1) * 128],
                                                       start=(kc == 0), stop=(kc == 1 and nk == 5)), "pe")
                                    p.inc(t.matmul(psc[:, (nk - 1) * 128:nk * 128], lhs,
                                                   kvb[:, kc * T + qb * 128:kc * T + qb * 128 + 128],
                                                   start=(kc == 0 and (nk == 5 or nk == 1)),
                                                   stop=(kc == 1)), "pe")
                                pe_sc[i] = p.c["pe"]
                                if i >= 1:
                                    emit_av(t, i - 1, info)
                            info = emit_pt_av(t, NIT - 1)
                            emit_av(t, NIT - 1, info)

                        @blk.vector
                        def _(v):
                            for i, (qb, hl) in enumerate(iters):
                                par = i % 2
                                psc = sc0 if par == 0 else sc1
                                nk = nkb(qb)
                                p.w(v, "pe", pe_sc[i])
                                if qb >= 4:
                                    p.inc(v.tensor_add(psc[:, 0:128], psc[:, 0:128], mskL[:, :]), "dve")
                                tok = p.inc(v.tensor_add(psc[:, (nk - 1) * 128:nk * 128],
                                                         psc[:, (nk - 1) * 128:nk * 128], mskD[:, :]), "dve")
                                assert tok == dtk["mask"][i]
                                if i >= 2:
                                    p.w(v, "act", atk["exp"][i - 2])
                                tok = p.inc(v.reduce_max(avp[:, par * 8:par * 8 + 1], psc[:, 0:nk * 128],
                                                         axis=AX.X, negate=True), "dve")
                                assert tok == dtk["max"][i]
                                p.w(v, "act", atk["exp"][i])
                                tok = p.inc(v.reduce_sum(avp[:, par * 8 + 1:par * 8 + 2],
                                                         rsm[:, par * 1024:par * 1024 + nk * 128],
                                                         axis=AX.X), "dve")
                                assert tok == dtk["sum"][i]
                                p.w(v, "dve", dtk["sum"][i])
                                if i >= 2:
                                    p.w(v, "act", atk["nrm"][i - 2])
                                tok = p.inc(v.reciprocal(avp[:, par * 8 + 2:par * 8 + 3],
                                                         avp[:, par * 8 + 1:par * 8 + 2]), "dve")
                                assert tok == dtk["rcp"][i]

                        @blk.scalar
                        def _(a):
                            for i, (qb, hl) in enumerate(iters):
                                par = i % 2
                                psc = sc0 if par == 0 else sc1
                                nk = nkb(qb)
                                if i >= 1:
                                    j = i - 1
                                    parp = j % 2
                                    ppt = qt0 if parp == 0 else qt1
                                    nkp = nkb(iters[j][0])
                                    p.w(a, "pe", pe_pt[j])
                                    tok = p.inc(a.copy(pnb[:, 0:nkp * 128],
                                                       ppt[:, 0:nkp * 128]), "act")
                                    assert tok == atk["ptv"][j]
                                p.w(a, "dve", dtk["max"][i])
                                tok = p.inc(a.activation(rsm[:, par * 1024:par * 1024 + nk * 128],
                                                         psc[:, 0:nk * 128], AF.Exp,
                                                         bias=avp[:, par * 8:par * 8 + 1]), "act")
                                assert tok == atk["exp"][i]
                                p.w(a, "dve", dtk["rcp"][i])
                                if i >= 1:
                                    p.w(a, "pe", pe_av[i - 1])
                                tok = p.inc(a.mul(pnb[:, 0:nk * 128],
                                                  rsm[:, par * 1024:par * 1024 + nk * 128],
                                                  avp[:, par * 8 + 2:par * 8 + 3]), "act")
                                assert tok == atk["nrm"][i]
                                if i >= 1:
                                    j = i - 1
                                    qbp, hlp = iters[j]
                                    pav = av0 if j % 2 == 0 else av1
                                    p.w(a, "pe", pe_av[j])
                                    d1 = biga[:, 8192 + (hlp * 2) * T + qbp * 128:8192 + (hlp * 2) * T + qbp * 128 + 128]
                                    d2 = biga[:, 8192 + (hlp * 2 + 1) * T + qbp * 128:8192 + (hlp * 2 + 1) * T + qbp * 128 + 128]
                                    p.inc(a.copy(d1, pav[:, 0:128]), "act")
                                    tok = p.inc(a.copy(d2, pav[:, 128:256]), "act")
                                    assert tok == atk["avv"][j]
                            j = NIT - 1
                            parp = j % 2
                            ppt = qt0 if parp == 0 else qt1
                            nkp = nkb(iters[j][0])
                            p.w(a, "pe", pe_pt[j])
                            tok = p.inc(a.copy(pnb[:, 0:nkp * 128],
                                               ppt[:, 0:nkp * 128]), "act")
                            assert tok == atk["ptv"][j]
                            qbp, hlp = iters[j]
                            pav = av0 if j % 2 == 0 else av1
                            p.w(a, "pe", pe_av[j])
                            d1 = biga[:, 8192 + (hlp * 2) * T + qbp * 128:8192 + (hlp * 2) * T + qbp * 128 + 128]
                            d2 = biga[:, 8192 + (hlp * 2 + 1) * T + qbp * 128:8192 + (hlp * 2 + 1) * T + qbp * 128 + 128]
                            p.inc(a.copy(d1, pav[:, 0:128]), "act")
                            tok = p.inc(a.copy(d2, pav[:, 128:256]), "act")
                            assert tok == atk["avv"][j]
                atd = p.mark()
                if DEBUG_AB and hp == 0 and wtok.get("dbg_seq") == DEBUG_SEQ and not wtok.get("dbg_done"):
                    emit_dbg("av", biga[:, 8192:16384])
                    wtok["dbg_done"] = True

                with p.pt("ps_w0", [128, 512], F32) as pw0, \
                     p.pt("ps_w1", [128, 512], F32) as pw1, \
                     p.blk("ab_wo2") as blk:
                    dve0 = atd["dve"]
                    wadd = lambda jj: dve0 + jj + 1
                    wtoks = []

                    @blk.tensor
                    def _(t):
                        p.wsnap(t, atd)
                        p.w(t, "dmas", wtok["abwo"])
                        jj = 0
                        for gq in range(4):
                            for m in range(8):
                                ps = pw0 if jj % 2 == 0 else pw1
                                if jj >= 2:
                                    p.w(t, "dve", wadd(jj - 2))
                                for cl in range(4):
                                    c = hp * 4 + cl
                                    p.inc(t.matmul(ps[:, :],
                                                   wsb[:, 8192 + c * 1024 + m * 128:8192 + c * 1024 + (m + 1) * 128],
                                                   biga[:, 8192 + cl * T + gq * 512:8192 + cl * T + (gq + 1) * 512],
                                                   start=(cl == 0), stop=(cl == 3)), "pe")
                                wtoks.append(p.c["pe"])
                                jj += 1

                    @blk.vector
                    def _(v):
                        jj = 0
                        for gq in range(4):
                            for m in range(8):
                                ps = pw0 if jj % 2 == 0 else pw1
                                p.w(v, "pe", wtoks[jj])
                                tok = p.inc(v.tensor_add(xfm[:, m * T + gq * 512:m * T + (gq + 1) * 512],
                                                         xfm[:, m * T + gq * 512:m * T + (gq + 1) * 512],
                                                         ps[:, :]), "dve")
                                assert tok == wadd(jj)
                                jj += 1
            return p.mark()

        def ln_phase():
            t0, t1 = emit_ln()
            wtok["lnh0"], wtok["lnh1"] = t0, t1

        for s_i in range(SEQS):
            wtok["dbg_seq"] = s_i
            ln_phase()
            if DO_ATTNA:
                emit_attn_a()
            if DO_FFN:
                ln_phase()
                emit_ffn()
            if DO_ATTNB:
                ln_phase()
                emit_attn_b()
            if DO_FFN:
                ln_phase()
                emit_ffn()
            fin = p.mark()
            last_oadd = wtok.get("ffn_oadd_last") if DO_FFN else None
            with p.blk("store") as blk:
                @blk.sync
                def _(s):
                    if last_oadd is not None and fin["dve"] == last_oadd[7]:
                        p.w(s, "pe", fin["pe"])
                        p.w(s, "act", fin["act"])
                        for c in range(NC):
                            p.w(s, "dve", last_oadd[c])
                            p.dma(s, y[s_i, c], xfm[:, c * T:(c + 1) * T])
                    else:
                        p.wsnap(s, fin)
                        for c in range(NC):
                            p.dma(s, y[s_i, c], xfm[:, c * T:(c + 1) * T])
                    if s_i + 1 < SEQS:
                        for c in range(NC):
                            p.dma(s, xfm[:, c * T:(c + 1) * T], xin[s_i + 1, c])
            wtok["x"] = p.c["dmas"]

        with p.blk("fin") as blk:
            final = p.c["dmas"]

            @blk.sync
            def _(s):
                s.wait_ge(p.s["dmas"], final)
    return nc


def _fmaj(wT):
    O = wT.shape[1]
    return np.ascontiguousarray(
        wT.reshape(8, 128, O).transpose(1, 0, 2).reshape(128, 8 * O)).astype(np.float32)


def _consts():
    inv = (np.float32(1.0) / np.float32(1e6) ** (np.arange(0, 256, 2, dtype=np.float32) / np.float32(256))).astype(np.float32)
    t = np.arange(T, dtype=np.float32)
    fr = np.outer(t, inv).astype(np.float32)
    co = np.cos(fr.astype(np.float64)).astype(np.float32).T.copy()
    si = np.sin(fr.astype(np.float64)).astype(np.float32).T.copy()
    cs = np.concatenate([co, si], axis=1)
    qi = np.arange(128)[:, None]
    kj = np.arange(128)[None, :]
    mL = np.where(kj >= qi, 0.0, NEG).astype(np.float32)
    mD = np.where(kj <= qi, 0.0, NEG).astype(np.float32)
    mk = np.concatenate([mL, mD], axis=1)
    idb = np.eye(128, dtype=BFNP)
    onb = np.zeros((128, 2), dtype=BFNP)
    onb[:, 0] = BFNP(1.0 / 1024.0)
    onb[:, 1] = BFNP(1.0 / 256.0)
    onr = np.ones((128, 128), dtype=np.float32)
    return cs, mk, idb, onb, onr


def kernel(L0_attn_a_x, L0_attn_a_W_theta, L0_attn_a_W_phi, L0_attn_a_W_g,
           L0_attn_b_Wq, L0_attn_b_Wk, L0_attn_b_Wv, L0_attn_b_Wo,
           L0_attn_b_q_gamma, L0_attn_b_k_gamma,
           L0_ffn_rail_maker, L0_ffn_rail_up_proj, L0_ffn_w_cproj):
    x = np.asarray(L0_attn_a_x, dtype=np.float32)
    xl = np.ascontiguousarray(x.reshape(32, T, NC, 128).transpose(0, 2, 3, 1))

    wth = _fmaj(np.asarray(L0_attn_a_W_theta, np.float32).T)
    wph = _fmaj(np.asarray(L0_attn_a_W_phi, np.float32).T)
    wg = _fmaj(np.asarray(L0_attn_a_W_g, np.float32).T)

    deint = np.concatenate([np.arange(0, 256, 2), np.arange(1, 256, 2)])
    qcols = np.concatenate([h * 256 + deint for h in range(4)])
    wq = _fmaj(np.asarray(L0_attn_b_Wq, np.float32).T[:, qcols])
    wk = _fmaj(np.asarray(L0_attn_b_Wk, np.float32).T[:, deint])
    wv = _fmaj(np.asarray(L0_attn_b_Wv, np.float32).T)
    wo = _fmaj(np.asarray(L0_attn_b_Wo, np.float32).T)

    maker = np.asarray(L0_ffn_rail_maker, np.float32)
    up = np.asarray(L0_ffn_rail_up_proj, np.float32)
    w1t = np.zeros((8, 128, 512), np.float32)
    w2t = np.zeros((32, 128, 512), np.float32)
    for h in range(4):
        for rr in range(64):
            c, rp = rr // 8, rr % 8
            w1t[c, rp * 16:rp * 16 + 16, rp * 64 + h * 16:rp * 64 + h * 16 + 16] = maker[h, rr]
            jy, rrel = (rr % 8) // 2, rr % 2
            yc = c * 4 + jy
            w2t[yc, rrel * 64 + h * 16:rrel * 64 + h * 16 + 16,
                rrel * 256 + h * 64:rrel * 256 + h * 64 + 64] = up[h, rr].T
    w1 = np.ascontiguousarray(w1t.transpose(1, 0, 2).reshape(128, 4096))
    w2 = np.ascontiguousarray(w2t.transpose(1, 0, 2).reshape(128, 16384))

    wc = np.asarray(L0_ffn_w_cproj, np.float32)
    ZC, PP = np.meshgrid(np.arange(128), np.arange(128), indexing="ij")
    cc_ = ZC // 16
    jy_ = (ZC % 16) // 4
    jz_ = ZC % 4
    v_ = jz_ * 128 + PP
    rrel_ = v_ // 256
    hh_ = (v_ % 256) // 64
    kk_ = v_ % 64
    r_ = cc_ * 8 + 2 * jy_ + rrel_
    zcol = (hh_ * 4096 + r_ * 64 + kk_).reshape(-1)
    wcT = wc[:, zcol].T
    wcd = np.ascontiguousarray(
        wcT.reshape(128, 128, 8, 128).transpose(2, 1, 0, 3).reshape(8, 128, 16384))

    qg = np.asarray(L0_attn_b_q_gamma, np.float32)
    kg = np.asarray(L0_attn_b_k_gamma, np.float32)
    qg2 = np.ascontiguousarray(np.stack([qg[0::2], qg[1::2]], axis=1))
    kg2 = np.ascontiguousarray(np.stack([kg[0::2], kg[1::2]], axis=1))

    csC, mkC, idC, onC, orC = _consts()

    nc = build_nc()
    in_maps = []
    for core in range(NCORE):
        in_maps.append({
            "xin": np.ascontiguousarray(xl[core * SEQS:(core + 1) * SEQS]),
            "wth": wth, "wph": wph, "wg": wg, "wq": wq, "wk": wk, "wv": wv,
            "wo": wo, "w1": w1, "w2": w2, "wc": wcd, "cs": csC, "mk": mkC,
            "idb": idC, "onb": onC, "onr": orC, "qg": qg2, "kg": kg2,
        })
    res = run_bass_kernel_spmd(nc, in_maps, core_ids=list(range(NCORE)))
    out = np.empty((32, T, D), dtype=np.float32)
    for core in range(NCORE):
        yv = np.asarray(res.results[core]["y"])
        out[core * SEQS:(core + 1) * SEQS] = yv.transpose(0, 3, 1, 2).reshape(SEQS, T, D)
    return out


# version 04
# speedup vs baseline: 1.0526x; 1.0044x over previous
import numpy as np
import ml_dtypes
import concourse.bass as bass
import concourse.mybir as mybir
from concourse.bass_utils import run_bass_kernel_spmd

BFNP = ml_dtypes.bfloat16
F32 = mybir.dt.float32
BF16 = mybir.dt.bfloat16
AF = mybir.ActivationFunctionType
OP = mybir.AluOpType
AX = mybir.AxisListType

NCORE = 8
SEQS = 4
T = 2048
D = 1024
NC = 8
TT = 512
NTILE = T // TT
NEG = -1.0e30

DO_ATTNA = True
DO_FFN = True
DO_ATTNB = True
DEBUG_AB = False
DEBUG_SEQ = 0


class B:
    def __init__(self, nc):
        self.nc = nc
        self.s = {}
        self.c = {}

    def blk(self, name):
        self._bn = getattr(self, "_bn", 0) + 1
        return self.nc.Block("%s_%d" % (name, self._bn))

    def pt(self, name, shape, dt):
        self._pn = getattr(self, "_pn", 0) + 1
        return self.nc.psum_tensor("%s_%d" % (name, self._pn), shape, dt)

    def mksem(self, stack, *names):
        for n in names:
            self.s[n] = stack.enter_context(self.nc.semaphore(n))
            self.c[n] = 0

    def inc(self, inst, name, delta=1):
        inst.then_inc(self.s[name], delta)
        self.c[name] += delta
        return self.c[name]

    def dma(self, eng, out, in_, q="dmas"):
        return self.inc(eng.dma_start(out, in_), q, 16)

    def w(self, eng, name, tok):
        if tok > 0:
            eng.wait_ge(self.s[name], tok)

    def mark(self):
        return dict(self.c)

    def wsnap(self, eng, snap):
        for n in ("pe", "dve", "act", "pool"):
            self.w(eng, n, snap[n])


def build_nc():
    nc = bass.Bass("TRN2", target_bir_lowering=False)
    p = B(nc)

    xin = nc.dram_tensor("xin", [SEQS, NC, 128, T], F32, kind="ExternalInput").ap()
    y = nc.dram_tensor("y", [SEQS, NC, 128, T], F32, kind="ExternalOutput").ap()
    wth_i = nc.dram_tensor("wth", [128, 2048], F32, kind="ExternalInput").ap()
    wph_i = nc.dram_tensor("wph", [128, 2048], F32, kind="ExternalInput").ap()
    wg_i = nc.dram_tensor("wg", [128, 8192], F32, kind="ExternalInput").ap()
    wq_i = nc.dram_tensor("wq", [128, 8192], F32, kind="ExternalInput").ap()
    wk_i = nc.dram_tensor("wk", [128, 2048], F32, kind="ExternalInput").ap()
    wv_i = nc.dram_tensor("wv", [128, 2048], F32, kind="ExternalInput").ap()
    wo_i = nc.dram_tensor("wo", [128, 8192], F32, kind="ExternalInput").ap()
    w1_i = nc.dram_tensor("w1", [128, 4096], F32, kind="ExternalInput").ap()
    w2_i = nc.dram_tensor("w2", [128, 16384], F32, kind="ExternalInput").ap()
    wc_i = nc.dram_tensor("wc", [8, 128, 16384], F32, kind="ExternalInput").ap()
    cs_i = nc.dram_tensor("cs", [128, 4096], F32, kind="ExternalInput").ap()
    mk_i = nc.dram_tensor("mk", [128, 256], F32, kind="ExternalInput").ap()
    id_i = nc.dram_tensor("idb", [128, 128], BF16, kind="ExternalInput").ap()
    on_i = nc.dram_tensor("onb", [128, 2], BF16, kind="ExternalInput").ap()
    or_i = nc.dram_tensor("onr", [64, 128], F32, kind="ExternalInput").ap()
    om_i = nc.dram_tensor("onm", [128, 128], BF16, kind="ExternalInput").ap()
    qg_i = nc.dram_tensor("qg", [128, 2], F32, kind="ExternalInput").ap()
    kg_i = nc.dram_tensor("kg", [128, 2], F32, kind="ExternalInput").ap()

    dbg = {}
    if DEBUG_AB:
        for nm, cols in [("ln3", 16384), ("kv", 8192), ("qp", 8192), ("av", 8192),
                         ("pn0", 640), ("p32", 1024), ("avp", 32)]:
            dbg[nm] = nc.dram_tensor("dbg_" + nm, [128, cols],
                                     F32 if nm in ("p32", "avp") else BF16,
                                     kind="ExternalOutput").ap()
    wth_b = nc.dram_tensor("wth_b", [128, 2048], BF16, kind="Internal").ap()
    wph_b = nc.dram_tensor("wph_b", [128, 2048], BF16, kind="Internal").ap()
    wg_b = nc.dram_tensor("wg_b", [128, 8192], BF16, kind="Internal").ap()
    wq_b = nc.dram_tensor("wq_b", [128, 8192], BF16, kind="Internal").ap()
    wk_b = nc.dram_tensor("wk_b", [128, 2048], BF16, kind="Internal").ap()
    wv_b = nc.dram_tensor("wv_b", [128, 2048], BF16, kind="Internal").ap()
    wo_b = nc.dram_tensor("wo_b", [128, 8192], BF16, kind="Internal").ap()
    w1_b = nc.dram_tensor("w1_b", [128, 4096], BF16, kind="Internal").ap()
    w2_b = nc.dram_tensor("w2_b", [128, 16384], BF16, kind="Internal").ap()
    wc_b = nc.dram_tensor("wc_b", [8, 128, 16384], BF16, kind="Internal").ap()

    import contextlib
    with contextlib.ExitStack() as st:
        sbt = lambda name, shape, dt: st.enter_context(nc.sbuf_tensor(name, shape, dt))
        xfm = sbt("xfm", [128, NC * T], F32)
        lnb = sbt("lnb", [128, NC * T], BF16)
        cs = sbt("cs_sb", [128, 2 * T], F32)
        wsb = sbt("wsb", [128, 16384], BF16)
        biga = sbt("biga", [128, 16384], BF16)
        kvb = sbt("kvb", [128, 8192], BF16)
        scr32 = sbt("scr32", [128, 1024], F32)
        rsm = sbt("rsm", [128, 2048], F32)
        pnb = sbt("pnb", [128, 640], BF16)
        avp = sbt("avp", [128, 32], F32)
        mskL = sbt("mskL", [128, 128], F32)
        mskD = sbt("mskD", [128, 128], F32)
        idb = sbt("idb_sb", [128, 128], BF16)
        onb = sbt("onb_sb", [128, 2], BF16)
        onr = sbt("onr_sb", [64, 128], F32)
        onm = sbt("onm_sb", [128, 128], BF16)
        qgs = sbt("qgs", [128, 2], F32)
        kgs = sbt("kgs", [128, 2], F32)

        p.mksem(st, "pe", "dve", "act", "pool", "dmas", "dmag")
        wtok = {}

        with p.blk("startup") as blk:
            @blk.gpsimd
            def _(g):
                wtok["wth"] = p.dma(g, wth_b, wth_i, "dmag")
                wtok["wph"] = p.dma(g, wph_b, wph_i, "dmag")
                wtok["wg"] = p.dma(g, wg_b, wg_i, "dmag")
                wtok["w1"] = p.dma(g, w1_b, w1_i, "dmag")
                wtok["w2"] = p.dma(g, w2_b, w2_i, "dmag")
                wtok["wq"] = p.dma(g, wq_b, wq_i, "dmag")
                wtok["wk"] = p.dma(g, wk_b, wk_i, "dmag")
                wtok["wv"] = p.dma(g, wv_b, wv_i, "dmag")
                wtok["wo"] = p.dma(g, wo_b, wo_i, "dmag")
                for m in range(8):
                    wtok["wc%d" % m] = p.dma(g, wc_b[m], wc_i[m], "dmag")
                wtok["castall"] = p.c["dmag"]

            @blk.sync
            def _(s):
                p.dma(s, cs[:, :], cs_i)
                p.dma(s, mskL[:, :], mk_i[:, 0:128])
                p.dma(s, mskD[:, :], mk_i[:, 128:256])
                p.dma(s, idb[:, :], id_i)
                p.dma(s, onb[:, :], on_i)
                p.dma(s, onr[:, :], or_i)
                p.dma(s, onm[:, :], om_i)
                p.dma(s, qgs[:, :], qg_i)
                p.dma(s, kgs[:, :], kg_i)
                for c in range(NC):
                    p.dma(s, xfm[:, c * T:(c + 1) * T], xin[0, c])
                wtok["x"] = p.c["dmas"]

            @blk.vector
            def _(v):
                p.w(v, "dmas", wtok["x"])
                p.inc(v.tensor_scalar_mul(qgs[:, :], qgs[:, :], 1.0 / 16.0), "dve")

        def emit_ln():
            snap = p.mark()
            pe0, dv0, ac0 = snap["pe"], snap["dve"], snap["act"]
            tok_h0, tok_h1 = dv0 + 36, dv0 + 52
            with p.pt("ps_m0", [128, 1024], F32) as pm0, \
                 p.pt("ps_e0", [128, 1024], F32) as pe_0, \
                 p.pt("ps_m1", [128, 1024], F32) as pm1, \
                 p.pt("ps_e1", [128, 1024], F32) as pe_1, \
                 p.blk("ln") as blk:

                @blk.vector
                def _(v):
                    p.w(v, "dmas", wtok["x"])
                    p.w(v, "pe", pe0)
                    p.w(v, "act", ac0)
                    for hf in range(2):
                        o = hf * 1024
                        for c in range(NC):
                            if c >= 4:
                                p.w(v, "pe", pe0 + hf * 32 + 2 * (c - 3))
                            slot = hf * 8192 + (c % 4) * 1024
                            tok = p.inc(v.tensor_copy(biga[:, slot:slot + 1024],
                                                      xfm[:, c * T + o:c * T + o + 1024]), "dve")
                            assert tok == dv0 + hf * 8 + c + 1
                    for hf in range(2):
                        o = hf * 1024
                        pm = pm0 if hf == 0 else pm1
                        pex = pe_0 if hf == 0 else pe_1
                        buf = scr32[:, 0:1024] if hf == 0 else rsm[:, 0:1024]
                        p.w(v, "act", ac0 + 16 + hf + 1)
                        tok = p.inc(v.scalar_tensor_tensor(buf, pex[:, :], 1e-5, buf,
                                                           OP.add, OP.subtract), "dve")
                        assert tok == dv0 + 16 + hf * 2 + 1
                        p.w(v, "act", ac0 + 18 + hf + 1)
                        tok = p.inc(v.reciprocal(pex[:, :], buf), "dve")
                        assert tok == dv0 + 16 + hf * 2 + 2
                        p.w(v, "dve", tok)
                    for hf in range(2):
                        o = hf * 1024
                        pm = pm0 if hf == 0 else pm1
                        pex = pe_0 if hf == 0 else pe_1
                        buf = scr32[:, 0:1024] if hf == 0 else rsm[:, 0:1024]
                        for c in range(NC):
                            p.inc(v.tensor_sub(buf, xfm[:, c * T + o:c * T + o + 1024],
                                               pm[:, :]), "dve")
                            p.inc(v.tensor_mul(lnb[:, c * T + o:c * T + o + 1024],
                                               buf, pex[:, :]), "dve")
                        assert p.c["dve"] == (tok_h0 if hf == 0 else tok_h1)

                @blk.scalar
                def _(a):
                    p.w(a, "dmas", wtok["x"])
                    p.w(a, "pe", pe0)
                    p.w(a, "dve", dv0)
                    for hf in range(2):
                        o = hf * 1024
                        for c in range(NC):
                            if c >= 4:
                                p.w(a, "pe", pe0 + hf * 32 + 16 + 2 * (c - 3))
                            slot = hf * 8192 + 4096 + (c % 4) * 1024
                            tok = p.inc(a.activation(biga[:, slot:slot + 1024],
                                                     xfm[:, c * T + o:c * T + o + 1024],
                                                     AF.Square), "act")
                            assert tok == ac0 + hf * 8 + c + 1
                    for hf in range(2):
                        pm = pm0 if hf == 0 else pm1
                        buf = scr32[:, 0:1024] if hf == 0 else rsm[:, 0:1024]
                        p.w(a, "pe", pe0 + 32 * (hf + 1))
                        tok = p.inc(a.activation(buf, pm[:, :], AF.Square), "act")
                        assert tok == ac0 + 16 + hf + 1
                    for hf in range(2):
                        buf = scr32[:, 0:1024] if hf == 0 else rsm[:, 0:1024]
                        p.w(a, "dve", dv0 + 16 + hf * 2 + 1)
                        tok = p.inc(a.activation(buf, buf, AF.Sqrt), "act")
                        assert tok == ac0 + 18 + hf + 1

                @blk.tensor
                def _(t):
                    for hf in range(2):
                        pm = pm0 if hf == 0 else pm1
                        pex = pe_0 if hf == 0 else pe_1
                        for c in range(NC):
                            p.w(t, "dve", dv0 + hf * 8 + c + 1)
                            slot = hf * 8192 + (c % 4) * 1024
                            for n in range(2):
                                p.inc(t.matmul(pm[:, n * 512:(n + 1) * 512], onm[:, :],
                                               biga[:, slot + n * 512:slot + (n + 1) * 512],
                                               start=(c == 0), stop=(c == NC - 1)), "pe")
                        for c in range(NC):
                            p.w(t, "act", ac0 + hf * 8 + c + 1)
                            slot = hf * 8192 + 4096 + (c % 4) * 1024
                            for n in range(2):
                                p.inc(t.matmul(pex[:, n * 512:(n + 1) * 512], onm[:, :],
                                               biga[:, slot + n * 512:slot + (n + 1) * 512],
                                               start=(c == 0), stop=(c == NC - 1)), "pe")
                        assert p.c["pe"] == pe0 + 32 * (hf + 1)

            wtok["ln_ct"] = {(hf, c): dv0 + 20 + hf * 16 + 2 * (c + 1)
                             for hf in range(2) for c in range(NC)}
            return tok_h0, tok_h1

        def emit_attn_a():
            snap = p.mark()
            A0, B0 = 0, 8192
            TH0, S0, PH0, G0 = 0, 4096, 6144, 6656
            with p.blk("aa_w") as blk:
                @blk.sync
                def _(s):
                    p.w(s, "dmag", wtok["castall"])
                    p.w(s, "pe", wtok["preln_pe"])
                    p.dma(s, wsb[:, B0:B0 + 2048], wth_b)
                    p.dma(s, wsb[:, B0 + 2048:B0 + 4096], wph_b)
                    p.w(s, "dmag", wtok["castall"])
                    wtok["aa"] = p.dma(s, wsb[:, A0:A0 + 8192], wg_b)

            with p.pt("ps_th0", [128, T], F32) as pt0, \
                 p.pt("ps_th1", [128, T], F32) as pt1, \
                 p.blk("aa_th") as blk:
                ttoks = [0, 0]

                @blk.tensor
                def _(t):
                    p.w(t, "act", snap["act"])
                    p.w(t, "dve", wtok["ln_ct"][(0, 0)])
                    p.w(t, "dmas", wtok["aa"])
                    for l in range(2):
                        ps = pt0 if l == 0 else pt1
                        for c in range(NC):
                            for n in range(4):
                                p.w(t, "dve", wtok["ln_ct"][(n // 2, c)])
                                p.inc(t.matmul(ps[:, n * 512:(n + 1) * 512],
                                               wsb[:, B0 + c * 256 + l * 128:B0 + c * 256 + l * 128 + 128],
                                               lnb[:, c * T + n * 512:c * T + (n + 1) * 512],
                                               start=(c == 0), stop=(c == NC - 1)), "pe")
                        ttoks[l] = p.c["pe"]

                @blk.scalar
                def _(a):
                    for l in range(2):
                        p.w(a, "pe", ttoks[l])
                        ps = pt0 if l == 0 else pt1
                        p.inc(a.copy(biga[:, TH0 + l * 2048:TH0 + (l + 1) * 2048], ps[:, :]), "act")
            th = p.mark()

            with p.pt("ps_ph", [128, 256], F32) as pph, \
                 p.pt("ps_g", [128, 1024], F32) as pg, \
                 p.pt("ps_s0", [128, 1024], F32) as ps0, \
                 p.pt("ps_s1", [128, 1024], F32) as ps1, \
                 p.blk("aa_s") as blk:
                dve0 = th["dve"]
                ev = lambda tt: dve0 + 2 * (tt + 1)
                pg_tok, s_tok = [], []

                @blk.tensor
                def _(t):
                    p.wsnap(t, th)
                    for tt in range(16):
                        if tt >= 1:
                            p.w(t, "dve", ev(tt - 1))
                        for c in range(NC):
                            p.inc(t.matmul(pph[:, :],
                                           lnb[:, c * T + tt * 128:c * T + tt * 128 + 128],
                                           wsb[:, B0 + 2048 + c * 256:B0 + 2048 + (c + 1) * 256],
                                           start=(c == 0), stop=(c == NC - 1)), "pe")
                        for c in range(NC):
                            for n in range(2):
                                p.inc(t.matmul(pg[:, n * 512:(n + 1) * 512],
                                               lnb[:, c * T + tt * 128:c * T + tt * 128 + 128],
                                               wsb[:, A0 + c * 1024 + n * 512:A0 + c * 1024 + (n + 1) * 512],
                                               start=(c == 0), stop=(c == NC - 1)), "pe")
                        pg_tok.append(p.c["pe"])
                        if tt >= 1:
                            j = tt - 1
                            par = j % 2
                            for l in range(2):
                                ps = ps0 if l == 0 else ps1
                                for n in range(2):
                                    p.inc(t.matmul(ps[:, n * 512:(n + 1) * 512],
                                                   biga[:, PH0 + par * 256 + l * 128:PH0 + par * 256 + l * 128 + 128],
                                                   biga[:, G0 + par * 1024 + n * 512:G0 + par * 1024 + (n + 1) * 512],
                                                   start=(j == 0), stop=(j == 15)), "pe")
                            s_tok.append(p.c["pe"])
                    p.w(t, "dve", ev(15))
                    j, par = 15, 15 % 2
                    for l in range(2):
                        ps = ps0 if l == 0 else ps1
                        for n in range(2):
                            p.inc(t.matmul(ps[:, n * 512:(n + 1) * 512],
                                           biga[:, PH0 + par * 256 + l * 128:PH0 + par * 256 + l * 128 + 128],
                                           biga[:, G0 + par * 1024 + n * 512:G0 + par * 1024 + (n + 1) * 512],
                                           start=False, stop=True), "pe")
                    s_tok.append(p.c["pe"])

                @blk.vector
                def _(v):
                    for tt in range(16):
                        par = tt % 2
                        p.w(v, "pe", pg_tok[tt])
                        if tt >= 2:
                            p.w(v, "pe", s_tok[tt - 2])
                        p.inc(v.tensor_copy(biga[:, PH0 + par * 256:PH0 + par * 256 + 256], pph[:, :]), "dve")
                        tok = p.inc(v.tensor_copy(biga[:, G0 + par * 1024:G0 + par * 1024 + 1024], pg[:, :]), "dve")
                        assert tok == ev(tt)
                    p.w(v, "pe", s_tok[15])
                    for l in range(2):
                        ps = ps0 if l == 0 else ps1
                        p.inc(v.tensor_scalar_mul(biga[:, S0 + l * 1024:S0 + (l + 1) * 1024],
                                                  ps[:, :], 1.0 / 32.0), "dve")
            sev = p.mark()

            with p.pt("ps_o0", [128, T], F32) as po0, \
                 p.pt("ps_o1", [128, T], F32) as po1, \
                 p.blk("aa_o") as blk:
                dve0 = sev["dve"]
                addt = lambda m: dve0 + m + 1
                mtoks = []

                @blk.tensor
                def _(t):
                    p.wsnap(t, sev)
                    for m in range(NC):
                        ps = po0 if m % 2 == 0 else po1
                        if m >= 2:
                            p.w(t, "dve", addt(m - 2))
                        for l in range(2):
                            for n in range(4):
                                p.inc(t.matmul(ps[:, n * 512:(n + 1) * 512],
                                               biga[:, S0 + l * 1024 + m * 128:S0 + l * 1024 + m * 128 + 128],
                                               biga[:, TH0 + l * 2048 + n * 512:TH0 + l * 2048 + (n + 1) * 512],
                                               start=(l == 0), stop=(l == 1)), "pe")
                        mtoks.append(p.c["pe"])

                @blk.vector
                def _(v):
                    for m in range(NC):
                        ps = po0 if m % 2 == 0 else po1
                        p.w(v, "pe", mtoks[m])
                        tok = p.inc(v.tensor_add(xfm[:, m * T:(m + 1) * T],
                                                 xfm[:, m * T:(m + 1) * T], ps[:, :]), "dve")
                        assert tok == addt(m)
            return p.mark()

        def emit_ffn():
            snap = p.mark()
            A0, WC0, B0, WC1 = 0, 4096, 8192, 12288
            with p.blk("ff_w") as blk:
                @blk.sync
                def _(s):
                    p.w(s, "dmag", wtok["castall"])
                    p.w(s, "pe", wtok["preln_pe"])
                    wtok["ffw1"] = p.dma(s, wsb[:, B0:B0 + 4096], w1_b)

            m0 = p.mark()
            pe0, dve0, act0, dm0, dg0 = m0["pe"], m0["dve"], m0["act"], m0["dmas"], m0["dmag"]
            pe_ymm = lambda tg, yci: pe0 + tg * 296 + yci * 5 + 1
            pe_rail = lambda tg, yci: pe0 + tg * 296 + yci * 5 + 5
            pe_cp = lambda tg, m: pe0 + tg * 296 + 40 + (m + 1) * 32
            yev = lambda tg, yci: dve0 + tg * 16 + yci + 1
            oadd = lambda tg, m: dve0 + tg * 16 + 8 + m + 1
            gel = lambda tg, zi: act0 + tg * 32 + zi + 1
            dm_w2 = lambda g: dm0 + (g * 17 + 1) * 16
            dm_wce = lambda tg, m: dm0 + ((tg // 4) * 17 + 1 + (tg % 4) * 4 + m // 2 + 1) * 16
            dm_wco = lambda tg, m: dg0 + (tg * 4 + (m - 1) // 2 + 1) * 16

            with p.pt("ps_y0", [128, TT], F32) as py0, \
                 p.pt("ps_y1", [128, TT], F32) as py1, \
                 p.pt("ps_z0", [128, TT], F32) as pz0, \
                 p.pt("ps_z1", [128, TT], F32) as pz1, \
                 p.pt("ps_oA", [128, TT], F32) as poA, \
                 p.pt("ps_oB", [128, TT], F32) as poB:
                for g in range(4):
                    for tile in range(NTILE):
                        tg = g * 4 + tile
                        with p.blk("ff") as blk:
                            @blk.sync
                            def _(s):
                                if tile == 0:
                                    if g == 0:
                                        p.w(s, "pe", wtok["preln_pe"])
                                    else:
                                        p.w(s, "pe", pe_rail(g * 4 - 1, 7))
                                    tok = p.dma(s, wsb[:, A0:A0 + 4096],
                                                w2_b[:, g * 4096:(g + 1) * 4096])
                                    assert tok == dm_w2(g)
                                for m in range(0, 8, 2):
                                    je = tg * 4 + m // 2
                                    if je == 0:
                                        p.w(s, "pe", wtok["preln_pe"])
                                    else:
                                        jp = je - 1
                                        p.w(s, "pe", pe_cp(jp // 4, 2 * (jp % 4)))
                                    tok = p.dma(s, wsb[:, WC0:WC0 + 4096],
                                                wc_b[m][:, g * 4096:(g + 1) * 4096])
                                    assert tok == dm_wce(tg, m)

                            @blk.gpsimd
                            def _(gq):
                                for m in range(1, 8, 2):
                                    jo = tg * 4 + (m - 1) // 2
                                    if jo == 0:
                                        p.w(gq, "pe", wtok["preln_pe"])
                                    else:
                                        jp = jo - 1
                                        p.w(gq, "pe", pe_cp(jp // 4, 2 * (jp % 4) + 1))
                                    tok = p.dma(gq, wsb[:, WC1:WC1 + 4096],
                                                wc_b[m][:, g * 4096:(g + 1) * 4096], "dmag")
                                    assert tok == dm_wco(tg, m)

                            @blk.tensor
                            def _(t):
                                if tg == 0:
                                    p.w(t, "dmas", wtok["ffw1"])
                                    p.w(t, "act", snap["act"])
                                if tile == 0:
                                    p.w(t, "dmas", dm_w2(g))
                                for yci in range(8):
                                    yc = g * 8 + yci
                                    c, jy = yc // 4, yc % 4
                                    p.w(t, "dve", wtok["ln_ct"][(tile // 2, c)])
                                    ypar = yci % 2
                                    psy = py0 if ypar == 0 else py1
                                    gi = tg * 8 + yci
                                    if gi >= 2:
                                        p.w(t, "dve", yev((gi - 2) // 8, (gi - 2) % 8))
                                    tok = p.inc(t.matmul(psy[:, :],
                                                         wsb[:, B0 + c * 512 + jy * 128:B0 + c * 512 + (jy + 1) * 128],
                                                         lnb[:, c * T + tile * TT:c * T + (tile + 1) * TT],
                                                         start=True, stop=True), "pe")
                                    assert tok == pe_ymm(tg, yci)
                                    p.w(t, "dve", yev(tg, yci))
                                    for jj in range(4):
                                        zi = yci * 4 + jj
                                        psz = pz0 if zi % 2 == 0 else pz1
                                        zz = tg * 32 + zi
                                        if zz >= 2:
                                            p.w(t, "act", gel((zz - 2) // 32, (zz - 2) % 32))
                                        p.inc(t.matmul(psz[:, :],
                                                       wsb[:, A0 + yci * 512 + jj * 128:A0 + yci * 512 + (jj + 1) * 128],
                                                       kvb[:, ypar * 512:(ypar + 1) * 512],
                                                       start=True, stop=True), "pe")
                                    assert p.c["pe"] == pe_rail(tg, yci)
                                for m in range(8):
                                    pso = poA if m % 2 == 0 else poB
                                    if tg == 0 and m == 0:
                                        p.w(t, "dve", wtok["lnh1"])
                                    if m >= 2:
                                        p.w(t, "dve", oadd(tg, m - 2))
                                    elif tg >= 1:
                                        p.w(t, "dve", oadd(tg - 1, m + 6))
                                    if m % 2 == 0:
                                        p.w(t, "dmas", dm_wce(tg, m))
                                    else:
                                        p.w(t, "dmag", dm_wco(tg, m))
                                    wcb = WC0 if m % 2 == 0 else WC1
                                    for zi in range(32):
                                        if m == 0:
                                            p.w(t, "act", gel(tg, zi))
                                        p.inc(t.matmul(pso[:, :],
                                                       wsb[:, wcb + zi * 128:wcb + (zi + 1) * 128],
                                                       biga[:, zi * 512:(zi + 1) * 512],
                                                       start=(zi == 0), stop=(zi == 31)), "pe")
                                    assert p.c["pe"] == pe_cp(tg, m)

                            @blk.vector
                            def _(v):
                                for yci in range(8):
                                    ypar = yci % 2
                                    psy = py0 if ypar == 0 else py1
                                    p.w(v, "pe", pe_ymm(tg, yci))
                                    tok = p.inc(v.tensor_copy(kvb[:, ypar * 512:(ypar + 1) * 512],
                                                              psy[:, :]), "dve")
                                    assert tok == yev(tg, yci)
                                for m in range(8):
                                    pso = poA if m % 2 == 0 else poB
                                    p.w(v, "pe", pe_cp(tg, m))
                                    tok = p.inc(v.tensor_add(
                                        xfm[:, m * T + tile * TT:m * T + (tile + 1) * TT],
                                        xfm[:, m * T + tile * TT:m * T + (tile + 1) * TT],
                                        pso[:, :]), "dve")
                                    assert tok == oadd(tg, m)

                            @blk.scalar
                            def _(a):
                                for zi in range(32):
                                    psz = pz0 if zi % 2 == 0 else pz1
                                    yci = zi // 4
                                    p.w(a, "pe", pe_rail(tg, yci) - (3 - zi % 4))
                                    tok = p.inc(a.activation(biga[:, zi * 512:(zi + 1) * 512],
                                                             psz[:, :], AF.Gelu), "act")
                                    assert tok == gel(tg, zi)
            wtok["ffn_oadd_last"] = {m: oadd(15, m) for m in range(8)}
            return p.mark()

        def emit_dbg(nm, src_ap):
            m0 = p.mark()
            with p.blk("dbg") as blk:
                @blk.sync
                def _(s):
                    p.wsnap(s, m0)
                    p.dma(s, dbg[nm], src_ap)

        def emit_attn_b():
            if DEBUG_AB and wtok.get("dbg_seq") == DEBUG_SEQ and not wtok.get("dbg_done"):
                emit_dbg("ln3", lnb[:, :])
            snap = p.mark()
            A0, B0 = 0, 8192
            with p.blk("ab_w") as blk:
                @blk.sync
                def _(s):
                    p.w(s, "dmag", wtok["castall"])
                    p.w(s, "pe", wtok["preln_pe"])
                    p.dma(s, wsb[:, A0:A0 + 8192], wq_b)
                    p.dma(s, wsb[:, B0:B0 + 2048], wk_b)
                    wtok["abqkv"] = p.dma(s, wsb[:, B0 + 2048:B0 + 4096], wv_b)

            def proj_norm_rope(kind, h, gam, dst, doff):
                for hf in range(2):
                    o = hf * 1024
                    sn = p.mark()
                    pe0, dv0, ac0 = sn["pe"], sn["dve"], sn["act"]
                    with p.pt("pp_e", [128, 1024], F32) as pqe, \
                         p.pt("pp_o", [128, 1024], F32) as pqo, \
                         p.pt("pp_ms", [1, 1024], F32) as pms, \
                         p.pt("pp_rb", [128, 1024], F32) as prb, \
                         p.blk("pnr") as blk:
                        wstride = 1024 if kind == "q" else 256

                        @blk.tensor
                        def _(t):
                            p.wsnap(t, snap)
                            p.w(t, "dmas", wtok["abqkv"])
                            p.w(t, "dve", dv0)
                            p.w(t, "act", ac0)
                            for kc in range(2):
                                ps = pqe if kc == 0 else pqo
                                for c in range(NC):
                                    for n in range(2):
                                        p.inc(t.matmul(
                                            ps[:, n * 512:(n + 1) * 512],
                                            wsb[:, (A0 if kind == "q" else B0) + c * wstride + h * 256 + kc * 128:
                                                   (A0 if kind == "q" else B0) + c * wstride + h * 256 + kc * 128 + 128],
                                            lnb[:, c * T + o + n * 512:c * T + o + (n + 1) * 512],
                                            start=(c == 0), stop=(c == NC - 1)), "pe")
                            assert p.c["pe"] == pe0 + 32
                            for kc in range(2):
                                p.w(t, "act", ac0 + kc + 1)
                                for n in range(2):
                                    p.inc(t.matmul(pms[:, n * 512:(n + 1) * 512], onb[:, 1:2],
                                                   biga[:, 8192 + kc * 1024 + n * 512:8192 + kc * 1024 + (n + 1) * 512],
                                                   start=(kc == 0), stop=(kc == 1)), "pe")
                            assert p.c["pe"] == pe0 + 36
                            p.w(t, "dve", dv0 + 4)
                            for n in range(2):
                                p.inc(t.matmul(prb[:, n * 512:(n + 1) * 512], onr[32:33, :],
                                               rsm[32:33, 1024 + n * 512:1024 + (n + 1) * 512],
                                               start=True, stop=True), "pe")
                            assert p.c["pe"] == pe0 + 38

                        @blk.scalar
                        def _(a):
                            p.w(a, "dve", dv0 + 1)
                            p.inc(a.activation(biga[:, 8192:8192 + 1024], scr32[:, :], AF.Square), "act")
                            p.w(a, "dve", dv0 + 2)
                            p.inc(a.activation(biga[:, 9216:9216 + 1024], rsm[:, 0:1024], AF.Square), "act")
                            p.w(a, "dve", dv0 + 3)
                            tok = p.inc(a.activation(rsm[64:65, 1024:2048], rsm[64:65, 1024:2048],
                                                     AF.Sqrt), "act")
                            assert tok == ac0 + 3

                        @blk.vector
                        def _(v):
                            p.w(v, "pe", pe0 + 16)
                            p.inc(v.tensor_copy(scr32[:, :], pqe[:, :]), "dve")
                            p.w(v, "pe", pe0 + 32)
                            p.inc(v.tensor_copy(rsm[:, 0:1024], pqo[:, :]), "dve")
                            p.w(v, "pe", pe0 + 36)
                            tok = p.inc(v.tensor_scalar_add(rsm[64:65, 1024:2048], pms[0:1, :], 1e-6), "dve")
                            assert tok == dv0 + 3
                            p.w(v, "act", ac0 + 3)
                            tok = p.inc(v.reciprocal(rsm[32:33, 1024:2048], rsm[64:65, 1024:2048]), "dve")
                            assert tok == dv0 + 4
                            p.w(v, "pe", pe0 + 38)
                            p.inc(v.tensor_mul(scr32[:, :], scr32[:, :], prb[:, :]), "dve")
                            p.inc(v.tensor_scalar_mul(scr32[:, :], scr32[:, :], gam[:, 0:1]), "dve")
                            p.inc(v.tensor_mul(rsm[:, 0:1024], rsm[:, 0:1024], prb[:, :]), "dve")
                            p.inc(v.tensor_scalar_mul(rsm[:, 0:1024], rsm[:, 0:1024], gam[:, 1:2]), "dve")
                            co = cs[:, o:o + 1024]
                            si = cs[:, T + o:T + o + 1024]
                            tA = biga[:, 8192:8192 + 1024]
                            tB = biga[:, 9216:9216 + 1024]
                            p.inc(v.tensor_mul(tA, scr32[:, :], co), "dve")
                            p.inc(v.tensor_mul(tB, rsm[:, 0:1024], si), "dve")
                            p.inc(v.tensor_sub(dst[:, doff + o:doff + o + 1024], tA, tB), "dve")
                            p.inc(v.tensor_mul(tA, scr32[:, :], si), "dve")
                            p.inc(v.tensor_mul(tB, rsm[:, 0:1024], co), "dve")
                            tok = p.inc(v.tensor_add(dst[:, doff + T + o:doff + T + o + 1024], tA, tB), "dve")
                            assert tok == dv0 + 14

            proj_norm_rope("k", 0, kgs, kvb, 0)

            kv0 = p.mark()
            with p.pt("ps_v0", [128, 256], F32) as pv0, \
                 p.pt("ps_v1", [128, 256], F32) as pv1, \
                 p.blk("ab_v") as blk:
                dve0 = kv0["dve"]
                vev = lambda tt: dve0 + tt + 1
                vtoks = []

                @blk.tensor
                def _(t):
                    p.wsnap(t, kv0)
                    for tt in range(16):
                        ps = pv0 if tt % 2 == 0 else pv1
                        if tt >= 2:
                            p.w(t, "dve", vev(tt - 2))
                        for c in range(NC):
                            p.inc(t.matmul(ps[:, :],
                                           lnb[:, c * T + tt * 128:c * T + tt * 128 + 128],
                                           wsb[:, B0 + 2048 + c * 256:B0 + 2048 + (c + 1) * 256],
                                           start=(c == 0), stop=(c == NC - 1)), "pe")
                        vtoks.append(p.c["pe"])

                @blk.vector
                def _(v):
                    for tt in range(16):
                        ps = pv0 if tt % 2 == 0 else pv1
                        p.w(v, "pe", vtoks[tt])
                        tok = p.inc(v.tensor_copy(kvb[:, 4096 + tt * 256:4096 + (tt + 1) * 256],
                                                  ps[:, :]), "dve")
                        assert tok == vev(tt)
            kvd = p.mark()
            if DEBUG_AB and wtok.get("dbg_seq") == DEBUG_SEQ and not wtok.get("dbg_done"):
                emit_dbg("kv", kvb[:, :])

            with p.blk("ab_wo") as blk:
                @blk.sync
                def _(s):
                    p.w(s, "dmag", wtok["castall"])
                    p.w(s, "dve", kvd["dve"])
                    p.w(s, "pe", kvd["pe"])
                    wtok["abwo"] = p.dma(s, wsb[:, 8192:16384], wo_b)

            for hp in range(2):
                for hl in range(2):
                    proj_norm_rope("q", hp * 2 + hl, qgs, biga, hl * 2 * T)
                if DEBUG_AB and hp == 0 and wtok.get("dbg_seq") == DEBUG_SEQ and not wtok.get("dbg_done"):
                    emit_dbg("qp", biga[:, 0:8192])
                qd = p.mark()

                with p.pt("ps_sc0", [128, 640], F32) as sc0, \
                     p.pt("ps_sc1", [128, 640], F32) as sc1, \
                     p.pt("ps_pt0", [128, 640], BF16) as qt0, \
                     p.pt("ps_pt1", [128, 640], BF16) as qt1, \
                     p.pt("ps_av0", [128, 256], F32) as av0, \
                     p.pt("ps_av1", [128, 256], F32) as av1:
                    iters = [(qb, hl) for qb in range(16) for hl in range(2)]
                    NIT = len(iters)
                    nkb = lambda qb: min(qb, 4) + 1
                    dve0, act0 = qd["dve"], qd["act"]
                    dtk = {k: {} for k in ("mask", "max", "sum", "rcp")}
                    dc = dve0
                    for i, (qb, hl) in enumerate(iters):
                        dc += 1 + (1 if qb >= 4 else 0); dtk["mask"][i] = dc
                        dc += 1; dtk["max"][i] = dc
                        dc += 1; dtk["sum"][i] = dc
                        dc += 1; dtk["rcp"][i] = dc
                    atk = {k: {} for k in ("ptv", "exp", "nrm", "avv")}
                    ac = act0
                    for i in range(NIT):
                        if i >= 1:
                            ac += 1; atk["ptv"][i - 1] = ac
                        ac += 1; atk["exp"][i] = ac
                        ac += 1; atk["nrm"][i] = ac
                        if i >= 1:
                            ac += 2; atk["avv"][i - 1] = ac
                    ac += 1; atk["ptv"][NIT - 1] = ac
                    ac += 2; atk["avv"][NIT - 1] = ac
                    pe_sc, pe_pt, pe_av = {}, {}, {}

                    def emit_pt_av(t, j):
                        qbp, hlp = iters[j]
                        parp = j % 2
                        ppt = qt0 if parp == 0 else qt1
                        pav = av0 if parp == 0 else av1
                        nkp = nkb(qbp)
                        kb0p = qbp - (nkp - 1)
                        p.w(t, "act", atk["nrm"][j])
                        if j >= 2:
                            p.w(t, "act", atk["ptv"][j - 2])
                        for kb in range(nkp):
                            p.inc(t.transpose(ppt[:, kb * 128:(kb + 1) * 128],
                                              pnb[:, kb * 128:(kb + 1) * 128],
                                              idb[:, :]), "pe")
                        pe_pt[j] = p.c["pe"]
                        return (pav, nkp, kb0p, qbp, hlp)

                    def emit_av(t, j, info):
                        pav, nkp, kb0p, qbp, hlp = info
                        parp = j % 2
                        p.w(t, "act", atk["ptv"][j])
                        if j >= 2:
                            p.w(t, "act", atk["avv"][j - 2])
                        for dh in range(2):
                            for kb in range(nkp):
                                p.inc(t.matmul(pav[:, dh * 128:(dh + 1) * 128],
                                               kvb[:, 4096 + (kb0p + kb) * 256 + dh * 128:
                                                      4096 + (kb0p + kb) * 256 + (dh + 1) * 128],
                                               pnb[:, kb * 128:(kb + 1) * 128],
                                               start=(kb == 0), stop=(kb == nkp - 1)), "pe")
                        pe_av[j] = p.c["pe"]

                    with p.blk("ab_at") as blk:
                        @blk.tensor
                        def _(t):
                            p.wsnap(t, qd)
                            for i, (qb, hl) in enumerate(iters):
                                par = i % 2
                                psc = sc0 if par == 0 else sc1
                                nk = nkb(qb)
                                kb0 = qb - (nk - 1)
                                if i >= 1:
                                    info = emit_pt_av(t, i - 1)
                                if i >= 2:
                                    p.w(t, "act", atk["exp"][i - 2])
                                for kc in range(2):
                                    lhs = biga[:, (hl * 2 + kc) * T + qb * 128:(hl * 2 + kc) * T + qb * 128 + 128]
                                    if nk > 1:
                                        p.inc(t.matmul(psc[:, 0:(nk - 1) * 128], lhs,
                                                       kvb[:, kc * T + kb0 * 128:kc * T + (kb0 + nk - 1) * 128],
                                                       start=(kc == 0), stop=(kc == 1 and nk == 5)), "pe")
                                    p.inc(t.matmul(psc[:, (nk - 1) * 128:nk * 128], lhs,
                                                   kvb[:, kc * T + qb * 128:kc * T + qb * 128 + 128],
                                                   start=(kc == 0 and (nk == 5 or nk == 1)),
                                                   stop=(kc == 1)), "pe")
                                pe_sc[i] = p.c["pe"]
                                if i >= 1:
                                    emit_av(t, i - 1, info)
                            info = emit_pt_av(t, NIT - 1)
                            emit_av(t, NIT - 1, info)

                        @blk.vector
                        def _(v):
                            for i, (qb, hl) in enumerate(iters):
                                par = i % 2
                                psc = sc0 if par == 0 else sc1
                                nk = nkb(qb)
                                p.w(v, "pe", pe_sc[i])
                                if qb >= 4:
                                    p.inc(v.tensor_add(psc[:, 0:128], psc[:, 0:128], mskL[:, :]), "dve")
                                tok = p.inc(v.tensor_add(psc[:, (nk - 1) * 128:nk * 128],
                                                         psc[:, (nk - 1) * 128:nk * 128], mskD[:, :]), "dve")
                                assert tok == dtk["mask"][i]
                                if i >= 2:
                                    p.w(v, "act", atk["exp"][i - 2])
                                tok = p.inc(v.reduce_max(avp[:, par * 8:par * 8 + 1], psc[:, 0:nk * 128],
                                                         axis=AX.X, negate=True), "dve")
                                assert tok == dtk["max"][i]
                                p.w(v, "act", atk["exp"][i])
                                tok = p.inc(v.reduce_sum(avp[:, par * 8 + 1:par * 8 + 2],
                                                         rsm[:, par * 1024:par * 1024 + nk * 128],
                                                         axis=AX.X), "dve")
                                assert tok == dtk["sum"][i]
                                p.w(v, "dve", dtk["sum"][i])
                                if i >= 2:
                                    p.w(v, "act", atk["nrm"][i - 2])
                                tok = p.inc(v.reciprocal(avp[:, par * 8 + 2:par * 8 + 3],
                                                         avp[:, par * 8 + 1:par * 8 + 2]), "dve")
                                assert tok == dtk["rcp"][i]

                        @blk.scalar
                        def _(a):
                            for i, (qb, hl) in enumerate(iters):
                                par = i % 2
                                psc = sc0 if par == 0 else sc1
                                nk = nkb(qb)
                                if i >= 1:
                                    j = i - 1
                                    parp = j % 2
                                    ppt = qt0 if parp == 0 else qt1
                                    nkp = nkb(iters[j][0])
                                    p.w(a, "pe", pe_pt[j])
                                    tok = p.inc(a.copy(pnb[:, 0:nkp * 128],
                                                       ppt[:, 0:nkp * 128]), "act")
                                    assert tok == atk["ptv"][j]
                                p.w(a, "dve", dtk["max"][i])
                                tok = p.inc(a.activation(rsm[:, par * 1024:par * 1024 + nk * 128],
                                                         psc[:, 0:nk * 128], AF.Exp,
                                                         bias=avp[:, par * 8:par * 8 + 1]), "act")
                                assert tok == atk["exp"][i]
                                p.w(a, "dve", dtk["rcp"][i])
                                if i >= 1:
                                    p.w(a, "pe", pe_av[i - 1])
                                tok = p.inc(a.mul(pnb[:, 0:nk * 128],
                                                  rsm[:, par * 1024:par * 1024 + nk * 128],
                                                  avp[:, par * 8 + 2:par * 8 + 3]), "act")
                                assert tok == atk["nrm"][i]
                                if i >= 1:
                                    j = i - 1
                                    qbp, hlp = iters[j]
                                    pav = av0 if j % 2 == 0 else av1
                                    p.w(a, "pe", pe_av[j])
                                    d1 = biga[:, 8192 + (hlp * 2) * T + qbp * 128:8192 + (hlp * 2) * T + qbp * 128 + 128]
                                    d2 = biga[:, 8192 + (hlp * 2 + 1) * T + qbp * 128:8192 + (hlp * 2 + 1) * T + qbp * 128 + 128]
                                    p.inc(a.copy(d1, pav[:, 0:128]), "act")
                                    tok = p.inc(a.copy(d2, pav[:, 128:256]), "act")
                                    assert tok == atk["avv"][j]
                            j = NIT - 1
                            parp = j % 2
                            ppt = qt0 if parp == 0 else qt1
                            nkp = nkb(iters[j][0])
                            p.w(a, "pe", pe_pt[j])
                            tok = p.inc(a.copy(pnb[:, 0:nkp * 128],
                                               ppt[:, 0:nkp * 128]), "act")
                            assert tok == atk["ptv"][j]
                            qbp, hlp = iters[j]
                            pav = av0 if j % 2 == 0 else av1
                            p.w(a, "pe", pe_av[j])
                            d1 = biga[:, 8192 + (hlp * 2) * T + qbp * 128:8192 + (hlp * 2) * T + qbp * 128 + 128]
                            d2 = biga[:, 8192 + (hlp * 2 + 1) * T + qbp * 128:8192 + (hlp * 2 + 1) * T + qbp * 128 + 128]
                            p.inc(a.copy(d1, pav[:, 0:128]), "act")
                            tok = p.inc(a.copy(d2, pav[:, 128:256]), "act")
                            assert tok == atk["avv"][j]
                atd = p.mark()
                if DEBUG_AB and hp == 0 and wtok.get("dbg_seq") == DEBUG_SEQ and not wtok.get("dbg_done"):
                    emit_dbg("av", biga[:, 8192:16384])
                    wtok["dbg_done"] = True

                with p.pt("ps_w0", [128, 512], F32) as pw0, \
                     p.pt("ps_w1", [128, 512], F32) as pw1, \
                     p.blk("ab_wo2") as blk:
                    dve0 = atd["dve"]
                    wadd = lambda jj: dve0 + jj + 1
                    wtoks = []

                    @blk.tensor
                    def _(t):
                        p.wsnap(t, atd)
                        p.w(t, "dmas", wtok["abwo"])
                        jj = 0
                        for gq in range(4):
                            for m in range(8):
                                ps = pw0 if jj % 2 == 0 else pw1
                                if jj >= 2:
                                    p.w(t, "dve", wadd(jj - 2))
                                for cl in range(4):
                                    c = hp * 4 + cl
                                    p.inc(t.matmul(ps[:, :],
                                                   wsb[:, 8192 + c * 1024 + m * 128:8192 + c * 1024 + (m + 1) * 128],
                                                   biga[:, 8192 + cl * T + gq * 512:8192 + cl * T + (gq + 1) * 512],
                                                   start=(cl == 0), stop=(cl == 3)), "pe")
                                wtoks.append(p.c["pe"])
                                jj += 1

                    @blk.vector
                    def _(v):
                        jj = 0
                        for gq in range(4):
                            for m in range(8):
                                ps = pw0 if jj % 2 == 0 else pw1
                                p.w(v, "pe", wtoks[jj])
                                tok = p.inc(v.tensor_add(xfm[:, m * T + gq * 512:m * T + (gq + 1) * 512],
                                                         xfm[:, m * T + gq * 512:m * T + (gq + 1) * 512],
                                                         ps[:, :]), "dve")
                                assert tok == wadd(jj)
                                jj += 1
            return p.mark()

        def ln_phase():
            wtok["preln_pe"] = p.c["pe"]
            t0, t1 = emit_ln()
            wtok["lnh0"], wtok["lnh1"] = t0, t1

        for s_i in range(SEQS):
            wtok["dbg_seq"] = s_i
            ln_phase()
            if DO_ATTNA:
                emit_attn_a()
            if DO_FFN:
                ln_phase()
                emit_ffn()
            if DO_ATTNB:
                ln_phase()
                emit_attn_b()
            if DO_FFN:
                ln_phase()
                emit_ffn()
            fin = p.mark()
            last_oadd = wtok.get("ffn_oadd_last") if DO_FFN else None
            with p.blk("store") as blk:
                @blk.sync
                def _(s):
                    if last_oadd is not None and fin["dve"] == last_oadd[7]:
                        p.w(s, "pe", fin["pe"])
                        p.w(s, "act", fin["act"])
                        for c in range(NC):
                            p.w(s, "dve", last_oadd[c])
                            p.dma(s, y[s_i, c], xfm[:, c * T:(c + 1) * T])
                    else:
                        p.wsnap(s, fin)
                        for c in range(NC):
                            p.dma(s, y[s_i, c], xfm[:, c * T:(c + 1) * T])
                    if s_i + 1 < SEQS:
                        for c in range(NC):
                            p.dma(s, xfm[:, c * T:(c + 1) * T], xin[s_i + 1, c])
            wtok["x"] = p.c["dmas"]

        with p.blk("fin") as blk:
            final = p.c["dmas"]

            @blk.sync
            def _(s):
                s.wait_ge(p.s["dmas"], final)
    return nc


def _fmaj(wT):
    O = wT.shape[1]
    return np.ascontiguousarray(
        wT.reshape(8, 128, O).transpose(1, 0, 2).reshape(128, 8 * O)).astype(np.float32)


def _consts():
    inv = (np.float32(1.0) / np.float32(1e6) ** (np.arange(0, 256, 2, dtype=np.float32) / np.float32(256))).astype(np.float32)
    t = np.arange(T, dtype=np.float32)
    fr = np.outer(t, inv).astype(np.float32)
    co = np.cos(fr.astype(np.float64)).astype(np.float32).T.copy()
    si = np.sin(fr.astype(np.float64)).astype(np.float32).T.copy()
    cs = np.concatenate([co, si], axis=1)
    qi = np.arange(128)[:, None]
    kj = np.arange(128)[None, :]
    mL = np.where(kj >= qi, 0.0, NEG).astype(np.float32)
    mD = np.where(kj <= qi, 0.0, NEG).astype(np.float32)
    mk = np.concatenate([mL, mD], axis=1)
    idb = np.eye(128, dtype=BFNP)
    onb = np.zeros((128, 2), dtype=BFNP)
    onb[:, 0] = BFNP(1.0 / 1024.0)
    onb[:, 1] = BFNP(1.0 / 256.0)
    onr = np.ones((64, 128), dtype=np.float32)
    onm = np.full((128, 128), BFNP(1.0 / 1024.0), dtype=BFNP)
    return cs, mk, idb, onb, onr, onm


def kernel(L0_attn_a_x, L0_attn_a_W_theta, L0_attn_a_W_phi, L0_attn_a_W_g,
           L0_attn_b_Wq, L0_attn_b_Wk, L0_attn_b_Wv, L0_attn_b_Wo,
           L0_attn_b_q_gamma, L0_attn_b_k_gamma,
           L0_ffn_rail_maker, L0_ffn_rail_up_proj, L0_ffn_w_cproj):
    x = np.asarray(L0_attn_a_x, dtype=np.float32)
    xl = np.ascontiguousarray(x.reshape(32, T, NC, 128).transpose(0, 2, 3, 1))

    wth = _fmaj(np.asarray(L0_attn_a_W_theta, np.float32).T)
    wph = _fmaj(np.asarray(L0_attn_a_W_phi, np.float32).T)
    wg = _fmaj(np.asarray(L0_attn_a_W_g, np.float32).T)

    deint = np.concatenate([np.arange(0, 256, 2), np.arange(1, 256, 2)])
    qcols = np.concatenate([h * 256 + deint for h in range(4)])
    wq = _fmaj(np.asarray(L0_attn_b_Wq, np.float32).T[:, qcols])
    wk = _fmaj(np.asarray(L0_attn_b_Wk, np.float32).T[:, deint])
    wv = _fmaj(np.asarray(L0_attn_b_Wv, np.float32).T)
    wo = _fmaj(np.asarray(L0_attn_b_Wo, np.float32).T)

    maker = np.asarray(L0_ffn_rail_maker, np.float32)
    up = np.asarray(L0_ffn_rail_up_proj, np.float32)
    w1t = np.zeros((8, 128, 512), np.float32)
    w2t = np.zeros((32, 128, 512), np.float32)
    for h in range(4):
        for rr in range(64):
            c, rp = rr // 8, rr % 8
            w1t[c, rp * 16:rp * 16 + 16, rp * 64 + h * 16:rp * 64 + h * 16 + 16] = maker[h, rr]
            jy, rrel = (rr % 8) // 2, rr % 2
            yc = c * 4 + jy
            w2t[yc, rrel * 64 + h * 16:rrel * 64 + h * 16 + 16,
                rrel * 256 + h * 64:rrel * 256 + h * 64 + 64] = up[h, rr].T
    w1 = np.ascontiguousarray(w1t.transpose(1, 0, 2).reshape(128, 4096))
    w2 = np.ascontiguousarray(w2t.transpose(1, 0, 2).reshape(128, 16384))

    wc = np.asarray(L0_ffn_w_cproj, np.float32)
    ZC, PP = np.meshgrid(np.arange(128), np.arange(128), indexing="ij")
    cc_ = ZC // 16
    jy_ = (ZC % 16) // 4
    jz_ = ZC % 4
    v_ = jz_ * 128 + PP
    rrel_ = v_ // 256
    hh_ = (v_ % 256) // 64
    kk_ = v_ % 64
    r_ = cc_ * 8 + 2 * jy_ + rrel_
    zcol = (hh_ * 4096 + r_ * 64 + kk_).reshape(-1)
    wcT = wc[:, zcol].T
    wcd = np.ascontiguousarray(
        wcT.reshape(128, 128, 8, 128).transpose(2, 1, 0, 3).reshape(8, 128, 16384))

    qg = np.asarray(L0_attn_b_q_gamma, np.float32)
    kg = np.asarray(L0_attn_b_k_gamma, np.float32)
    qg2 = np.ascontiguousarray(np.stack([qg[0::2], qg[1::2]], axis=1))
    kg2 = np.ascontiguousarray(np.stack([kg[0::2], kg[1::2]], axis=1))

    csC, mkC, idC, onC, orC, omC = _consts()

    nc = build_nc()
    in_maps = []
    for core in range(NCORE):
        in_maps.append({
            "xin": np.ascontiguousarray(xl[core * SEQS:(core + 1) * SEQS]),
            "wth": wth, "wph": wph, "wg": wg, "wq": wq, "wk": wk, "wv": wv,
            "wo": wo, "w1": w1, "w2": w2, "wc": wcd, "cs": csC, "mk": mkC,
            "idb": idC, "onb": onC, "onr": orC, "onm": omC, "qg": qg2, "kg": kg2,
        })
    res = run_bass_kernel_spmd(nc, in_maps, core_ids=list(range(NCORE)))
    out = np.empty((32, T, D), dtype=np.float32)
    for core in range(NCORE):
        yv = np.asarray(res.results[core]["y"])
        out[core * SEQS:(core + 1) * SEQS] = yv.transpose(0, 3, 1, 2).reshape(SEQS, T, D)
    return out


# version 05
# speedup vs baseline: 1.1312x; 1.0747x over previous
import numpy as np
import ml_dtypes
import concourse.bass as bass
import concourse.mybir as mybir
from concourse.bass_utils import run_bass_kernel_spmd

BFNP = ml_dtypes.bfloat16
F32 = mybir.dt.float32
BF16 = mybir.dt.bfloat16
AF = mybir.ActivationFunctionType
OP = mybir.AluOpType
AX = mybir.AxisListType

NCORE = 8
SEQS = 4
T = 2048
D = 1024
NC = 8
TT = 512
NTILE = T // TT
NEG = -1.0e30

DO_ATTNA = True
DO_FFN = True
DO_ATTNB = True
DEBUG_AB = False
DEBUG_SEQ = 0


class B:
    def __init__(self, nc):
        self.nc = nc
        self.s = {}
        self.c = {}

    def blk(self, name):
        self._bn = getattr(self, "_bn", 0) + 1
        return self.nc.Block("%s_%d" % (name, self._bn))

    def pt(self, name, shape, dt):
        self._pn = getattr(self, "_pn", 0) + 1
        return self.nc.psum_tensor("%s_%d" % (name, self._pn), shape, dt)

    def mksem(self, stack, *names):
        for n in names:
            self.s[n] = stack.enter_context(self.nc.semaphore(n))
            self.c[n] = 0

    def inc(self, inst, name, delta=1):
        inst.then_inc(self.s[name], delta)
        self.c[name] += delta
        return self.c[name]

    def dma(self, eng, out, in_, q="dmas"):
        return self.inc(eng.dma_start(out, in_), q, 16)

    def w(self, eng, name, tok):
        if tok > 0:
            eng.wait_ge(self.s[name], tok)

    def mark(self):
        return dict(self.c)

    def wsnap(self, eng, snap):
        for n in ("pe", "dve", "act", "pool"):
            self.w(eng, n, snap[n])


def build_nc():
    nc = bass.Bass("TRN2", target_bir_lowering=False)
    p = B(nc)

    xin = nc.dram_tensor("xin", [SEQS, NC, 128, T], F32, kind="ExternalInput").ap()
    y = nc.dram_tensor("y", [SEQS, NC, 128, T], F32, kind="ExternalOutput").ap()
    wth_i = nc.dram_tensor("wth", [128, 2048], F32, kind="ExternalInput").ap()
    wph_i = nc.dram_tensor("wph", [128, 2048], F32, kind="ExternalInput").ap()
    wg_i = nc.dram_tensor("wg", [128, 8192], F32, kind="ExternalInput").ap()
    wq_i = nc.dram_tensor("wq", [128, 8192], F32, kind="ExternalInput").ap()
    wk_i = nc.dram_tensor("wk", [128, 2048], F32, kind="ExternalInput").ap()
    wv_i = nc.dram_tensor("wv", [128, 2048], F32, kind="ExternalInput").ap()
    wo_i = nc.dram_tensor("wo", [128, 8192], F32, kind="ExternalInput").ap()
    w1_i = nc.dram_tensor("w1", [128, 4096], F32, kind="ExternalInput").ap()
    w2_i = nc.dram_tensor("w2", [128, 16384], F32, kind="ExternalInput").ap()
    wc_i = nc.dram_tensor("wc", [8, 128, 16384], F32, kind="ExternalInput").ap()
    cs_i = nc.dram_tensor("cs", [128, 4096], F32, kind="ExternalInput").ap()
    mk_i = nc.dram_tensor("mk", [128, 256], F32, kind="ExternalInput").ap()
    id_i = nc.dram_tensor("idb", [128, 128], BF16, kind="ExternalInput").ap()
    on_i = nc.dram_tensor("onb", [128, 2], BF16, kind="ExternalInput").ap()
    or_i = nc.dram_tensor("onr", [64, 128], F32, kind="ExternalInput").ap()
    om_i = nc.dram_tensor("onm", [128, 128], BF16, kind="ExternalInput").ap()
    qg_i = nc.dram_tensor("qg", [128, 2], F32, kind="ExternalInput").ap()
    kg_i = nc.dram_tensor("kg", [128, 2], F32, kind="ExternalInput").ap()

    dbg = {}
    if DEBUG_AB:
        for nm, cols in [("ln3", 16384), ("kv", 8192), ("qp", 8192), ("av", 8192),
                         ("pn0", 640), ("p32", 1024), ("avp", 32)]:
            dbg[nm] = nc.dram_tensor("dbg_" + nm, [128, cols],
                                     F32 if nm in ("p32", "avp") else BF16,
                                     kind="ExternalOutput").ap()
    wth_b = nc.dram_tensor("wth_b", [128, 2048], BF16, kind="Internal").ap()
    wph_b = nc.dram_tensor("wph_b", [128, 2048], BF16, kind="Internal").ap()
    wg_b = nc.dram_tensor("wg_b", [128, 8192], BF16, kind="Internal").ap()
    wq_b = nc.dram_tensor("wq_b", [128, 8192], BF16, kind="Internal").ap()
    wk_b = nc.dram_tensor("wk_b", [128, 2048], BF16, kind="Internal").ap()
    wv_b = nc.dram_tensor("wv_b", [128, 2048], BF16, kind="Internal").ap()
    wo_b = nc.dram_tensor("wo_b", [128, 8192], BF16, kind="Internal").ap()
    w1_b = nc.dram_tensor("w1_b", [128, 4096], BF16, kind="Internal").ap()
    w2_b = nc.dram_tensor("w2_b", [128, 16384], BF16, kind="Internal").ap()
    wc_b = nc.dram_tensor("wc_b", [8, 128, 16384], BF16, kind="Internal").ap()

    import contextlib
    with contextlib.ExitStack() as st:
        sbt = lambda name, shape, dt: st.enter_context(nc.sbuf_tensor(name, shape, dt))
        xfm = sbt("xfm", [128, NC * T], F32)
        lnb = sbt("lnb", [128, NC * T], BF16)
        cs = sbt("cs_sb", [128, 2 * T], F32)
        wsb = sbt("wsb", [128, 16384], BF16)
        biga = sbt("biga", [128, 16384], BF16)
        kvb = sbt("kvb", [128, 8192], BF16)
        scr32 = sbt("scr32", [128, 1024], F32)
        rsm = sbt("rsm", [128, 2048], F32)
        pnb = sbt("pnb", [128, 640], BF16)
        avp = sbt("avp", [128, 32], F32)
        mskL = sbt("mskL", [128, 128], F32)
        mskD = sbt("mskD", [128, 128], F32)
        idb = sbt("idb_sb", [128, 128], BF16)
        onb = sbt("onb_sb", [128, 2], BF16)
        onr = sbt("onr_sb", [64, 128], F32)
        onm = sbt("onm_sb", [128, 128], BF16)
        qgs = sbt("qgs", [128, 2], F32)
        kgs = sbt("kgs", [128, 2], F32)

        p.mksem(st, "pe", "dve", "act", "pool", "dmas", "dmag")
        wtok = {}

        with p.blk("startup") as blk:
            @blk.gpsimd
            def _(g):
                wtok["wth"] = p.dma(g, wth_b, wth_i, "dmag")
                wtok["wph"] = p.dma(g, wph_b, wph_i, "dmag")
                wtok["wg"] = p.dma(g, wg_b, wg_i, "dmag")
                wtok["w1"] = p.dma(g, w1_b, w1_i, "dmag")
                wtok["w2"] = p.dma(g, w2_b, w2_i, "dmag")
                wtok["wq"] = p.dma(g, wq_b, wq_i, "dmag")
                wtok["wk"] = p.dma(g, wk_b, wk_i, "dmag")
                wtok["wv"] = p.dma(g, wv_b, wv_i, "dmag")
                wtok["wo"] = p.dma(g, wo_b, wo_i, "dmag")
                for m in range(8):
                    wtok["wc%d" % m] = p.dma(g, wc_b[m], wc_i[m], "dmag")
                wtok["castall"] = p.c["dmag"]

            @blk.sync
            def _(s):
                p.dma(s, cs[:, :], cs_i)
                p.dma(s, mskL[:, :], mk_i[:, 0:128])
                p.dma(s, mskD[:, :], mk_i[:, 128:256])
                p.dma(s, idb[:, :], id_i)
                p.dma(s, onb[:, :], on_i)
                p.dma(s, onr[:, :], or_i)
                p.dma(s, onm[:, :], om_i)
                p.dma(s, qgs[:, :], qg_i)
                p.dma(s, kgs[:, :], kg_i)
                for c in range(NC):
                    p.dma(s, xfm[:, c * T:(c + 1) * T], xin[0, c])
                wtok["x"] = p.c["dmas"]

            @blk.vector
            def _(v):
                p.w(v, "dmas", wtok["x"])
                p.inc(v.tensor_scalar_mul(qgs[:, :], qgs[:, :], 1.0 / 16.0), "dve")

        def emit_ln():
            snap = p.mark()
            pe0, dv0, ac0 = snap["pe"], snap["dve"], snap["act"]
            tok_h0, tok_h1 = dv0 + 36, dv0 + 52
            with p.pt("ps_m0", [128, 1024], F32) as pm0, \
                 p.pt("ps_e0", [128, 1024], F32) as pe_0, \
                 p.pt("ps_m1", [128, 1024], F32) as pm1, \
                 p.pt("ps_e1", [128, 1024], F32) as pe_1, \
                 p.blk("ln") as blk:

                @blk.vector
                def _(v):
                    p.w(v, "dmas", wtok["x"])
                    p.w(v, "pe", pe0)
                    p.w(v, "act", ac0)
                    for hf in range(2):
                        o = hf * 1024
                        for c in range(NC):
                            if c >= 4:
                                p.w(v, "pe", pe0 + hf * 32 + 2 * (c - 3))
                            slot = hf * 8192 + (c % 4) * 1024
                            tok = p.inc(v.tensor_copy(biga[:, slot:slot + 1024],
                                                      xfm[:, c * T + o:c * T + o + 1024]), "dve")
                            assert tok == dv0 + hf * 8 + c + 1
                    for hf in range(2):
                        o = hf * 1024
                        pm = pm0 if hf == 0 else pm1
                        pex = pe_0 if hf == 0 else pe_1
                        buf = scr32[:, 0:1024] if hf == 0 else rsm[:, 0:1024]
                        p.w(v, "act", ac0 + 16 + hf + 1)
                        tok = p.inc(v.scalar_tensor_tensor(buf, pex[:, :], 1e-5, buf,
                                                           OP.add, OP.subtract), "dve")
                        assert tok == dv0 + 16 + hf * 2 + 1
                        p.w(v, "act", ac0 + 18 + hf + 1)
                        tok = p.inc(v.reciprocal(pex[:, :], buf), "dve")
                        assert tok == dv0 + 16 + hf * 2 + 2
                        p.w(v, "dve", tok)
                    for hf in range(2):
                        o = hf * 1024
                        pm = pm0 if hf == 0 else pm1
                        pex = pe_0 if hf == 0 else pe_1
                        buf = scr32[:, 0:1024] if hf == 0 else rsm[:, 0:1024]
                        for c in range(NC):
                            p.inc(v.tensor_sub(buf, xfm[:, c * T + o:c * T + o + 1024],
                                               pm[:, :]), "dve")
                            p.inc(v.tensor_mul(lnb[:, c * T + o:c * T + o + 1024],
                                               buf, pex[:, :]), "dve")
                        assert p.c["dve"] == (tok_h0 if hf == 0 else tok_h1)

                @blk.scalar
                def _(a):
                    p.w(a, "dmas", wtok["x"])
                    p.w(a, "pe", pe0)
                    p.w(a, "dve", dv0)
                    for hf in range(2):
                        o = hf * 1024
                        for c in range(NC):
                            if c >= 4:
                                p.w(a, "pe", pe0 + hf * 32 + 16 + 2 * (c - 3))
                            slot = hf * 8192 + 4096 + (c % 4) * 1024
                            tok = p.inc(a.activation(biga[:, slot:slot + 1024],
                                                     xfm[:, c * T + o:c * T + o + 1024],
                                                     AF.Square), "act")
                            assert tok == ac0 + hf * 8 + c + 1
                    for hf in range(2):
                        pm = pm0 if hf == 0 else pm1
                        buf = scr32[:, 0:1024] if hf == 0 else rsm[:, 0:1024]
                        p.w(a, "pe", pe0 + 32 * (hf + 1))
                        tok = p.inc(a.activation(buf, pm[:, :], AF.Square), "act")
                        assert tok == ac0 + 16 + hf + 1
                    for hf in range(2):
                        buf = scr32[:, 0:1024] if hf == 0 else rsm[:, 0:1024]
                        p.w(a, "dve", dv0 + 16 + hf * 2 + 1)
                        tok = p.inc(a.activation(buf, buf, AF.Sqrt), "act")
                        assert tok == ac0 + 18 + hf + 1

                @blk.tensor
                def _(t):
                    for hf in range(2):
                        pm = pm0 if hf == 0 else pm1
                        pex = pe_0 if hf == 0 else pe_1
                        for c in range(NC):
                            p.w(t, "dve", dv0 + hf * 8 + c + 1)
                            slot = hf * 8192 + (c % 4) * 1024
                            for n in range(2):
                                p.inc(t.matmul(pm[:, n * 512:(n + 1) * 512], onm[:, :],
                                               biga[:, slot + n * 512:slot + (n + 1) * 512],
                                               start=(c == 0), stop=(c == NC - 1)), "pe")
                        for c in range(NC):
                            p.w(t, "act", ac0 + hf * 8 + c + 1)
                            slot = hf * 8192 + 4096 + (c % 4) * 1024
                            for n in range(2):
                                p.inc(t.matmul(pex[:, n * 512:(n + 1) * 512], onm[:, :],
                                               biga[:, slot + n * 512:slot + (n + 1) * 512],
                                               start=(c == 0), stop=(c == NC - 1)), "pe")
                        assert p.c["pe"] == pe0 + 32 * (hf + 1)

            wtok["ln_ct"] = {(hf, c): dv0 + 20 + hf * 16 + 2 * (c + 1)
                             for hf in range(2) for c in range(NC)}
            return tok_h0, tok_h1

        def emit_attn_a():
            snap = p.mark()
            A0, B0 = 0, 8192
            TH0, S0, PH0, G0 = 0, 4096, 6144, 6656
            with p.blk("aa_w") as blk:
                @blk.sync
                def _(s):
                    p.w(s, "dmag", wtok["castall"])
                    p.w(s, "pe", wtok["preln_pe"])
                    p.dma(s, wsb[:, B0:B0 + 2048], wth_b)
                    p.dma(s, wsb[:, B0 + 2048:B0 + 4096], wph_b)
                    p.w(s, "dmag", wtok["castall"])
                    wtok["aa"] = p.dma(s, wsb[:, A0:A0 + 8192], wg_b)

            with p.pt("ps_th0", [128, T], F32) as pt0, \
                 p.pt("ps_th1", [128, T], F32) as pt1, \
                 p.blk("aa_th") as blk:
                ttoks = [0, 0]

                @blk.tensor
                def _(t):
                    p.w(t, "act", snap["act"])
                    p.w(t, "dve", wtok["lnh0"])
                    p.w(t, "dmas", wtok["aa"])
                    for l in range(2):
                        ps = pt0 if l == 0 else pt1
                        if l == 1:
                            p.w(t, "dve", wtok["lnh1"])
                        for c in range(NC):
                            for n in range(4):
                                if n == 2:
                                    p.w(t, "dve", wtok["ln_ct"][(1, c)])
                                p.inc(t.matmul(ps[:, n * 512:(n + 1) * 512],
                                               wsb[:, B0 + c * 256 + l * 128:B0 + c * 256 + l * 128 + 128],
                                               lnb[:, c * T + n * 512:c * T + (n + 1) * 512],
                                               start=(c == 0), stop=(c == NC - 1)), "pe")
                        ttoks[l] = p.c["pe"]

                @blk.scalar
                def _(a):
                    for l in range(2):
                        p.w(a, "pe", ttoks[l])
                        ps = pt0 if l == 0 else pt1
                        p.inc(a.copy(biga[:, TH0 + l * 2048:TH0 + (l + 1) * 2048], ps[:, :]), "act")
            th = p.mark()

            with p.pt("ps_ph", [128, 256], F32) as pph, \
                 p.pt("ps_g", [128, 1024], F32) as pg, \
                 p.pt("ps_s0", [128, 1024], F32) as ps0, \
                 p.pt("ps_s1", [128, 1024], F32) as ps1, \
                 p.blk("aa_s") as blk:
                dve0 = th["dve"]
                ev = lambda tt: dve0 + 2 * (tt + 1)
                pg_tok, s_tok = [], []

                @blk.tensor
                def _(t):
                    p.wsnap(t, th)
                    for tt in range(16):
                        if tt >= 1:
                            p.w(t, "dve", ev(tt - 1))
                        for c in range(NC):
                            p.inc(t.matmul(pph[:, :],
                                           lnb[:, c * T + tt * 128:c * T + tt * 128 + 128],
                                           wsb[:, B0 + 2048 + c * 256:B0 + 2048 + (c + 1) * 256],
                                           start=(c == 0), stop=(c == NC - 1)), "pe")
                        for c in range(NC):
                            for n in range(2):
                                p.inc(t.matmul(pg[:, n * 512:(n + 1) * 512],
                                               lnb[:, c * T + tt * 128:c * T + tt * 128 + 128],
                                               wsb[:, A0 + c * 1024 + n * 512:A0 + c * 1024 + (n + 1) * 512],
                                               start=(c == 0), stop=(c == NC - 1)), "pe")
                        pg_tok.append(p.c["pe"])
                        if tt >= 1:
                            j = tt - 1
                            par = j % 2
                            for l in range(2):
                                ps = ps0 if l == 0 else ps1
                                for n in range(2):
                                    p.inc(t.matmul(ps[:, n * 512:(n + 1) * 512],
                                                   biga[:, PH0 + par * 256 + l * 128:PH0 + par * 256 + l * 128 + 128],
                                                   biga[:, G0 + par * 1024 + n * 512:G0 + par * 1024 + (n + 1) * 512],
                                                   start=(j == 0), stop=(j == 15)), "pe")
                            s_tok.append(p.c["pe"])
                    p.w(t, "dve", ev(15))
                    j, par = 15, 15 % 2
                    for l in range(2):
                        ps = ps0 if l == 0 else ps1
                        for n in range(2):
                            p.inc(t.matmul(ps[:, n * 512:(n + 1) * 512],
                                           biga[:, PH0 + par * 256 + l * 128:PH0 + par * 256 + l * 128 + 128],
                                           biga[:, G0 + par * 1024 + n * 512:G0 + par * 1024 + (n + 1) * 512],
                                           start=False, stop=True), "pe")
                    s_tok.append(p.c["pe"])

                @blk.vector
                def _(v):
                    for tt in range(16):
                        par = tt % 2
                        p.w(v, "pe", pg_tok[tt])
                        if tt >= 2:
                            p.w(v, "pe", s_tok[tt - 2])
                        p.inc(v.tensor_copy(biga[:, PH0 + par * 256:PH0 + par * 256 + 256], pph[:, :]), "dve")
                        tok = p.inc(v.tensor_copy(biga[:, G0 + par * 1024:G0 + par * 1024 + 1024], pg[:, :]), "dve")
                        assert tok == ev(tt)
                    p.w(v, "pe", s_tok[15])
                    for l in range(2):
                        ps = ps0 if l == 0 else ps1
                        p.inc(v.tensor_scalar_mul(biga[:, S0 + l * 1024:S0 + (l + 1) * 1024],
                                                  ps[:, :], 1.0 / 32.0), "dve")
            sev = p.mark()

            with p.pt("ps_o0", [128, T], F32) as po0, \
                 p.pt("ps_o1", [128, T], F32) as po1, \
                 p.blk("aa_o") as blk:
                dve0 = sev["dve"]
                addt = lambda m: dve0 + m + 1
                mtoks = []

                @blk.tensor
                def _(t):
                    p.wsnap(t, sev)
                    for m in range(NC):
                        ps = po0 if m % 2 == 0 else po1
                        if m >= 2:
                            p.w(t, "dve", addt(m - 2))
                        for l in range(2):
                            for n in range(4):
                                p.inc(t.matmul(ps[:, n * 512:(n + 1) * 512],
                                               biga[:, S0 + l * 1024 + m * 128:S0 + l * 1024 + m * 128 + 128],
                                               biga[:, TH0 + l * 2048 + n * 512:TH0 + l * 2048 + (n + 1) * 512],
                                               start=(l == 0), stop=(l == 1)), "pe")
                        mtoks.append(p.c["pe"])

                @blk.vector
                def _(v):
                    for m in range(NC):
                        ps = po0 if m % 2 == 0 else po1
                        p.w(v, "pe", mtoks[m])
                        tok = p.inc(v.tensor_add(xfm[:, m * T:(m + 1) * T],
                                                 xfm[:, m * T:(m + 1) * T], ps[:, :]), "dve")
                        assert tok == addt(m)
            return p.mark()

        def emit_ffn():
            snap = p.mark()
            A0, B0 = 0, 8192
            WCE = [4096, 6144]
            WCO = [12288, 14336]
            with p.blk("ff_w") as blk:
                @blk.sync
                def _(s):
                    p.w(s, "dmag", wtok["castall"])
                    p.w(s, "pe", wtok["preln_pe"])
                    wtok["ffw1"] = p.dma(s, wsb[:, B0:B0 + 4096], w1_b)

            m0 = p.mark()
            pe0, dve0, act0, dm0, dg0 = m0["pe"], m0["dve"], m0["act"], m0["dmas"], m0["dmag"]

            PT, DT, AT = {}, {}, {}
            DME, DMO = {}, {}
            pc, dc, ac, de, do = pe0, dve0, act0, dm0, dg0

            def sim():
                global pc
                nonlocal_vars = None
            pc_ = [pe0]; dc_ = [dve0]; ac_ = [act0]; de_ = [dm0]; do_ = [dg0]

            def pe_op(key):
                pc_[0] += 1; PT[key] = pc_[0]
            def dve_op(key):
                dc_[0] += 1; DT[key] = dc_[0]
            def act_op(key):
                ac_[0] += 1; AT[key] = ac_[0]
            def dme_op(key):
                de_[0] += 16; DME[key] = de_[0]
            def dmo_op(key):
                do_[0] += 16; DMO[key] = do_[0]

            def pe_walk(emitf):
                for yci in range(8):
                    emitf("y", 0, yci)
                for zi in range(32):
                    emitf("z", 0, zi)
                for tg in range(16):
                    for q in range(4):
                        for zi in range(32):
                            if q == 2 and zi == 16 and tg < 15:
                                for yci in range(8):
                                    emitf("y", tg + 1, yci)
                            emitf("cp", tg, 2 * q, zi)
                            emitf("cp", tg, 2 * q + 1, zi)
                            if q == 3 and tg < 15:
                                emitf("z", tg + 1, zi)

            def dve_walk(emitf):
                for yci in range(8):
                    emitf("yev", 0, yci)
                for tg in range(16):
                    for m in range(4):
                        emitf("oadd", tg, m)
                    if tg < 15:
                        for yci in range(8):
                            emitf("yev", tg + 1, yci)
                    for m in range(4, 8):
                        emitf("oadd", tg, m)

            pe_walk(lambda k, *args: pe_op((k,) + args))
            dve_walk(lambda k, *args: dve_op((k,) + args))
            for tg in range(16):
                for zi in range(32):
                    act_op(("gel", tg, zi))
            for tg in range(16):
                g, tile = tg // 4, tg % 4
                if tile == 0:
                    dme_op(("w2", g))
                for q in range(4):
                    for h in range(2):
                        dme_op(("wc", tg, 2 * q, h))
                        dmo_op(("wc", tg, 2 * q + 1, h))

            with p.pt("ps_y0", [128, TT], F32) as py0, \
                 p.pt("ps_y1", [128, TT], F32) as py1, \
                 p.pt("ps_z0", [128, TT], F32) as pz0, \
                 p.pt("ps_z1", [128, TT], F32) as pz1, \
                 p.pt("ps_oA", [128, TT], F32) as poA, \
                 p.pt("ps_oB", [128, TT], F32) as poB, \
                 p.blk("ffm") as blk:

                @blk.sync
                def _(s):
                    for tg in range(16):
                        g, tile = tg // 4, tg % 4
                        if tile == 0:
                            if g == 0:
                                p.w(s, "pe", wtok["preln_pe"])
                            else:
                                p.w(s, "pe", PT[("z", g * 4 - 1, 31)])
                            tok = p.dma(s, wsb[:, A0:A0 + 4096],
                                        w2_b[:, g * 4096:(g + 1) * 4096])
                            assert tok == DME[("w2", g)]
                        for q in range(4):
                            for h in range(2):
                                m = 2 * q
                                ke = tg * 8 + q * 2 + h
                                if ke < 2:
                                    p.w(s, "pe", wtok["preln_pe"])
                                else:
                                    pk = ke - 2
                                    ptg, prem = pk // 8, pk % 8
                                    pq, ph = prem // 2, prem % 2
                                    p.w(s, "pe", PT[("cp", ptg, 2 * pq + 1, ph * 16 + 15)])
                                tok = p.dma(s, wsb[:, WCE[ke % 2]:WCE[ke % 2] + 2048],
                                            wc_b[m][:, g * 4096 + h * 2048:g * 4096 + h * 2048 + 2048])
                                assert tok == DME[("wc", tg, m, h)]

                @blk.gpsimd
                def _(gq):
                    for tg in range(16):
                        g = tg // 4
                        for q in range(4):
                            for h in range(2):
                                m = 2 * q + 1
                                ko = tg * 8 + q * 2 + h
                                if ko < 2:
                                    p.w(gq, "pe", wtok["preln_pe"])
                                else:
                                    pk = ko - 2
                                    ptg, prem = pk // 8, pk % 8
                                    pq, ph = prem // 2, prem % 2
                                    p.w(gq, "pe", PT[("cp", ptg, 2 * pq + 1, ph * 16 + 15)])
                                tok = p.dma(gq, wsb[:, WCO[ko % 2]:WCO[ko % 2] + 2048],
                                            wc_b[m][:, g * 4096 + h * 2048:g * 4096 + h * 2048 + 2048],
                                            "dmag")
                                assert tok == DMO[("wc", tg, m, h)]

                @blk.tensor
                def _(t):
                    first = [True]

                    def emit(kind, *args):
                        if kind == "y":
                            tg2, yci = args
                            g2, tile2 = tg2 // 4, tg2 % 4
                            yc = g2 * 8 + yci
                            c, jy = yc // 4, yc % 4
                            psy = py0 if yci % 2 == 0 else py1
                            if first[0]:
                                p.w(t, "dmas", wtok["ffw1"])
                                p.w(t, "act", snap["act"])
                                p.w(t, "dve", wtok["lnh0"])
                                first[0] = False
                            if tg2 >= 1:
                                p.w(t, "dve", wtok["ln_ct"][(tile2 // 2, c)])
                            gi = tg2 * 8 + yci
                            if gi >= 2:
                                p.w(t, "dve", DT[("yev", (gi - 2) // 8, (gi - 2) % 8)])
                            tok = p.inc(t.matmul(psy[:, :],
                                                 wsb[:, B0 + c * 512 + jy * 128:B0 + c * 512 + (jy + 1) * 128],
                                                 lnb[:, c * T + tile2 * TT:c * T + (tile2 + 1) * TT],
                                                 start=True, stop=True), "pe")
                            assert tok == PT[("y", tg2, yci)]
                        elif kind == "z":
                            tg2, zi = args
                            g2, tile2 = tg2 // 4, tg2 % 4
                            yci = zi // 4
                            jj = zi % 4
                            ypar = yci % 2
                            psz = pz0 if zi % 2 == 0 else pz1
                            if tile2 == 0 and zi == 0:
                                p.w(t, "dmas", DME[("w2", g2)])
                            p.w(t, "dve", DT[("yev", tg2, yci)])
                            zz = tg2 * 32 + zi
                            if zz >= 2:
                                p.w(t, "act", AT[("gel", (zz - 2) // 32, (zz - 2) % 32)])
                            tok = p.inc(t.matmul(psz[:, :],
                                                 wsb[:, A0 + yci * 512 + jj * 128:A0 + yci * 512 + (jj + 1) * 128],
                                                 kvb[:, yci * 512:(yci + 1) * 512],
                                                 start=True, stop=True), "pe")
                            assert tok == PT[("z", tg2, zi)]
                        else:
                            tg2, m, zi = args
                            q, h = m // 2, zi // 16
                            pso = poA if m % 2 == 0 else poB
                            if zi == 0:
                                if tg2 == 0 and m == 0:
                                    p.w(t, "dve", wtok["lnh1"])
                                if m >= 2:
                                    p.w(t, "dve", DT[("oadd", tg2, m - 2)])
                                elif tg2 >= 1:
                                    p.w(t, "dve", DT[("oadd", tg2 - 1, m + 6)])
                            if zi % 16 == 0:
                                if m % 2 == 0:
                                    p.w(t, "dmas", DME[("wc", tg2, m, h)])
                                else:
                                    p.w(t, "dmag", DMO[("wc", tg2, m, h)])
                            if m % 2 == 0:
                                p.w(t, "act", AT[("gel", tg2, zi)])
                                wcb = WCE[(tg2 * 8 + q * 2 + h) % 2]
                            else:
                                wcb = WCO[(tg2 * 8 + q * 2 + h) % 2]
                            tok = p.inc(t.matmul(pso[:, :],
                                                 wsb[:, wcb + (zi % 16) * 128:wcb + (zi % 16 + 1) * 128],
                                                 biga[:, zi * 512:(zi + 1) * 512],
                                                 start=(zi == 0), stop=(zi == 31)), "pe")
                            assert tok == PT[("cp", tg2, m, zi)]

                    pe_walk(emit)

                @blk.vector
                def _(v):
                    def emit(kind, *args):
                        if kind == "yev":
                            tg2, yci = args
                            psy = py0 if yci % 2 == 0 else py1
                            p.w(v, "pe", PT[("y", tg2, yci)])
                            tok = p.inc(v.tensor_copy(kvb[:, yci * 512:(yci + 1) * 512],
                                                      psy[:, :]), "dve")
                            assert tok == DT[("yev", tg2, yci)]
                        else:
                            tg2, m = args
                            g2, tile2 = tg2 // 4, tg2 % 4
                            pso = poA if m % 2 == 0 else poB
                            p.w(v, "pe", PT[("cp", tg2, m, 31)])
                            tok = p.inc(v.tensor_add(
                                xfm[:, m * T + tile2 * TT:m * T + (tile2 + 1) * TT],
                                xfm[:, m * T + tile2 * TT:m * T + (tile2 + 1) * TT],
                                pso[:, :]), "dve")
                            assert tok == DT[("oadd", tg2, m)]

                    dve_walk(emit)

                @blk.scalar
                def _(a):
                    for tg in range(16):
                        for zi in range(32):
                            psz = pz0 if zi % 2 == 0 else pz1
                            p.w(a, "pe", PT[("z", tg, zi)])
                            tok = p.inc(a.activation(biga[:, zi * 512:(zi + 1) * 512],
                                                     psz[:, :], AF.Gelu), "act")
                            assert tok == AT[("gel", tg, zi)]

            wtok["ffn_oadd_last"] = {m: DT[("oadd", 15, m)] for m in range(8)}
            return p.mark()

        def emit_dbg(nm, src_ap):
            m0 = p.mark()
            with p.blk("dbg") as blk:
                @blk.sync
                def _(s):
                    p.wsnap(s, m0)
                    p.dma(s, dbg[nm], src_ap)

        def emit_attn_b():
            if DEBUG_AB and wtok.get("dbg_seq") == DEBUG_SEQ and not wtok.get("dbg_done"):
                emit_dbg("ln3", lnb[:, :])
            snap = p.mark()
            A0, B0 = 0, 8192
            with p.blk("ab_w") as blk:
                @blk.sync
                def _(s):
                    p.w(s, "dmag", wtok["castall"])
                    p.w(s, "pe", wtok["preln_pe"])
                    p.dma(s, wsb[:, A0:A0 + 8192], wq_b)
                    p.dma(s, wsb[:, B0:B0 + 2048], wk_b)
                    wtok["abqkv"] = p.dma(s, wsb[:, B0 + 2048:B0 + 4096], wv_b)

            def proj_norm_rope(kind, h, gam, dst, doff):
                for hf in range(2):
                    o = hf * 1024
                    sn = p.mark()
                    pe0, dv0, ac0 = sn["pe"], sn["dve"], sn["act"]
                    with p.pt("pp_e", [128, 1024], F32) as pqe, \
                         p.pt("pp_o", [128, 1024], F32) as pqo, \
                         p.pt("pp_ms", [1, 1024], F32) as pms, \
                         p.pt("pp_rb", [128, 1024], F32) as prb, \
                         p.blk("pnr") as blk:
                        wstride = 1024 if kind == "q" else 256

                        @blk.tensor
                        def _(t):
                            p.wsnap(t, snap)
                            p.w(t, "dmas", wtok["abqkv"])
                            p.w(t, "dve", dv0)
                            p.w(t, "act", ac0)
                            for kc in range(2):
                                ps = pqe if kc == 0 else pqo
                                for c in range(NC):
                                    for n in range(2):
                                        p.inc(t.matmul(
                                            ps[:, n * 512:(n + 1) * 512],
                                            wsb[:, (A0 if kind == "q" else B0) + c * wstride + h * 256 + kc * 128:
                                                   (A0 if kind == "q" else B0) + c * wstride + h * 256 + kc * 128 + 128],
                                            lnb[:, c * T + o + n * 512:c * T + o + (n + 1) * 512],
                                            start=(c == 0), stop=(c == NC - 1)), "pe")
                            assert p.c["pe"] == pe0 + 32
                            for kc in range(2):
                                p.w(t, "act", ac0 + kc + 1)
                                for n in range(2):
                                    p.inc(t.matmul(pms[:, n * 512:(n + 1) * 512], onb[:, 1:2],
                                                   biga[:, 8192 + kc * 1024 + n * 512:8192 + kc * 1024 + (n + 1) * 512],
                                                   start=(kc == 0), stop=(kc == 1)), "pe")
                            assert p.c["pe"] == pe0 + 36
                            p.w(t, "dve", dv0 + 4)
                            for n in range(2):
                                p.inc(t.matmul(prb[:, n * 512:(n + 1) * 512], onr[32:33, :],
                                               rsm[32:33, 1024 + n * 512:1024 + (n + 1) * 512],
                                               start=True, stop=True), "pe")
                            assert p.c["pe"] == pe0 + 38

                        @blk.scalar
                        def _(a):
                            p.w(a, "dve", dv0 + 1)
                            p.inc(a.activation(biga[:, 8192:8192 + 1024], scr32[:, :], AF.Square), "act")
                            p.w(a, "dve", dv0 + 2)
                            p.inc(a.activation(biga[:, 9216:9216 + 1024], rsm[:, 0:1024], AF.Square), "act")
                            p.w(a, "dve", dv0 + 3)
                            tok = p.inc(a.activation(rsm[64:65, 1024:2048], rsm[64:65, 1024:2048],
                                                     AF.Sqrt), "act")
                            assert tok == ac0 + 3

                        @blk.vector
                        def _(v):
                            p.w(v, "pe", pe0 + 16)
                            p.inc(v.tensor_copy(scr32[:, :], pqe[:, :]), "dve")
                            p.w(v, "pe", pe0 + 32)
                            p.inc(v.tensor_copy(rsm[:, 0:1024], pqo[:, :]), "dve")
                            p.w(v, "pe", pe0 + 36)
                            tok = p.inc(v.tensor_scalar_add(rsm[64:65, 1024:2048], pms[0:1, :], 1e-6), "dve")
                            assert tok == dv0 + 3
                            p.w(v, "act", ac0 + 3)
                            tok = p.inc(v.reciprocal(rsm[32:33, 1024:2048], rsm[64:65, 1024:2048]), "dve")
                            assert tok == dv0 + 4
                            p.w(v, "pe", pe0 + 38)
                            p.inc(v.tensor_mul(scr32[:, :], scr32[:, :], prb[:, :]), "dve")
                            p.inc(v.tensor_scalar_mul(scr32[:, :], scr32[:, :], gam[:, 0:1]), "dve")
                            p.inc(v.tensor_mul(rsm[:, 0:1024], rsm[:, 0:1024], prb[:, :]), "dve")
                            p.inc(v.tensor_scalar_mul(rsm[:, 0:1024], rsm[:, 0:1024], gam[:, 1:2]), "dve")
                            co = cs[:, o:o + 1024]
                            si = cs[:, T + o:T + o + 1024]
                            tA = biga[:, 8192:8192 + 1024]
                            tB = biga[:, 9216:9216 + 1024]
                            p.inc(v.tensor_mul(tA, scr32[:, :], co), "dve")
                            p.inc(v.tensor_mul(tB, rsm[:, 0:1024], si), "dve")
                            p.inc(v.tensor_sub(dst[:, doff + o:doff + o + 1024], tA, tB), "dve")
                            p.inc(v.tensor_mul(tA, scr32[:, :], si), "dve")
                            p.inc(v.tensor_mul(tB, rsm[:, 0:1024], co), "dve")
                            tok = p.inc(v.tensor_add(dst[:, doff + T + o:doff + T + o + 1024], tA, tB), "dve")
                            assert tok == dv0 + 14

            proj_norm_rope("k", 0, kgs, kvb, 0)

            kv0 = p.mark()
            with p.pt("ps_v0", [128, 256], F32) as pv0, \
                 p.pt("ps_v1", [128, 256], F32) as pv1, \
                 p.blk("ab_v") as blk:
                dve0 = kv0["dve"]
                vev = lambda tt: dve0 + tt + 1
                vtoks = []

                @blk.tensor
                def _(t):
                    p.wsnap(t, kv0)
                    for tt in range(16):
                        ps = pv0 if tt % 2 == 0 else pv1
                        if tt >= 2:
                            p.w(t, "dve", vev(tt - 2))
                        for c in range(NC):
                            p.inc(t.matmul(ps[:, :],
                                           lnb[:, c * T + tt * 128:c * T + tt * 128 + 128],
                                           wsb[:, B0 + 2048 + c * 256:B0 + 2048 + (c + 1) * 256],
                                           start=(c == 0), stop=(c == NC - 1)), "pe")
                        vtoks.append(p.c["pe"])

                @blk.vector
                def _(v):
                    for tt in range(16):
                        ps = pv0 if tt % 2 == 0 else pv1
                        p.w(v, "pe", vtoks[tt])
                        tok = p.inc(v.tensor_copy(kvb[:, 4096 + tt * 256:4096 + (tt + 1) * 256],
                                                  ps[:, :]), "dve")
                        assert tok == vev(tt)
            kvd = p.mark()
            if DEBUG_AB and wtok.get("dbg_seq") == DEBUG_SEQ and not wtok.get("dbg_done"):
                emit_dbg("kv", kvb[:, :])

            with p.blk("ab_wo") as blk:
                @blk.sync
                def _(s):
                    p.w(s, "dmag", wtok["castall"])
                    p.w(s, "dve", kvd["dve"])
                    p.w(s, "pe", kvd["pe"])
                    wtok["abwo"] = p.dma(s, wsb[:, 8192:16384], wo_b)

            for hp in range(2):
                for hl in range(2):
                    proj_norm_rope("q", hp * 2 + hl, qgs, biga, hl * 2 * T)
                if DEBUG_AB and hp == 0 and wtok.get("dbg_seq") == DEBUG_SEQ and not wtok.get("dbg_done"):
                    emit_dbg("qp", biga[:, 0:8192])
                qd = p.mark()

                with p.pt("ps_sc0", [128, 640], F32) as sc0, \
                     p.pt("ps_sc1", [128, 640], F32) as sc1, \
                     p.pt("ps_pt0", [128, 640], BF16) as qt0, \
                     p.pt("ps_pt1", [128, 640], BF16) as qt1, \
                     p.pt("ps_av0", [128, 256], F32) as av0, \
                     p.pt("ps_av1", [128, 256], F32) as av1:
                    iters = [(qb, hl) for qb in range(16) for hl in range(2)]
                    NIT = len(iters)
                    nkb = lambda qb: min(qb, 4) + 1
                    dve0, act0 = qd["dve"], qd["act"]
                    dtk = {k: {} for k in ("mask", "max", "sum", "rcp")}
                    dc = dve0
                    for i, (qb, hl) in enumerate(iters):
                        dc += 1 + (1 if qb >= 4 else 0); dtk["mask"][i] = dc
                        dc += 1; dtk["max"][i] = dc
                        dc += 1; dtk["sum"][i] = dc
                        dc += 1; dtk["rcp"][i] = dc
                    atk = {k: {} for k in ("ptv", "exp", "nrm", "avv")}
                    ac = act0
                    for i in range(NIT):
                        if i >= 1:
                            ac += 1; atk["ptv"][i - 1] = ac
                        ac += 1; atk["exp"][i] = ac
                        ac += 1; atk["nrm"][i] = ac
                        if i >= 1:
                            ac += 2; atk["avv"][i - 1] = ac
                    ac += 1; atk["ptv"][NIT - 1] = ac
                    ac += 2; atk["avv"][NIT - 1] = ac
                    pe_sc, pe_pt, pe_av = {}, {}, {}

                    def emit_pt_av(t, j):
                        qbp, hlp = iters[j]
                        parp = j % 2
                        ppt = qt0 if parp == 0 else qt1
                        pav = av0 if parp == 0 else av1
                        nkp = nkb(qbp)
                        kb0p = qbp - (nkp - 1)
                        p.w(t, "act", atk["nrm"][j])
                        if j >= 2:
                            p.w(t, "act", atk["ptv"][j - 2])
                        for kb in range(nkp):
                            p.inc(t.transpose(ppt[:, kb * 128:(kb + 1) * 128],
                                              pnb[:, kb * 128:(kb + 1) * 128],
                                              idb[:, :]), "pe")
                        pe_pt[j] = p.c["pe"]
                        return (pav, nkp, kb0p, qbp, hlp)

                    def emit_av(t, j, info):
                        pav, nkp, kb0p, qbp, hlp = info
                        parp = j % 2
                        p.w(t, "act", atk["ptv"][j])
                        if j >= 2:
                            p.w(t, "act", atk["avv"][j - 2])
                        for dh in range(2):
                            for kb in range(nkp):
                                p.inc(t.matmul(pav[:, dh * 128:(dh + 1) * 128],
                                               kvb[:, 4096 + (kb0p + kb) * 256 + dh * 128:
                                                      4096 + (kb0p + kb) * 256 + (dh + 1) * 128],
                                               pnb[:, kb * 128:(kb + 1) * 128],
                                               start=(kb == 0), stop=(kb == nkp - 1)), "pe")
                        pe_av[j] = p.c["pe"]

                    with p.blk("ab_at") as blk:
                        @blk.tensor
                        def _(t):
                            p.wsnap(t, qd)
                            for i, (qb, hl) in enumerate(iters):
                                par = i % 2
                                psc = sc0 if par == 0 else sc1
                                nk = nkb(qb)
                                kb0 = qb - (nk - 1)
                                if i >= 1:
                                    info = emit_pt_av(t, i - 1)
                                if i >= 2:
                                    p.w(t, "act", atk["exp"][i - 2])
                                for kc in range(2):
                                    lhs = biga[:, (hl * 2 + kc) * T + qb * 128:(hl * 2 + kc) * T + qb * 128 + 128]
                                    if nk > 1:
                                        p.inc(t.matmul(psc[:, 0:(nk - 1) * 128], lhs,
                                                       kvb[:, kc * T + kb0 * 128:kc * T + (kb0 + nk - 1) * 128],
                                                       start=(kc == 0), stop=(kc == 1 and nk == 5)), "pe")
                                    p.inc(t.matmul(psc[:, (nk - 1) * 128:nk * 128], lhs,
                                                   kvb[:, kc * T + qb * 128:kc * T + qb * 128 + 128],
                                                   start=(kc == 0 and (nk == 5 or nk == 1)),
                                                   stop=(kc == 1)), "pe")
                                pe_sc[i] = p.c["pe"]
                                if i >= 1:
                                    emit_av(t, i - 1, info)
                            info = emit_pt_av(t, NIT - 1)
                            emit_av(t, NIT - 1, info)

                        @blk.vector
                        def _(v):
                            for i, (qb, hl) in enumerate(iters):
                                par = i % 2
                                psc = sc0 if par == 0 else sc1
                                nk = nkb(qb)
                                p.w(v, "pe", pe_sc[i])
                                if qb >= 4:
                                    p.inc(v.tensor_add(psc[:, 0:128], psc[:, 0:128], mskL[:, :]), "dve")
                                tok = p.inc(v.tensor_add(psc[:, (nk - 1) * 128:nk * 128],
                                                         psc[:, (nk - 1) * 128:nk * 128], mskD[:, :]), "dve")
                                assert tok == dtk["mask"][i]
                                if i >= 2:
                                    p.w(v, "act", atk["exp"][i - 2])
                                tok = p.inc(v.reduce_max(avp[:, par * 8:par * 8 + 1], psc[:, 0:nk * 128],
                                                         axis=AX.X, negate=True), "dve")
                                assert tok == dtk["max"][i]
                                p.w(v, "act", atk["exp"][i])
                                tok = p.inc(v.reduce_sum(avp[:, par * 8 + 1:par * 8 + 2],
                                                         rsm[:, par * 1024:par * 1024 + nk * 128],
                                                         axis=AX.X), "dve")
                                assert tok == dtk["sum"][i]
                                p.w(v, "dve", dtk["sum"][i])
                                if i >= 2:
                                    p.w(v, "act", atk["nrm"][i - 2])
                                tok = p.inc(v.reciprocal(avp[:, par * 8 + 2:par * 8 + 3],
                                                         avp[:, par * 8 + 1:par * 8 + 2]), "dve")
                                assert tok == dtk["rcp"][i]

                        @blk.scalar
                        def _(a):
                            for i, (qb, hl) in enumerate(iters):
                                par = i % 2
                                psc = sc0 if par == 0 else sc1
                                nk = nkb(qb)
                                if i >= 1:
                                    j = i - 1
                                    parp = j % 2
                                    ppt = qt0 if parp == 0 else qt1
                                    nkp = nkb(iters[j][0])
                                    p.w(a, "pe", pe_pt[j])
                                    tok = p.inc(a.copy(pnb[:, 0:nkp * 128],
                                                       ppt[:, 0:nkp * 128]), "act")
                                    assert tok == atk["ptv"][j]
                                p.w(a, "dve", dtk["max"][i])
                                tok = p.inc(a.activation(rsm[:, par * 1024:par * 1024 + nk * 128],
                                                         psc[:, 0:nk * 128], AF.Exp,
                                                         bias=avp[:, par * 8:par * 8 + 1]), "act")
                                assert tok == atk["exp"][i]
                                p.w(a, "dve", dtk["rcp"][i])
                                if i >= 1:
                                    p.w(a, "pe", pe_av[i - 1])
                                tok = p.inc(a.mul(pnb[:, 0:nk * 128],
                                                  rsm[:, par * 1024:par * 1024 + nk * 128],
                                                  avp[:, par * 8 + 2:par * 8 + 3]), "act")
                                assert tok == atk["nrm"][i]
                                if i >= 1:
                                    j = i - 1
                                    qbp, hlp = iters[j]
                                    pav = av0 if j % 2 == 0 else av1
                                    p.w(a, "pe", pe_av[j])
                                    d1 = biga[:, 8192 + (hlp * 2) * T + qbp * 128:8192 + (hlp * 2) * T + qbp * 128 + 128]
                                    d2 = biga[:, 8192 + (hlp * 2 + 1) * T + qbp * 128:8192 + (hlp * 2 + 1) * T + qbp * 128 + 128]
                                    p.inc(a.copy(d1, pav[:, 0:128]), "act")
                                    tok = p.inc(a.copy(d2, pav[:, 128:256]), "act")
                                    assert tok == atk["avv"][j]
                            j = NIT - 1
                            parp = j % 2
                            ppt = qt0 if parp == 0 else qt1
                            nkp = nkb(iters[j][0])
                            p.w(a, "pe", pe_pt[j])
                            tok = p.inc(a.copy(pnb[:, 0:nkp * 128],
                                               ppt[:, 0:nkp * 128]), "act")
                            assert tok == atk["ptv"][j]
                            qbp, hlp = iters[j]
                            pav = av0 if j % 2 == 0 else av1
                            p.w(a, "pe", pe_av[j])
                            d1 = biga[:, 8192 + (hlp * 2) * T + qbp * 128:8192 + (hlp * 2) * T + qbp * 128 + 128]
                            d2 = biga[:, 8192 + (hlp * 2 + 1) * T + qbp * 128:8192 + (hlp * 2 + 1) * T + qbp * 128 + 128]
                            p.inc(a.copy(d1, pav[:, 0:128]), "act")
                            tok = p.inc(a.copy(d2, pav[:, 128:256]), "act")
                            assert tok == atk["avv"][j]
                atd = p.mark()
                if DEBUG_AB and hp == 0 and wtok.get("dbg_seq") == DEBUG_SEQ and not wtok.get("dbg_done"):
                    emit_dbg("av", biga[:, 8192:16384])
                    wtok["dbg_done"] = True

                with p.pt("ps_w0", [128, 512], F32) as pw0, \
                     p.pt("ps_w1", [128, 512], F32) as pw1, \
                     p.blk("ab_wo2") as blk:
                    dve0 = atd["dve"]
                    wadd = lambda jj: dve0 + jj + 1
                    wtoks = []

                    @blk.tensor
                    def _(t):
                        p.wsnap(t, atd)
                        p.w(t, "dmas", wtok["abwo"])
                        jj = 0
                        for gq in range(4):
                            for m in range(8):
                                ps = pw0 if jj % 2 == 0 else pw1
                                if jj >= 2:
                                    p.w(t, "dve", wadd(jj - 2))
                                for cl in range(4):
                                    c = hp * 4 + cl
                                    p.inc(t.matmul(ps[:, :],
                                                   wsb[:, 8192 + c * 1024 + m * 128:8192 + c * 1024 + (m + 1) * 128],
                                                   biga[:, 8192 + cl * T + gq * 512:8192 + cl * T + (gq + 1) * 512],
                                                   start=(cl == 0), stop=(cl == 3)), "pe")
                                wtoks.append(p.c["pe"])
                                jj += 1

                    @blk.vector
                    def _(v):
                        jj = 0
                        for gq in range(4):
                            for m in range(8):
                                ps = pw0 if jj % 2 == 0 else pw1
                                p.w(v, "pe", wtoks[jj])
                                tok = p.inc(v.tensor_add(xfm[:, m * T + gq * 512:m * T + (gq + 1) * 512],
                                                         xfm[:, m * T + gq * 512:m * T + (gq + 1) * 512],
                                                         ps[:, :]), "dve")
                                assert tok == wadd(jj)
                                jj += 1
            return p.mark()

        def ln_phase():
            wtok["preln_pe"] = p.c["pe"]
            t0, t1 = emit_ln()
            wtok["lnh0"], wtok["lnh1"] = t0, t1

        for s_i in range(SEQS):
            wtok["dbg_seq"] = s_i
            ln_phase()
            if DO_ATTNA:
                emit_attn_a()
            if DO_FFN:
                ln_phase()
                emit_ffn()
            if DO_ATTNB:
                ln_phase()
                emit_attn_b()
            if DO_FFN:
                ln_phase()
                emit_ffn()
            fin = p.mark()
            last_oadd = wtok.get("ffn_oadd_last") if DO_FFN else None
            with p.blk("store") as blk:
                @blk.sync
                def _(s):
                    if last_oadd is not None and fin["dve"] == last_oadd[7]:
                        p.w(s, "pe", fin["pe"])
                        p.w(s, "act", fin["act"])
                        for c in range(NC):
                            p.w(s, "dve", last_oadd[c])
                            p.dma(s, y[s_i, c], xfm[:, c * T:(c + 1) * T])
                    else:
                        p.wsnap(s, fin)
                        for c in range(NC):
                            p.dma(s, y[s_i, c], xfm[:, c * T:(c + 1) * T])
                    if s_i + 1 < SEQS:
                        for c in range(NC):
                            p.dma(s, xfm[:, c * T:(c + 1) * T], xin[s_i + 1, c])
            wtok["x"] = p.c["dmas"]

        with p.blk("fin") as blk:
            final = p.c["dmas"]

            @blk.sync
            def _(s):
                s.wait_ge(p.s["dmas"], final)
    return nc


def _fmaj(wT):
    O = wT.shape[1]
    return np.ascontiguousarray(
        wT.reshape(8, 128, O).transpose(1, 0, 2).reshape(128, 8 * O)).astype(np.float32)


def _consts():
    inv = (np.float32(1.0) / np.float32(1e6) ** (np.arange(0, 256, 2, dtype=np.float32) / np.float32(256))).astype(np.float32)
    t = np.arange(T, dtype=np.float32)
    fr = np.outer(t, inv).astype(np.float32)
    co = np.cos(fr.astype(np.float64)).astype(np.float32).T.copy()
    si = np.sin(fr.astype(np.float64)).astype(np.float32).T.copy()
    cs = np.concatenate([co, si], axis=1)
    qi = np.arange(128)[:, None]
    kj = np.arange(128)[None, :]
    mL = np.where(kj >= qi, 0.0, NEG).astype(np.float32)
    mD = np.where(kj <= qi, 0.0, NEG).astype(np.float32)
    mk = np.concatenate([mL, mD], axis=1)
    idb = np.eye(128, dtype=BFNP)
    onb = np.zeros((128, 2), dtype=BFNP)
    onb[:, 0] = BFNP(1.0 / 1024.0)
    onb[:, 1] = BFNP(1.0 / 256.0)
    onr = np.ones((64, 128), dtype=np.float32)
    onm = np.full((128, 128), BFNP(1.0 / 1024.0), dtype=BFNP)
    return cs, mk, idb, onb, onr, onm


def kernel(L0_attn_a_x, L0_attn_a_W_theta, L0_attn_a_W_phi, L0_attn_a_W_g,
           L0_attn_b_Wq, L0_attn_b_Wk, L0_attn_b_Wv, L0_attn_b_Wo,
           L0_attn_b_q_gamma, L0_attn_b_k_gamma,
           L0_ffn_rail_maker, L0_ffn_rail_up_proj, L0_ffn_w_cproj):
    x = np.asarray(L0_attn_a_x, dtype=np.float32)
    xl = np.ascontiguousarray(x.reshape(32, T, NC, 128).transpose(0, 2, 3, 1))

    wth = _fmaj(np.asarray(L0_attn_a_W_theta, np.float32).T)
    wph = _fmaj(np.asarray(L0_attn_a_W_phi, np.float32).T)
    wg = _fmaj(np.asarray(L0_attn_a_W_g, np.float32).T)

    deint = np.concatenate([np.arange(0, 256, 2), np.arange(1, 256, 2)])
    qcols = np.concatenate([h * 256 + deint for h in range(4)])
    wq = _fmaj(np.asarray(L0_attn_b_Wq, np.float32).T[:, qcols])
    wk = _fmaj(np.asarray(L0_attn_b_Wk, np.float32).T[:, deint])
    wv = _fmaj(np.asarray(L0_attn_b_Wv, np.float32).T)
    wo = _fmaj(np.asarray(L0_attn_b_Wo, np.float32).T)

    maker = np.asarray(L0_ffn_rail_maker, np.float32)
    up = np.asarray(L0_ffn_rail_up_proj, np.float32)
    w1t = np.zeros((8, 128, 512), np.float32)
    w2t = np.zeros((32, 128, 512), np.float32)
    for h in range(4):
        for rr in range(64):
            c, rp = rr // 8, rr % 8
            w1t[c, rp * 16:rp * 16 + 16, rp * 64 + h * 16:rp * 64 + h * 16 + 16] = maker[h, rr]
            jy, rrel = (rr % 8) // 2, rr % 2
            yc = c * 4 + jy
            w2t[yc, rrel * 64 + h * 16:rrel * 64 + h * 16 + 16,
                rrel * 256 + h * 64:rrel * 256 + h * 64 + 64] = up[h, rr].T
    w1 = np.ascontiguousarray(w1t.transpose(1, 0, 2).reshape(128, 4096))
    w2 = np.ascontiguousarray(w2t.transpose(1, 0, 2).reshape(128, 16384))

    wc = np.asarray(L0_ffn_w_cproj, np.float32)
    ZC, PP = np.meshgrid(np.arange(128), np.arange(128), indexing="ij")
    cc_ = ZC // 16
    jy_ = (ZC % 16) // 4
    jz_ = ZC % 4
    v_ = jz_ * 128 + PP
    rrel_ = v_ // 256
    hh_ = (v_ % 256) // 64
    kk_ = v_ % 64
    r_ = cc_ * 8 + 2 * jy_ + rrel_
    zcol = (hh_ * 4096 + r_ * 64 + kk_).reshape(-1)
    wcT = wc[:, zcol].T
    wcd = np.ascontiguousarray(
        wcT.reshape(128, 128, 8, 128).transpose(2, 1, 0, 3).reshape(8, 128, 16384))

    qg = np.asarray(L0_attn_b_q_gamma, np.float32)
    kg = np.asarray(L0_attn_b_k_gamma, np.float32)
    qg2 = np.ascontiguousarray(np.stack([qg[0::2], qg[1::2]], axis=1))
    kg2 = np.ascontiguousarray(np.stack([kg[0::2], kg[1::2]], axis=1))

    csC, mkC, idC, onC, orC, omC = _consts()

    nc = build_nc()
    in_maps = []
    for core in range(NCORE):
        in_maps.append({
            "xin": np.ascontiguousarray(xl[core * SEQS:(core + 1) * SEQS]),
            "wth": wth, "wph": wph, "wg": wg, "wq": wq, "wk": wk, "wv": wv,
            "wo": wo, "w1": w1, "w2": w2, "wc": wcd, "cs": csC, "mk": mkC,
            "idb": idC, "onb": onC, "onr": orC, "onm": omC, "qg": qg2, "kg": kg2,
        })
    res = run_bass_kernel_spmd(nc, in_maps, core_ids=list(range(NCORE)))
    out = np.empty((32, T, D), dtype=np.float32)
    for core in range(NCORE):
        yv = np.asarray(res.results[core]["y"])
        out[core * SEQS:(core + 1) * SEQS] = yv.transpose(0, 3, 1, 2).reshape(SEQS, T, D)
    return out
